# version 35
# baseline (speedup 1.0000x reference)
import numpy as np
import ml_dtypes
import concourse.bass as bass
import concourse.mybir as mybir
from concourse.bass_utils import run_bass_kernel_spmd
from concourse.alu_op_type import AluOpType as ALU
from concourse.bass_isa import ReduceOp
from concourse.library_config import attn as LIB_ATTN
from contextlib import ExitStack

F32 = mybir.dt.float32
BF16 = mybir.dt.bfloat16
AF = mybir.ActivationFunctionType
AX = mybir.AxisListType
BF = ml_dtypes.bfloat16

S, D, H, HD, DH = 4096, 1024, 16, 64, 4096
T = 128
NT = S // T
GT = 2
NG = NT // GT
RING = 768
NSLOT = 4
EPS = 1e-5


def _stage_matrix(weights, s, dim=DH):
    w = np.asarray(weights[s], dtype=np.float32).reshape(dim // 2, 2, 2)
    s2 = 2 ** s
    M = np.zeros((dim, dim), dtype=np.float32)
    k = np.arange(dim // 2)
    q = k // s2
    r = k % s2
    ia = q * 2 * s2 + r
    ib = ia + s2
    M[ia, ia] = w[k, 0, 0]
    M[ib, ia] = w[k, 0, 1]
    M[ia, ib] = w[k, 1, 0]
    M[ib, ib] = w[k, 1, 1]
    return M


def _pi_perm(dim=DH):
    j = np.arange(dim)
    return (j & 127) * 32 + (j >> 7)


def _blocks(M):
    return np.stack([M[c * T:(c + 1) * T, c * T:(c + 1) * T] for c in range(32)])


def _conj_pi(M):
    pi = _pi_perm()
    out = np.zeros_like(M)
    out[np.ix_(pi, pi)] = M
    return out


def _fold_factor_arrays(up_w, down_w):
    Mu = [_stage_matrix(up_w, s) for s in range(12)]
    Md = [_stage_matrix(down_w, s) for s in range(12)]
    upA = np.stack([_blocks(Mu[6])] + [_blocks(Mu[s].T) for s in range(5, -1, -1)], axis=1)
    Nu = [_conj_pi(Mu[s]) for s in range(7, 12)]
    upB = np.stack([_blocks(Nu[4])] + [_blocks(Nu[s].T) for s in (3, 2, 1, 0)], axis=1)
    Nd = [_conj_pi(Md[s]) for s in range(7, 12)]
    dnB = np.stack([_blocks(Nd[0].T)] + [_blocks(Nd[s - 7]) for s in (8, 9, 10, 11)], axis=1)
    dnA = np.stack([_blocks(Md[0].T)] + [_blocks(Md[s]) for s in range(1, 7)], axis=1)
    return (upA.astype(BF), upB.astype(BF),
            dnB.astype(BF), dnA.astype(BF))


def _mask_D(kind):
    r = np.arange(128)
    c = np.arange(128)
    d0 = np.abs(c[None, :] - (r[:, None] - 64)).astype(np.float32)
    d1 = np.abs(c[None, :] - (r[:, None] + 64)).astype(np.float32)
    BIGD = np.float32(1e6)
    if kind == "first":
        d0 = d0.copy(); d0[r < 64, :] = BIGD
    if kind == "last":
        d1 = d1.copy(); d1[r >= 64, :] = BIGD
    return np.concatenate([d0, d1], axis=1)


def _prep(inputs):
    g = {}
    f32 = lambda a: np.ascontiguousarray(np.asarray(a), dtype=np.float32)
    bf = lambda a: np.ascontiguousarray(np.asarray(a).astype(BF))
    pi = _pi_perm()
    inv = np.argsort(pi)

    x = np.asarray(inputs["L0_attn_x"])
    g["alpha"] = f32(inputs["L0_attn_alpha"]).reshape(1, 1)
    g["tail_last"] = f32(x[:, S - 1, :])
    g["tail_prev"] = f32(x[:, S - 2, :])
    for nm, w in (("wq", "L0_attn_Wq"), ("wk", "L0_attn_Wk"),
                  ("wv", "L0_attn_Wv")):
        g[nm] = bf(np.asarray(inputs[w]).reshape(8, 128, 8, 128)
                   .transpose(1, 2, 0, 3).reshape(128, 8 * 1024))
    g["wfc"] = bf(inputs["L0_attn_Wfc"])
    g["wo"] = bf(inputs["L0_attn_W_o"])
    for nm, b in (("bq", "L0_attn_bq"), ("bk", "L0_attn_bk"), ("bv", "L0_attn_bv")):
        g[nm] = f32(np.asarray(inputs[b]).reshape(8, 128).T)
    g["bfc_row"] = bf(np.asarray(inputs["L0_attn_bfc"]).reshape(1, D))
    g["bcon_row"] = bf(np.asarray(inputs["L0_ffn_b_con"]).reshape(1, D))
    g["dbias_row"] = bf(np.asarray(inputs["L0_ffn_down_bias"]).reshape(1, D))
    g["wexpT_aug"] = bf(np.concatenate(
        [np.asarray(inputs["L0_ffn_W_exp"], dtype=np.float32).T,
         np.asarray(inputs["L0_ffn_b_exp"], dtype=np.float32).reshape(DH, 1)], axis=1))
    g["wcon_pi"] = bf(np.asarray(inputs["L0_ffn_W_con"])[inv, :])
    g["up_bias_pi"] = f32(np.asarray(inputs["L0_ffn_up_bias"])[inv].reshape(32, 128).T)
    upA, upB, dnB, dnA = _fold_factor_arrays(inputs["L0_ffn_up_weights"],
                                             inputs["L0_ffn_down_weights"])
    g["upA"], g["upB"], g["dnB"], g["dnA"] = upA, upB, dnB, dnA
    g["id_f32"] = np.eye(128, dtype=np.float32)
    g["id_bf"] = np.eye(128).astype(BF)
    g["dmask"] = np.stack([_mask_D("mid"), _mask_D("first"), _mask_D("last")])
    return g



DMA_SEMS = ("scst", "sfold0", "sfold1", "sfc0", "sfc1", "sfc2", "sfc3",
            "sff0", "sff1", "sff2", "sff3",
            "sx0", "sx1", "sx2", "sx3", "sout0", "sout1",
            "gw0", "gw1", "gw2", "gw3", "gf0", "gf1", "gf2", "gf3",
            "gg0", "gg1", "gg2", "gg3",
            "gfc0", "gfc1", "gfc2", "gfc3")


class Sched:
    ENG = ("pe", "act", "dve", "gp", "sp")

    def __init__(self):
        self.ops = {e: [] for e in self.ENG}
        self.cnt = {s: 0 for s in ("pe", "act", "dve", "pool") + DMA_SEMS}
        self.wm = {e: {s: 0 for s in self.cnt} for e in self.ENG}
        self.sems = {}
        self.final = {}

    def dep(self, eng, tok):
        if tok is None:
            return
        sem, v = tok
        fv = self.final.get(sem)
        if fv is not None and v < fv:
            v = fv
        if self.wm[eng][sem] >= v:
            return
        self.ops[eng].append(("w", sem, v))
        self.wm[eng][sem] = v

    def add(self, eng, fn, deps=(), inc=True, dma=False, base=0):
        for t in deps:
            self.dep(eng, t)
        if eng == "pe":
            if base != getattr(self, "_pe_base", 0):
                if not self.ops["pe"] or self.ops["pe"][-1][0] != "w":
                    self.ops["pe"].append(("o", lambda q: q.nop(), None, 0))
            self._pe_base = base
        if dma:
            sem = dma
            assert sem in DMA_SEMS, sem
            self.cnt[sem] += 16
            self.ops[eng].append(("o", fn, sem, 16))
            return (sem, self.cnt[sem])
        sem = "pool" if eng in ("gp", "sp") else eng
        if eng == "sp":
            inc = False
        if inc:
            self.cnt[sem] += 1
            self.ops[eng].append(("o", fn, sem, 1))
        else:
            self.ops[eng].append(("o", fn, None, 0))
        return (sem, self.cnt[sem])

    def replay(self, eng, q):
        for rec in self.ops[eng]:
            if rec[0] == "w":
                q.wait_ge(self.sems[rec[1]], rec[2])
            else:
                _, fn, sem, n = rec
                inst = fn(q)
                if sem is not None:
                    inst.then_inc(self.sems[sem], n)


def later(*toks):
    best = {}
    for t in toks:
        if t is None:
            continue
        s, v = t
        if best.get(s, -1) < v:
            best[s] = v
    return [(s, v) for s, v in best.items()]



def build_program(stop_after=None, attn_stop=None, grp_stop=None, vs_stop=None):
    nc = bass.Bass("TRN2", target_bir_lowering=False, debug=False)

    def din(name, shape, dt):
        return nc.dram_tensor(name, list(shape), dt, kind="ExternalInput").ap()

    xb = din("xb", (S, D), F32)
    tail_last = din("tail_last", (8, D), F32)
    tail_prev = din("tail_prev", (8, D), F32)
    alpha = din("alpha", (1, 1), F32)
    wq_d = din("wq", (128, 8 * D), BF16)
    wk_d = din("wk", (128, 8 * D), BF16)
    wv_d = din("wv", (128, 8 * D), BF16)
    wfc_d = din("wfc", (D, D), BF16)
    wo_d = din("wo", (H, HD, HD), BF16)
    bq_d = din("bq", (128, 8), F32)
    bk_d = din("bk", (128, 8), F32)
    bv_d = din("bv", (128, 8), F32)
    bfc_row_d = din("bfc_row", (1, D), BF16)
    bcon_row_d = din("bcon_row", (1, D), BF16)
    dbias_row_d = din("dbias_row", (1, D), BF16)
    wexpT_d = din("wexpT_aug", (DH, D + 1), BF16)
    wcon_d = din("wcon_pi", (DH, D), BF16)
    upbias_d = din("up_bias_pi", (128, 32), F32)
    upA_d = din("upA", (32, 7, 128, 128), BF16)
    upB_d = din("upB", (32, 5, 128, 128), BF16)
    dnB_d = din("dnB", (32, 5, 128, 128), BF16)
    dnA_d = din("dnA", (32, 7, 128, 128), BF16)
    idf_d = din("id_f32", (128, 128), F32)
    idb_d = din("id_bf", (128, 128), BF16)
    dmask_d = din("dmask", (3, 128, 256), F32)

    out_d = nc.dram_tensor("out", [S, D], F32, kind="ExternalOutput").ap()
    rt1 = nc.dram_tensor("rt1", [DH, D + 1], BF16, kind="Internal").ap()
    rt2 = nc.dram_tensor("rt2", [DH, D], BF16, kind="Internal").ap()
    weff_h = nc.dram_tensor("weff_h", [32, 8, 128, 128], BF16, kind="Internal").ap()
    weff2_h = nc.dram_tensor("weff2_h", [DH, D], BF16, kind="Internal").ap()

    Sc = Sched()

    with ExitStack() as st:
        sbuf = lambda n, sh, dt: st.enter_context(nc.sbuf_tensor(n, list(sh), dt))
        id_f = sbuf("id_f", (128, 128), F32)
        id_b = sbuf("id_b", (128, 128), BF16)
        ones_row = sbuf("ones_row", (1, 128), F32)
        ones_rowb = sbuf("ones_rowb", (1, 128), BF16)
        ones_colb = sbuf("ones_colb", (128, 1), BF16)
        zrow_b = sbuf("zrow_b", (1, 128), BF16)
        bq8 = sbuf("bq8", (128, 8), F32)
        bk_s = sbuf("bk_s", (128, 8), F32)
        bv_s = sbuf("bv_s", (128, 8), F32)
        brow_fc = sbuf("brow_fc", (1, D), BF16)
        brow_con = sbuf("brow_con", (1, D), BF16)
        brow_db = sbuf("brow_db", (1, D), BF16)
        wo_s = sbuf("wo_s", (128, 512), BF16)
        bgelu = sbuf("bgelu", (128, 32), F32)
        dm_s = sbuf("dm_s", (128, 3 * 256), F32)
        badd = sbuf("badd", (128, 3 * 512), BF16)
        vB = sbuf("vB", (128, 1), F32)
        epsB = sbuf("epsB", (128, 1), F32)
        small = sbuf("small", (128, 112), F32)
        rtmp = sbuf("rtmp", (128, 32), F32)

        AR_N = 45056
        ar = sbuf("arena", (128, AR_N), F32)
        A = ar.ap()
        Ab = A.bitcast(BF16)
        _off = [0]

        def carve(n):
            o = _off[0]
            _off[0] += n
            assert _off[0] <= AR_N, f"arena overflow {_off[0]}"
            return o

        o_ktr = carve(3072)
        o_vtr = carve(3072)
        o_qt = carve(2048)
        o_vp = carve(NSLOT * 520)
        o_x = carve(4 * 1024)
        o_x2 = carve(6 * 1024)
        o_h = carve(2 * 512)
        o_h1 = carve(2 * 512)
        o_ht = carve(1024)
        o_g = carve(4096)
        o_pt = carve(2 * 512)
        o_ot = carve(512)
        o_osb = carve(512)
        o_y = carve(2 * 1024)
        o_qkvw = carve(4 * 512)
        o_wf1 = carve(6 * 512)
        o_wf2 = carve(6 * 512)
        o_dump = carve(512)
        o_z = carve(2 * 1024)
        o_ln = carve(1024)
        o_outb = carve(1024)
        fa_cb = [o_g, o_g + 1025, o_g + 2050, o_z]
        fa_fb = [o_z + 1025, o_wf1, o_wf1 + 896, o_outb]
        fa_t = [o_wf2, o_wf2 + 1024]
        fa_t2 = [o_wf2 + 1024, o_wf2 + 2048]
        fa_p = [o_wf1 + 1792, o_wf1 + 2048]

        def arf(c0, c1):
            return A[:, c0:c1]

        def arbf(c0, c1):
            return Ab[:, 2 * c0:2 * c1]

        tl_s = arf(o_x2 + 0, o_x2 + 1024)
        tp_s = arf(o_x2 + 1024, o_x2 + 2048)
        tsq = arf(o_x2 + 2048, o_x2 + 3072)
        lnl = arf(o_x2 + 3072, o_x2 + 4096)
        lnp = arf(o_x2 + 4096, o_x2 + 5120)

        pa = st.enter_context(nc.psum_tensor("pa", [128, 4096], F32))
        pab = pa.ap().bitcast(BF16)
        ST0, ST1 = 0, 1024
        VPP = 1024
        OP = 2048
        Z0, Z1 = 1024, 2048
        TRB = 3072
        RP = 3584
        NMP = 3200
        NMB = 3456
        FLP = 3800

        for s in Sc.cnt:
            Sc.sems[s] = st.enter_context(nc.semaphore(s))
        blk = st.enter_context(nc.Block())

        def _emit_engine_blocks():
            @blk.tensor
            def _(q):
                Sc.replay("pe", q)

            @blk.scalar
            def _(q):
                Sc.replay("act", q)

            @blk.vector
            def _(q):
                Sc.replay("dve", q)

            @blk.gpsimd
            def _(q):
                Sc.replay("gp", q)

            @blk.sync
            def _(q):
                Sc.replay("sp", q)

        reg = {"b01": None, "b23": None, "b45": None, "b6": None, "b7": None}

        t_idf = Sc.add("sp", lambda q: q.dma_start(out=id_f[:], in_=idf_d), dma="scst")
        t_idb = Sc.add("sp", lambda q: q.dma_start(out=id_b[:], in_=idb_d), dma="scst")
        t_bq = Sc.add("sp", lambda q: q.dma_start(out=bq8[:], in_=bq_d), dma="scst")
        t_bk = Sc.add("sp", lambda q: q.dma_start(out=bk_s[:], in_=bk_d), dma="scst")
        t_bv = Sc.add("sp", lambda q: q.dma_start(out=bv_s[:], in_=bv_d), dma="scst")
        t_br1 = Sc.add("sp", lambda q: q.dma_start(out=brow_fc[:], in_=bfc_row_d), dma="scst")
        t_br2 = Sc.add("sp", lambda q: q.dma_start(out=brow_con[:], in_=bcon_row_d), dma="scst")
        t_br3 = Sc.add("sp", lambda q: q.dma_start(out=brow_db[:], in_=dbias_row_d), dma="scst")
        t_brz = Sc.add("dve", lambda q: q.tensor_add(brow_con[:], brow_con[:],
                                                     brow_db[:]), deps=[t_br2, t_br3])
        t_wo = Sc.add("sp", lambda q: q.dma_start(
            out=wo_s[:].rearrange("p (hh n) -> p hh n", n=64),
            in_=wo_d.rearrange("(hh two) hd n -> (two hd) hh n", two=2)), dma="scst")
        t_dm = Sc.add("sp", lambda q: q.dma_start(
            out=dm_s[:].rearrange("p (m n) -> p m n", n=256),
            in_=dmask_d.rearrange("m p n -> p m n")), dma="scst")
        t_tl = Sc.add("sp", lambda q: q.dma_start(out=tl_s[0:8, :], in_=tail_last), dma="scst")
        t_tp = Sc.add("sp", lambda q: q.dma_start(out=tp_s[0:8, :], in_=tail_prev), dma="scst")
        t_al = Sc.add("sp", lambda q: q.dma_start(out=small[0:1, 0:1], in_=alpha), dma="scst")
        t_ub = Sc.add("sp", lambda q: q.dma_start(out=bgelu[:], in_=upbias_d), dma="scst")
        Sc.final["scst"] = Sc.cnt["scst"]

        t_ones = Sc.add("dve", lambda q: q.memset(ones_row[:], 1.0))
        t_ones2 = Sc.add("dve", lambda q: q.memset(ones_rowb[:], 1.0))
        t_ones3 = Sc.add("dve", lambda q: q.memset(ones_colb[:], 1.0))
        t_zrow = Sc.add("dve", lambda q: q.memset(zrow_b[:], 0.0))
        t_eps = Sc.add("dve", lambda q: q.memset(epsB[:], EPS))
        t_bq8 = Sc.add("dve", lambda q: q.tensor_scalar_mul(bq8[:], bq8[:], 0.125),
                       deps=[t_bq])
        t_oc8 = Sc.add("dve", lambda q: q.memset(small[0:8, 1:2], 1.0))

        def ln_rows(src, dst, tok, cbase):
            c = cbase
            t1 = Sc.add("dve", lambda q: q.reduce_sum(srowap(c), src[0:8, :], axis=AX.X),
                        deps=[tok])
            t3 = Sc.add("act", lambda q: q.activation(tsq[0:8, :], src[0:8, :], AF.Square,
                                                      accum_out=srowap(c + 1)), deps=[tok])
            t4 = Sc.add("dve", lambda q: q.tensor_scalar(srowap(c + 2), srowap(c),
                                                         1.0 / D, None, op0=ALU.mult),
                        deps=[t1])
            t5 = Sc.add("dve", lambda q: q.tensor_tensor(srowap(c + 3), srowap(c + 2),
                                                         srowap(c + 2), op=ALU.mult), deps=[t4])
            t6 = Sc.add("dve", lambda q: q.tensor_scalar(srowap(c + 4), srowap(c + 1),
                                                         1.0 / D, None, op0=ALU.mult), deps=[t3])
            t7 = Sc.add("dve", lambda q: q.tensor_tensor(srowap(c + 5), srowap(c + 4),
                                                         srowap(c + 3), op=ALU.subtract),
                        deps=[t5, t6])
            t8 = Sc.add("act", lambda q: q.activation(srowap(c + 6), srowap(c + 5), AF.Ln,
                                                      bias=epsB[0:8, :]), deps=[t7, t_eps])
            t9 = Sc.add("act", lambda q: q.activation(srowap(c + 7), srowap(c + 6), AF.Exp,
                                                      scale=-0.5), deps=[t8])
            ta = Sc.add("dve", lambda q: q.tensor_scalar(srowap(c + 8), srowap(c + 2),
                                                         srowap(c + 7), -1.0, op0=ALU.mult,
                                                         op1=ALU.mult), deps=[t9])
            tb = Sc.add("act", lambda q: q.activation(dst[0:8, :], src[0:8, :], AF.Identity,
                                                      bias=srowap(c + 8),
                                                      scale=srowap(c + 7)), deps=[ta])
            return tb

        def srowap(c):
            return small[0:8, c:c + 1]

        tk1 = ln_rows(tl_s, lnl, t_tl, 2)
        tk2 = ln_rows(tp_s, lnp, t_tp, 11)
        t_fl = Sc.add("dve", lambda q: q.tensor_sub(tsq[0:8, :], lnl[0:8, :], lnp[0:8, :]),
                      deps=[tk1, tk2])
        t_fa = Sc.add("act", lambda q: q.activation(lnl[0:8, :], tsq[0:8, :], AF.Abs,
                                                    accum_out=small[0:8, 20:21]), deps=[t_fl])
        t_fs = Sc.add("pe", lambda q: q.matmul(pa[0:1, FLP:FLP + 1], small[0:8, 20:21],
                                               small[0:8, 1:2], start=True, stop=True),
                      deps=[t_fa, t_oc8])
        reg["b7"] = t_fs
        t_fm = Sc.add("dve", lambda q: q.tensor_scalar(small[0:1, 21:22], pa[0:1, FLP:FLP + 1],
                                                       1.0 / (8 * D), None, op0=ALU.mult),
                      deps=[t_fs])
        reg["b7"] = t_fm
        t_a = Sc.add("act", lambda q: q.activation(small[0:1, 22:23], small[0:1, 0:1],
                                                   AF.Sigmoid), deps=[t_al])
        t_s1 = Sc.add("dve", lambda q: q.tensor_sub(small[0:1, 23:24], small[0:1, 21:22],
                                                    lnl[0:1, 0:1]), deps=[t_fm, t_fa])
        t_s2 = Sc.add("dve", lambda q: q.tensor_tensor(small[0:1, 24:25], small[0:1, 23:24],
                                                       small[0:1, 22:23], op=ALU.mult),
                      deps=[t_s1, t_a])
        t_s3 = Sc.add("dve", lambda q: q.tensor_add(small[0:1, 25:26], small[0:1, 24:25],
                                                    lnl[0:1, 0:1]), deps=[t_s2])
        t_s4 = Sc.add("act", lambda q: q.activation(small[0:1, 26:27], small[0:1, 25:26],
                                                    AF.Sigmoid), deps=[t_s3])
        t_s5 = Sc.add("dve", lambda q: q.tensor_scalar(small[0:1, 27:28], small[0:1, 26:27],
                                                       96.0, 24.0, op0=ALU.mult, op1=ALU.add),
                      deps=[t_s4])
        t_vb1 = Sc.add("pe", lambda q: q.matmul(pa[:, NMB:NMB + 1], ones_row[:],
                                                small[0:1, 27:28], start=True, stop=True),
                       deps=[t_s5, t_ones, reg["b7"]])
        t_vb = Sc.add("dve", lambda q: q.tensor_copy(vB[:], pa[:, NMB:NMB + 1]), deps=[t_vb1])
        reg["b7"] = t_vb
        t_b1 = Sc.add("dve", lambda q: q.tensor_scalar(dm_s[:], dm_s[:], 2.0, vB[:],
                                                       op0=ALU.mult, op1=ALU.subtract),
                      deps=[t_dm, t_vb])
        t_bg = Sc.add("dve", lambda q: q.tensor_single_scalar(dm_s[:], dm_s[:], 0.0,
                                                              op=ALU.is_gt), deps=[t_b1])
        t_b01 = None
        for rep in (0, 1):
            t_b01 = Sc.add("dve", lambda q, rep=rep: q.tensor_scalar(
                badd.ap().rearrange("p (k n) -> p k n", n=512)
                [:, :, rep * 256:(rep + 1) * 256],
                dm_s[:].rearrange("p (k n) -> p k n", n=256),
                -1e30, None, op0=ALU.mult), deps=[t_bg])
        phase0_sbuf_done = t_b01

        PF0, PF1 = 0, 512
        PCH = 1024

        def fold_products(fbuf, nf, t_fac, extra):
            f0 = fbuf[0]
            fac = lambda i: Ab[:, 2 * f0 + i * 128:2 * f0 + (i + 1) * 128]
            cur = fac(0)
            tok = t_fac
            dst0b = Ab[:, 2 * fa_p[0]:2 * fa_p[0] + 128]
            dst1b = Ab[:, 2 * fa_p[0] + 128:2 * fa_p[0] + 256]
            for i in range(1, nf):
                pf = PF0 if (i % 2) else PF1
                tm = Sc.add("pe", lambda q, i=i, cur=cur, pf=pf, fac=fac:
                            q.matmul(pa[:, pf:pf + 128], fac(i), cur,
                                     start=True, stop=True),
                            deps=later(tok, reg["b01"], *extra))
                dst = dst0b if (i % 2) else dst1b
                te = Sc.add("dve", lambda q, dst=dst, pf=pf:
                            q.tensor_copy(dst, pa[:, pf:pf + 128]), deps=[tm])
                reg["b01"] = te
                cur = dst
                tok = te
            return cur, tok

        def fold_apply(lhsT, w, rhs_ap, deps):
            toks = []
            c0 = 0
            dd = list(deps)
            while c0 < w:
                cw = min(512, w - c0)
                tm = Sc.add("pe", lambda q, c0=c0, cw=cw, lhsT=lhsT, rhs_ap=rhs_ap:
                            q.matmul(pa[:, PCH + c0:PCH + c0 + cw], lhsT,
                                     rhs_ap[:, c0:c0 + cw], start=True, stop=True),
                            deps=dd)
                dd = []
                toks.append(tm)
                c0 += cw
            return toks

        rt1v = rt1.rearrange("(p cc) n -> p cc n", cc=32)
        rt2v = rt2.rearrange("(l mm) n -> mm l n", mm=128)
        w2v = weff2_h.rearrange("(p cc) n -> p cc n", cc=32)
        prev_w = None
        prev_ev = None
        prevw2 = [None, None]
        buf_free = [None] * 4
        fac_free = [None] * 4

        def fold_pass(load_src, fac_src, nfac, width, out_fn, pass_id):
            nonlocal prev_w, prev_ev
            pass_start = later(prevw2[0], prevw2[1])
            pass_start = pass_start if pass_start else [None]
            pass_start0 = list(pass_start)
            for c in range(32):
                parity = c % 4
                cb = [fa_cb[parity]]
                fb = [fa_fb[parity]]
                t_l = Sc.add("sp", lambda q, c=c, cb=cb: q.dma_start(
                    out=Ab[:, 2 * cb[0]:2 * cb[0] + width],
                    in_=load_src(c)), deps=later(buf_free[parity], *pass_start0),
                    dma="sfc%d" % parity)
                t_f = Sc.add("gp", lambda q, c=c, fb=fb, nfac=nfac: q.dma_start(
                    out=Ab[:, 2 * fb[0]:2 * fb[0] + nfac * 128]
                    .rearrange("p (f n) -> p f n", n=128),
                    in_=fac_src(c)), deps=later(fac_free[parity]), dma="sff%d" % parity)
                lhsT, tokp = fold_products(fb, nfac, t_f, [])
                fac_free[parity] = tokp
                toks = fold_apply(lhsT, width, Ab[:, 2 * cb[0]:2 * cb[0] + width],
                                  later(tokp, t_l, reg["b23"], reg["b45"]))
                buf_free[parity] = toks[-1]
                prev_w, prev_ev = out_fn(c, toks)
                reg["b23"] = prev_ev
                reg["b45"] = prev_ev

        def outA(c, toks):
            ft = fa_t if c % 2 == 0 else fa_t2
            e1 = Ab[:, 2 * ft[0]:2 * ft[0] + 1025]
            t1 = Sc.add("dve", lambda q, e1=e1: q.tensor_copy(e1, pa[:, PCH:PCH + 1025]),
                        deps=toks + later(prevw2[c % 2]))
            w1 = Sc.add("sp", lambda q, c=c, e1=e1: q.dma_start(
                out=rt1v[:, c, :], in_=e1), deps=[t1], dma="sfold%d" % (c % 2))
            prevw2[c % 2] = w1
            return w1, t1

        fold_pass(lambda c: wexpT_d[c * 128:(c + 1) * 128, :],
                  lambda c: upA_d[c].rearrange("f pp n -> pp f n"), 7, 1025, outA, 0)

        t_rt1_done = prev_w

        def outB(c, toks):
            ft = fa_t if c % 2 == 0 else fa_t2
            tb = Sc.add("dve", lambda q, c=c: q.tensor_add(
                bgelu[:, c:c + 1], pa[:, PCH + 1024:PCH + 1025], bgelu[:, c:c + 1]),
                deps=toks + [t_ub])
            evb = arbf(ft[0], ft[0] + 512)
            e1 = Sc.add("act", lambda q, evb=evb: q.copy(evb, pa[:, PCH:PCH + 1024]),
                        deps=toks + later(prevw2[c % 2]))
            ttr = []
            dep0 = later(e1, t_idb, reg["b6"])
            for dc in range(8):
                tt = Sc.add("pe", lambda q, dc=dc, evb=evb: q.transpose(
                    pab[:, 2 * TRB + dc * 128:2 * TRB + (dc + 1) * 128],
                    evb[:, dc * 128:(dc + 1) * 128], id_b[:]),
                    deps=dep0 if dc == 0 else [], inc=(dc == 7))
                ttr.append(tt)
            e2 = Sc.add("act", lambda q, ft=ft: q.copy(arbf(ft[0] + 512, ft[0] + 1024),
                                                       pab[:, 2 * TRB:2 * TRB + 1024]),
                        deps=[ttr[-1]])
            reg["b6"] = e2
            w1 = Sc.add("sp", lambda q, c=c, ft=ft: q.dma_start(
                out=weff_h[c].rearrange("dc pp n -> pp dc n"),
                in_=arbf(ft[0] + 512, ft[0] + 1024).rearrange(
                    "p (dc n) -> p dc n", n=128)), deps=[e2], dma="sfold%d" % (c % 2))
            prevw2[c % 2] = w1
            return w1, tb

        fold_pass(lambda c: rt1[c * 128:(c + 1) * 128, :],
                  lambda c: upB_d[c].rearrange("f pp n -> pp f n"), 5, 1025, outB, 1)
        t_bgelu_done = prev_ev

        def outDB(c, toks):
            ft = fa_t if c % 2 == 0 else fa_t2
            e1 = Ab[:, 2 * ft[0]:2 * ft[0] + 1024]
            t1 = Sc.add("dve", lambda q, e1=e1: q.tensor_copy(e1, pa[:, PCH:PCH + 1024]),
                        deps=toks + later(prevw2[c % 2]))
            w1 = t1
            for i in range(4):
                w1 = Sc.add("sp", lambda q, c=c, e1=e1, i=i: q.dma_start(
                    out=rt2v[4 * c + i], in_=e1[32 * i:32 * (i + 1), :]),
                    deps=[t1], dma="sfold%d" % (c % 2))
            prevw2[c % 2] = w1
            return w1, t1

        fold_pass(lambda c: wcon_d[c * 128:(c + 1) * 128, :],
                  lambda c: dnB_d[c].rearrange("f pp n -> pp f n"), 5, 1024, outDB, 2)

        def outDA(c, toks):
            ft = fa_t if c % 2 == 0 else fa_t2
            evb = arbf(ft[0], ft[0] + 512)
            e1 = Sc.add("act", lambda q, evb=evb: q.copy(evb, pa[:, PCH:PCH + 1024]),
                        deps=toks + later(prevw2[c % 2]))
            w1 = Sc.add("sp", lambda q, c=c, evb=evb: q.dma_start(
                out=w2v[:, c, :], in_=evb), deps=[e1], dma="sfold%d" % (c % 2))
            prevw2[c % 2] = w1
            return w1, e1

        fold_pass(lambda c: rt2[c * 128:(c + 1) * 128, :],
                  lambda c: dnA_d[c].rearrange("f pp n -> pp f n"), 7, 1024, outDA, 3)
        t_folds = [prevw2[0], prevw2[1]]

        tok_x = {}
        tok_x2 = {}
        tok_xfree = {}
        tok_x2free = {}
        tok_vp = {}
        tok_grp_evac = {}
        attn_done = {}
        qkv_mm_done = {}
        ffn_mm_done = {}
        NWQ = 4
        NWF = 6
        tok_qt_free = [None, None]
        tok_wq_free = [None] * NWQ
        tok_wf1_free = [None] * NWF
        tok_wf2_free = [None] * NWF
        tok_vslot_free = {}
        st_free = [None, None]
        pt_free = [None, None]
        negm_col_free = [None] * 4
        cm_col_free = [None] * 4
        ar_col_free = [None] * 4
        rtmp_free = [None]
        tok_dump_free = [None, None]
        dump_ctr = [0]
        tok_h_free = [None, None]
        tok_h1_free = [None, None]
        h3tok = {}
        grp_h1 = {}
        grp_wl = {}
        ffn_wl = {}
        ffn_wl2 = {}
        tok_ht_writer = [None]
        tok_ht_readers = [None]
        tok_osb_free = [None]
        tok_ot_free = [None]
        tok_y_free = [None, None]
        tok_ln_free = [None]
        tok_z_free = [None, None]
        tok_zsb_free = [None, None]
        tok_out_free = [None, None]
        tok_g_free = [t_folds[0] if t_folds[0] else None]
        for _tf in t_folds:
            tok_g_free[0] = _tf if _tf else tok_g_free[0]
        for _sl5 in range(5):
            tok_x2free[_sl5] = phase0_sbuf_done
        tok_out_free[0] = t_folds[1] if len(t_folds) > 1 and t_folds[1] else t_folds[0]
        tok_zsb_free[0] = tok_zsb_free[1] = tok_g_free[0]
        gel_free = [None, None]
        store_toks = []
        ln_set = [0]
        pend_ln2 = []
        pend_ln4 = []
        pair_acc = {"c": None, "ts": [], "yes": []}

        def emit_ln(src_ap, dst_ap, deps_src):
            c = 40 + 8 * (ln_set[0] % 6)
            ln_set[0] += 1
            di = 0
            dump_ctr[0] += 1
            dmp = arbf(o_dump + di * 512, o_dump + (di + 1) * 512)
            sets_prev = ln_guard.get(c)
            t1 = Sc.add("dve", lambda q: q.tensor_scalar(dmp, src_ap, 1.0, 0.0,
                                                         op0=ALU.mult, op1=ALU.add,
                                                         accum_out=small[:, c:c + 1]),
                        deps=later(*deps_src, tok_dump_free[di], sets_prev))
            t2 = Sc.add("dve", lambda q: q.scalar_tensor_tensor(
                dmp, src_ap, 1.0, src_ap, op0=ALU.mult, op1=ALU.mult,
                accum_out=small[:, c + 1:c + 2]), deps=[t1])
            tok_dump_free[di] = t2
            t3 = Sc.add("dve", lambda q: q.tensor_scalar(small[:, c + 2:c + 3],
                                                         small[:, c:c + 1], 1.0 / D, None,
                                                         op0=ALU.mult), deps=[t1])
            t4 = Sc.add("dve", lambda q: q.tensor_tensor(small[:, c + 3:c + 4],
                                                         small[:, c + 2:c + 3],
                                                         small[:, c + 2:c + 3],
                                                         op=ALU.mult), deps=[t3])
            t5 = Sc.add("dve", lambda q: q.tensor_scalar(small[:, c + 4:c + 5],
                                                         small[:, c + 1:c + 2], 1.0 / D,
                                                         small[:, c + 3:c + 4],
                                                         op0=ALU.mult, op1=ALU.subtract),
                        deps=[t2, t4])
            t6 = Sc.add("act", lambda q: q.activation(small[:, c + 5:c + 6],
                                                      small[:, c + 4:c + 5], AF.Ln,
                                                      bias=epsB[:]), deps=[t5])
            t7 = Sc.add("act", lambda q: q.activation(small[:, c + 6:c + 7],
                                                      small[:, c + 5:c + 6], AF.Exp,
                                                      scale=-0.5), deps=[t6])
            t8 = Sc.add("dve", lambda q: q.tensor_scalar(small[:, c + 7:c + 8],
                                                         small[:, c + 2:c + 3],
                                                         small[:, c + 6:c + 7], -1.0,
                                                         op0=ALU.mult, op1=ALU.mult),
                        deps=[t7])
            t9 = Sc.add("act", lambda q: q.activation(dst_ap, src_ap, AF.Identity,
                                                      bias=small[:, c + 7:c + 8],
                                                      scale=small[:, c + 6:c + 7]),
                        deps=[t8])
            ln_guard[c] = t9
            return t9

        ln_guard = {}

        ktv = arbf(o_ktr, o_ktr + 3072).rearrange("p (ch n) -> p ch n", n=RING)
        vtv = arbf(o_vtr, o_vtr + 3072).rearrange("p (ch n) -> p ch n", n=RING)
        htv = arbf(o_ht, o_ht + 1024)

        def emit_vslot(w, gdep):
            sl = w % NSLOT
            vp0 = o_vp + sl * 520
            if w == 0:
                segs = [(64, 0, 64)]
            elif w == NT:
                segs = [(0, w * T - 64, 64)]
            else:
                segs = [(0, w * T - 64, 128)]
            vjobs = []
            for h in list(range(0, H, 2)) + list(range(1, H, 2)):
                for (pb, ts0, cnt) in segs:
                    p0 = ts0 % RING
                    take = min(cnt, RING - p0)
                    parts = ((pb, take, p0),) if take == cnt else \
                        ((pb, take, p0), (pb + take, cnt - take, 0))
                    for (mb, cnt2, rp) in parts:
                        vjobs.append((h, mb, cnt2, rp))
            vbanks = [(h % 2, mb) for (h, mb, _, _) in vjobs]
            last = None
            for ji, (h, mb, cnt2, rp) in enumerate(vjobs):
                b = vbanks[ji]
                vs = (vbanks.index(b) == ji,
                      (len(vbanks) - 1 - vbanks[::-1].index(b)) == ji)
                dd = later(gdep, t_wo, reg["b23"],
                           tok_vslot_free.get(sl)) if ji == 0 else []
                vc = (h % 2) * 512 + (h // 2) * 64
                last = Sc.add("pe", lambda q, h=h, mb=mb, cnt2=cnt2, rp=rp, vs=vs, vc=vc:
                              q.matmul(pa[mb:mb + cnt2,
                                          VPP + vc:VPP + vc + 64],
                                       vtv[(h % 2) * 64:(h % 2) * 64 + 64, h // 2,
                                           rp:rp + cnt2],
                                       wo_s[(h % 2) * 64:(h % 2) * 64 + 64,
                                            (h // 2) * 64:(h // 2) * 64 + 64],
                                       start=vs[0], stop=vs[1]),
                              deps=dd, inc=(ji == len(vjobs) - 1),
                              base=(h % 2) * 64)
            if vs_stop == "mm":
                tok_vp[w] = last
                return
            if w == 0:
                evrows = (64, 128)
            elif w == NT:
                evrows = (0, 64)
            else:
                evrows = (0, 128)
            t_ev = Sc.add("act", lambda q, vp0=vp0, evrows=evrows: q.copy(
                arbf(vp0, vp0 + 512)[evrows[0]:evrows[1], :],
                pa[evrows[0]:evrows[1], VPP:VPP + 1024]), deps=[last])
            reg["b23"] = t_ev
            if w == 0 or w == NT:
                z0, z1 = (0, 64) if w == 0 else (64, 128)
                t_ev = Sc.add("dve", lambda q, vp0=vp0, z0=z0, z1=z1: q.memset(
                    arbf(vp0, vp0 + 512)[z0:z1, :], 0.0), deps=[t_ev])
            if vs_stop == "ev":
                tok_vp[w] = t_ev
                return
            tok_vp[w] = t_ev

        def emit_pre_x(gidx):
            g0 = gidx * GT
            c = 40 + 8 * (ln_set[0] % 6)
            ln_set[0] += 1
            tx = []
            srcs = []
            for ii in range(GT):
                ti = g0 + ii
                sl = ti % 4
                t_x = Sc.add("sp", lambda q, ti=ti, sl=sl: q.dma_start(
                    out=arf(o_x + sl * 1024, o_x + (sl + 1) * 1024),
                    in_=xb[ti * T:(ti + 1) * T, :]),
                    deps=later(tok_xfree.get(sl)), dma="sx%d" % sl)
                tok_x[ti] = t_x
                tx.append(t_x)
                srcs.append(arf(o_x + sl * 1024, o_x + (sl + 1) * 1024))
            sets_prev = ln_guard.get(c)
            tss = []
            for ii in range(GT):
                di = 0
                dump_ctr[0] += 1
                dmp = arbf(o_dump + di * 512, o_dump + (di + 1) * 512)
                t1 = Sc.add("dve", lambda q, ii=ii, c=c, dmp=dmp: q.tensor_scalar(
                    dmp, srcs[ii], 1.0, 0.0, op0=ALU.mult, op1=ALU.add,
                    accum_out=small[:, c + ii:c + ii + 1]),
                    deps=later(tx[ii], tok_dump_free[di],
                               sets_prev if ii == 0 else None))
                tok_dump_free[di] = t1
                di = 0
                dump_ctr[0] += 1
                dmp = arbf(o_dump + di * 512, o_dump + (di + 1) * 512)
                t2 = Sc.add("dve", lambda q, ii=ii, c=c, dmp=dmp:
                            q.scalar_tensor_tensor(
                                dmp, srcs[ii], 1.0, srcs[ii],
                                op0=ALU.mult, op1=ALU.mult,
                                accum_out=small[:, c + 2 + ii:c + 3 + ii]),
                            deps=later(tx[ii], tok_dump_free[di]))
                tok_dump_free[di] = t2
                tss.append((t1, t2))
            d1 = Sc.add("dve", lambda q, c=c: q.tensor_scalar(
                small[:, c + 4:c + 6], small[:, c:c + 2], -1.0 / D, None,
                op0=ALU.mult), deps=[tss[0][0], tss[1][0]])
            d2 = Sc.add("dve", lambda q, c=c: q.scalar_tensor_tensor(
                small[:, c:c + 2], small[:, c + 4:c + 6], 1.0,
                small[:, c + 4:c + 6], op0=ALU.mult, op1=ALU.mult), deps=[d1])
            d3 = Sc.add("dve", lambda q, c=c: q.scalar_tensor_tensor(
                small[:, c + 2:c + 4], small[:, c + 2:c + 4], 1.0 / D,
                small[:, c:c + 2], op0=ALU.mult, op1=ALU.subtract),
                deps=[tss[0][1], tss[1][1], d2])
            a1 = Sc.add("act", lambda q, c=c: q.activation(
                small[:, c:c + 2], small[:, c + 2:c + 4], AF.Ln, bias=epsB[:]),
                deps=[d3])
            a2 = Sc.add("act", lambda q, c=c: q.activation(
                small[:, c + 2:c + 4], small[:, c:c + 2], AF.Exp, scale=-0.5),
                deps=[a1])
            toks = []
            for ii in range(GT):
                a3 = Sc.add("dve", lambda q, ii=ii, c=c: q.tensor_scalar(
                    small[:, c + 6 + ii:c + 7 + ii], small[:, c + 4 + ii:c + 5 + ii],
                    small[:, c + 2 + ii:c + 3 + ii], None, op0=ALU.mult), deps=[a2])
                hdst = arbf(o_h1 + ii * 512, o_h1 + (ii + 1) * 512)
                t9 = Sc.add("dve", lambda q, ii=ii, c=c, hdst=hdst: q.tensor_scalar(
                    hdst, srcs[ii],
                    small[:, c + 2 + ii:c + 3 + ii],
                    small[:, c + 6 + ii:c + 7 + ii],
                    op0=ALU.mult, op1=ALU.add),
                    deps=later(a3, tok_h1_free[ii]))
                toks.append(t9)
            ln_guard[c] = toks[-1]
            grp_h1[gidx] = toks

        def emit_group_pre(gidx):
            wl = []
            for nf in range(NWQ):
                wi = nf
                t_wl = Sc.add("sp", lambda q, nf=nf, wi=wi: q.dma_start(
                    out=arbf(o_qkvw + wi * 512, o_qkvw + (wi + 1) * 512),
                    in_=wq_d[:, nf * 1024:(nf + 1) * 1024]),
                    deps=later(tok_wq_free[wi]), dma="gw%d" % (wi % 4))
                wl.append(t_wl)
            grp_wl[gidx] = wl
            if gidx >= 2:
                p2 = gidx - 2
                wl2 = []
                for hc in range(NWF):
                    wi = hc
                    t_wl = Sc.add("sp", lambda q, hc=hc, wi=wi: q.dma_start(
                        out=arbf(o_wf2 + wi * 512, o_wf2 + (wi + 1) * 512),
                        in_=weff2_h[hc * 128:(hc + 1) * 128, :]),
                        deps=later(tok_wf2_free[wi], *t_folds), dma="gg%d" % (wi % 4))
                    wl2.append(t_wl)
                ffn_wl2[p2] = wl2
                wl1 = []
                for hc in range(NWF):
                    wi = hc
                    t_wl = Sc.add("gp", lambda q, hc=hc, wi=wi: q.dma_start(
                        out=arbf(o_wf1 + wi * 512, o_wf1 + (wi + 1) * 512).rearrange(
                            "p (dc n) -> p dc n", n=128),
                        in_=weff_h[hc].rearrange("dc pp n -> pp dc n")),
                        deps=later(tok_wf1_free[wi], *t_folds), dma="gf%d" % (wi % 4))
                    wl1.append(t_wl)
                ffn_wl[p2] = wl1

        def emit_group(gidx):
            g0 = gidx * GT
            pos = (g0 * T) % RING
            ht_deps = later(ffn_mm_done.get(gidx - 2))
            toks_ht = []
            for ii in range(GT):
                hdst = arbf(o_h1 + ii * 512, o_h1 + (ii + 1) * 512)
                t_h = grp_h1[gidx][ii]
                ttr = []
                d0 = later(t_h, t_idb, reg["b6"]) + ht_deps
                for dc in range(8):
                    tt = Sc.add("pe", lambda q, dc=dc, hdst=hdst: q.transpose(
                        pab[:, 2 * TRB + dc * 128:2 * TRB + (dc + 1) * 128],
                        hdst[:, dc * 128:(dc + 1) * 128], id_b[:]),
                        deps=d0 if dc == 0 else [], inc=(dc == 7))
                    ttr.append(tt)
                te = Sc.add("act", lambda q, ii=ii: q.copy(
                    htv.rearrange("p (dc n) -> p dc n", n=256)[:, :, ii * 128:(ii + 1) * 128],
                    pab[:, 2 * TRB:2 * TRB + 1024].rearrange("p (dc n) -> p dc n", n=128)),
                    deps=[ttr[-1]])
                reg["b6"] = te
                tok_h1_free[ii] = te
                toks_ht.append(te)
            if grp_stop == "ln":
                tok_grp_evac[gidx] = toks_ht[-1]
                qkv_mm_done[gidx] = None
                return
            wb = [arbf(o_qkvw + i * 512, o_qkvw + (i + 1) * 512) for i in range(NWQ)]
            qkv_ps_free = [reg["b01"], reg["b01"]]
            nw = [0]
            last_ev = None
            last_mm = None
            for (wd, kindp) in ((wq_d, "q"), (wk_d, "k"), (wv_d, "v")):
                for nf in range(8):
                    wi = nw[0] % NWQ
                    if nw[0] < NWQ:
                        t_wl = grp_wl[gidx][nf]
                    else:
                        t_wl = Sc.add("sp", lambda q, wd=wd, nf=nf, wi=wi: q.dma_start(
                            out=arbf(o_qkvw + wi * 512, o_qkvw + (wi + 1) * 512),
                            in_=wd[:, nf * 1024:(nf + 1) * 1024]),
                            deps=later(tok_wq_free[wi]), dma="gw%d" % (wi % 4))
                    nw[0] += 1
                    pp = (nf % 2) * 512
                    mms = []
                    for dc in range(8):
                        tm = Sc.add("pe", lambda q, dc=dc, pp=pp, wi=wi: q.matmul(
                            pa[:, pp:pp + 256], wb[wi][:, dc * 128:(dc + 1) * 128],
                            htv[:, dc * 256:(dc + 1) * 256],
                            start=(dc == 0), stop=(dc == 7)),
                            deps=later(t_wl, *toks_ht, qkv_ps_free[nf % 2])
                            if dc == 0 else [], inc=(dc == 7))
                        mms.append(tm)
                    last_mm = mms[-1]
                    tok_wq_free[wi] = mms[-1]
                    if kindp == "q":
                        dst = arbf(o_qt + (gidx % 2) * 1024,
                                   o_qt + (gidx % 2) * 1024 + 1024)[:, nf * 256:(nf + 1) * 256]
                        te = Sc.add("dve", lambda q, dst=dst, pp=pp, nf=nf:
                                    q.tensor_scalar(dst, pa[:, pp:pp + 256], 0.125,
                                                    bq8[:, nf:nf + 1],
                                                    op0=ALU.mult, op1=ALU.add),
                                    deps=later(mms[-1], t_bq8,
                                               attn_done.get(gidx - 1)))
                    else:
                        rv = ktv if kindp == "k" else vtv
                        te = Sc.add("dve", lambda q, rv=rv, pp=pp, nf=nf, pos=pos,
                                    bb=(bk_s if kindp == "k" else bv_s):
                                    q.tensor_scalar(rv[:, nf, pos:pos + 256],
                                                    pa[:, pp:pp + 256],
                                                    bb[:, nf:nf + 1], None, op0=ALU.add),
                                    deps=later(mms[-1], t_bk, t_bv,
                                               attn_done.get(gidx - 2)))
                    qkv_ps_free[nf % 2] = te
                    last_ev = te
            reg["b01"] = last_ev
            qkv_mm_done[gidx] = last_mm
            tok_grp_evac[gidx] = last_ev
            if grp_stop == "qkv":
                return
            for w in (g0, g0 + 1):
                emit_vslot(w, last_ev)

        def emit_attn(t):
            if attn_stop == "skip":
                return None, (lambda: None)
            mk = 1 if t == 0 else (2 if t == NT - 1 else 0)
            b01t = badd.ap()[:, mk * 512:(mk + 1) * 512]
            gq = t // GT
            qtb = arbf(o_qt + (gq % 2) * 1024, o_qt + (gq % 2) * 1024 + 1024)
            qoff = (t % GT) * 128
            gready = tok_grp_evac.get((t + 1) // GT if (t + 1) < NT else t // GT)
            wave_state = {}
            av_last_l = [None]

            def av_wave(wv):
                t_msk, ptb = wave_state[wv]
                avfirst = True
                last = None
                for hh in range(4):
                    h = wv * 4 + hh
                    for kt in range(2):
                        w = t + kt
                        vp0 = o_vp + (w % NSLOT) * 520
                        dd = later(t_msk, tok_vp[t], tok_vp[t + 1],
                                   reg["b45"] if wv == 0 else None,
                                   reg["b7"] if wv == 0 else None) if avfirst else \
                            later(t_msk)
                        avfirst = False
                        o_start = (h % 8 == 0 and kt == 0)
                        o_stop = (h % 8 == 7 and kt == 1)
                        r_start = (h == 0 and kt == 0)
                        r_stop = (h == 15 and kt == 1)
                        sb_ = (hh % 2) * 2 + hh // 2
                        vs_ = (h % 2) * 8 + h // 2
                        Sc.add("pe", lambda q, sb_=sb_, vs_=vs_, h=h, kt=kt, vp0=vp0,
                               ptb=ptb, o_start=o_start, o_stop=o_stop:
                               q.matmul(pa[:, OP + h * 64:OP + h * 64 + 64],
                                        ptb[:, sb_ * 256 + kt * 128:sb_ * 256 + kt * 128 + 128],
                                        arbf(vp0, vp0 + 520)[:, vs_ * 64:vs_ * 64 + 64],
                                        start=o_start, stop=o_stop),
                               deps=dd, inc=False)
                        last = Sc.add("pe", lambda q, sb_=sb_, h=h, kt=kt, ptb=ptb,
                                      r_start=r_start, r_stop=r_stop:
                                      q.matmul(pa[:, RP + h:RP + h + 1],
                                               ptb[:, sb_ * 256 + kt * 128:
                                                   sb_ * 256 + kt * 128 + 128],
                                               ones_colb[:], start=r_start, stop=r_stop),
                                      deps=[t_ones3] if (wv == 0 and hh == 0 and kt == 0)
                                      else [], inc=(hh == 3 and kt == 1))
                pt_free[wv % 2] = last
                av_last_l[0] = last

            for wv in range(4):
                stp = ST0 if (wv % 2 == 0) else ST1
                regk = "b01" if (wv % 2 == 0) else "b23"
                jobs = []
                job_bases_l = []
                wraps = any(((t * T - 64 + kk * 128) % RING) + 128 > RING
                            for kk in range(2))
                split_mode = bool(mk) or wraps
                bank_regions = ({}, {})
                for hh in (0, 2, 1, 3):
                    h = wv * 4 + hh
                    for kt in range(2):
                        k0 = t * T - 64 + kt * 128
                        col = stp + (hh % 2) * 512 + (hh // 2) * 256 + kt * 128
                        if split_mode:
                            segs = []
                            for half in range(2):
                                kk = k0 + half * 64
                                if 0 <= kk and kk + 64 <= S:
                                    segs.append((half * 64, kk, 64))
                        else:
                            segs = [(0, k0, 128)]
                        for (mb, ts0, cnt) in segs:
                            p0 = ts0 % RING
                            take = min(cnt, RING - p0)
                            parts = ((mb, take, p0),) if take == cnt else \
                                ((mb, take, p0), (mb + take, cnt - take, 0))
                            for (mb2, cnt2, rp) in parts:
                                jobs.append((lambda q, h=h, mb2=mb2, cnt2=cnt2, rp=rp,
                                             col=col, qoff=qoff, qtb=qtb, ss=None:
                                             q.matmul(
                                    pa[mb2:mb2 + cnt2, col:col + 128],
                                    ktv[(h % 2) * 64:(h % 2) * 64 + 64, h // 2, rp:rp + cnt2],
                                    qtb[(h % 2) * 64:(h % 2) * 64 + 64,
                                        (h // 2) * 256 + qoff:(h // 2) * 256 + qoff + 128],
                                    start=ss[0], stop=ss[1]), None, col, mb2))
                                job_bases_l.append((h % 2) * 64)
                                br = bank_regions[hh % 2]
                                br[mb2] = max(br.get(mb2, 0), cnt2)
                        if split_mode:
                            have = {mb for (mb, _, _) in segs}
                            for pb in (0, 64):
                                if pb not in have:
                                    br = bank_regions[hh % 2]
                                    br[pb] = max(br.get(pb, 0), 64)
                for bank in (0, 1):
                    for mb2 in sorted(bank_regions[bank]):
                        cnt2 = bank_regions[bank][mb2]
                        jobs.append((lambda q, bank=bank, mb2=mb2, cnt2=cnt2,
                                     stp=stp, b01t=b01t, ss=None: q.matmul(
                            pa[mb2:mb2 + cnt2, stp + bank * 512:stp + bank * 512 + 512],
                            id_b[:, mb2:mb2 + cnt2], b01t,
                            start=ss[0], stop=ss[1]), None, stp + bank * 512, mb2))
                        job_bases_l.append(0)
                keys = [((c - stp) // 512, mb) for (_, _, c, mb) in jobs]
                job_bases = job_bases_l
                t_mmw = None
                for ji, (fn, extra, col, mb) in enumerate(jobs):
                    k = keys[ji]
                    ss = (keys.index(k) == ji,
                          (len(keys) - 1 - keys[::-1].index(k)) == ji)
                    fn2 = (lambda q, fn=fn, ss=ss: fn(q, ss=ss))
                    dd = later(gready, tok_grp_evac[t // GT], st_free[wv % 2],
                               reg[regk], extra, t_b01, t_idb) if ji == 0 else \
                        later(extra)
                    t_mmw = Sc.add("pe", fn2, deps=dd, inc=(ji == len(jobs) - 1),
                                   base=job_bases[ji])
                if wv - 1 in wave_state:
                    av_wave(wv - 1)
                if attn_stop in ("st", "cm", "negm"):
                    st_free[wv % 2] = t_mmw
                    av_last_l[0] = t_mmw
                    continue
                ptb = arbf(o_pt + (wv % 2) * 512, o_pt + (wv % 2) * 512 + 512)
                t_exp = Sc.add("act", lambda q, stp=stp, ptb=ptb: q.activation(
                    ptb, pa[:, stp:stp + 1024], AF.Exp),
                    deps=later(t_mmw, pt_free[wv % 2]))
                st_free[wv % 2] = t_exp
                reg[regk] = t_exp
                if attn_stop == "exp":
                    av_last_l[0] = t_exp
                    pt_free[wv % 2] = t_exp
                    continue
                wave_state[wv] = (t_exp, ptb)
            if 3 in wave_state:
                av_wave(3)
            if attn_stop == "waves":
                return av_last_l[0], (lambda: None)
            t_r2 = Sc.add("dve", lambda q: q.reciprocal(rtmp[:, 16:32], pa[:, RP:RP + 16]),
                          deps=later(av_last_l[0], rtmp_free[0]))
            reg["b7"] = t_r2
            rin = bass.AP(rtmp.ap().tensor, 16, [[32, 128], [1, 16], [0, 64]])
            t_oe = Sc.add("dve", lambda q, rin=rin: q.tensor_tensor(
                arbf(o_osb, o_osb + 512).rearrange("p (hh n) -> p hh n", n=64),
                pa[:, OP:OP + 1024].rearrange("p (hh n) -> p hh n", n=64),
                rin, op=ALU.mult), deps=later(t_r2, tok_osb_free[0]))
            rtmp_free[0] = t_oe
            reg["b45"] = t_oe
            if attn_stop == "rinv":
                return t_oe, (lambda: None)
            ttr = []
            d0 = later(t_oe, t_idb, reg["b6"])
            for dc in range(8):
                tt = Sc.add("pe", lambda q, dc=dc: q.transpose(
                    pab[:, 2 * TRB + dc * 128:2 * TRB + (dc + 1) * 128],
                    arbf(o_osb, o_osb + 512)[:, dc * 128:(dc + 1) * 128], id_b[:]),
                    deps=d0 if dc == 0 else [], inc=(dc == 7))
                ttr.append(tt)
            t_ote = Sc.add("dve", lambda q: q.tensor_copy(arbf(o_ot, o_ot + 512),
                                                          pab[:, 2 * TRB:2 * TRB + 1024]),
                           deps=later(ttr[-1], tok_ot_free[0]))
            reg["b6"] = t_ote
            tok_osb_free[0] = t_ote
            if attn_stop == "otr":
                return t_ote, (lambda: None)
            wb = [arbf(o_qkvw + i * 512, o_qkvw + (i + 1) * 512) for i in range(NWQ)]
            mms = []
            for dc in range(8):
                wi = dc % NWQ
                t_wl = Sc.add("sp", lambda q, dc=dc, wi=wi: q.dma_start(
                    out=wb[wi][:], in_=wfc_d[dc * 128:(dc + 1) * 128, :]),
                    deps=later(tok_wq_free[wi]), dma="gfc%d" % (wi % 4))
                for nf2 in range(2):
                    tm = Sc.add("pe", lambda q, dc=dc, nf2=nf2, wi=wi: q.matmul(
                        pa[:, OP + nf2 * 512:OP + (nf2 + 1) * 512],
                        arbf(o_ot, o_ot + 512)[:, dc * 128:(dc + 1) * 128],
                        wb[wi][:, nf2 * 512:(nf2 + 1) * 512],
                        start=(dc == 0), stop=False),
                        deps=later(t_wl, t_ote,
                                   reg["b45"] if (dc == 0 and nf2 == 0) else None,
                                   tok_y_free[t % 2] if (dc == 0 and nf2 == 0) else None),
                        inc=(nf2 == 1))
                    mms.append(tm)
                tok_wq_free[wi] = mms[-1]
            for nf2 in range(2):
                tm = Sc.add("pe", lambda q, nf2=nf2: q.matmul(
                    pa[:, OP + nf2 * 512:OP + (nf2 + 1) * 512], ones_rowb[:],
                    brow_fc[0:1, nf2 * 512:(nf2 + 1) * 512], start=False, stop=True),
                    deps=later(t_ones2, t_br1), inc=(nf2 == 1))
                mms.append(tm)
            t_fc = mms[-1]
            si = t % 2
            if t % 2 == 0:
                pair_acc["c"] = 40 + 8 * (ln_set[0] % 6)
                ln_set[0] += 1
                pair_acc["ts"] = []
                pair_acc["yes"] = []
            c = pair_acc["c"]
            idx = t % 2
            ysb = arf(o_y + si * 1024, o_y + (si + 1) * 1024)
            t_ye = Sc.add("dve", lambda q, c=c, idx=idx, ysb=ysb: q.tensor_scalar(
                ysb, pa[:, OP:OP + 1024], 1.0, 0.0, op0=ALU.mult, op1=ALU.add,
                accum_out=small[:, c + idx:c + idx + 1]),
                deps=later(t_fc, tok_y_free[si],
                           ln_guard.get(c) if idx == 0 else None))
            reg["b45"] = t_ye
            pair_acc["ts"].append(t)
            pair_acc["yes"].append(t_ye)
            if t % 2 == 0:
                return t_fc, None

            def ln23_pair(c=c, ts=tuple(pair_acc["ts"]), yes=tuple(pair_acc["yes"])):
                ysbs = [arf(o_y + (t_ % 2) * 1024, o_y + (t_ % 2 + 1) * 1024)
                        for t_ in ts]
                sqs = []
                for i_ in range(2):
                    di = 0
                    dump_ctr[0] += 1
                    dmp = arbf(o_dump + di * 512, o_dump + (di + 1) * 512)
                    t2 = Sc.add("dve", lambda q, i_=i_, c=c, dmp=dmp:
                                q.scalar_tensor_tensor(
                                    dmp, ysbs[i_], 1.0, ysbs[i_],
                                    op0=ALU.mult, op1=ALU.mult,
                                    accum_out=small[:, c + 2 + i_:c + 3 + i_]),
                                deps=later(yes[i_], tok_dump_free[di]))
                    tok_dump_free[di] = t2
                    sqs.append(t2)
                d1 = Sc.add("dve", lambda q, c=c: q.tensor_scalar(
                    small[:, c + 4:c + 6], small[:, c:c + 2], -1.0 / D, None,
                    op0=ALU.mult), deps=[yes[0], yes[1]])
                d2 = Sc.add("dve", lambda q, c=c: q.scalar_tensor_tensor(
                    small[:, c:c + 2], small[:, c + 4:c + 6], 1.0,
                    small[:, c + 4:c + 6], op0=ALU.mult, op1=ALU.mult), deps=[d1])
                d3 = Sc.add("dve", lambda q, c=c: q.scalar_tensor_tensor(
                    small[:, c + 2:c + 4], small[:, c + 2:c + 4], 1.0 / D,
                    small[:, c:c + 2], op0=ALU.mult, op1=ALU.subtract),
                    deps=[sqs[0], sqs[1], d2])
                a1 = Sc.add("act", lambda q, c=c: q.activation(
                    small[:, c:c + 2], small[:, c + 2:c + 4], AF.Ln, bias=epsB[:]),
                    deps=[d3])
                a2 = Sc.add("act", lambda q, c=c: q.activation(
                    small[:, c + 2:c + 4], small[:, c:c + 2], AF.Exp, scale=-0.5),
                    deps=[a1])
                c2 = 40 + 8 * (ln_set[0] % 6)
                ln_set[0] += 1
                x2bs = []
                tx2s = []
                for i_, t_ in enumerate(ts):
                    a3 = Sc.add("dve", lambda q, i_=i_, c=c: q.tensor_scalar(
                        small[:, c + 6 + i_:c + 7 + i_], small[:, c + 4 + i_:c + 5 + i_],
                        small[:, c + 2 + i_:c + 3 + i_], None, op0=ALU.mult), deps=[a2])
                    t9 = Sc.add("dve", lambda q, i_=i_, c=c: q.tensor_scalar(
                        arf(o_ln, o_ln + 1024), ysbs[i_],
                        small[:, c + 2 + i_:c + 3 + i_],
                        small[:, c + 6 + i_:c + 7 + i_],
                        op0=ALU.mult, op1=ALU.add),
                        deps=later(a3, tok_ln_free[0]))
                    ln_guard[c] = t9
                    tok_y_free[t_ % 2] = t9
                    x2sl = t_ % 6
                    xsl = t_ % 4
                    x2b = arf(o_x2 + x2sl * 1024, o_x2 + (x2sl + 1) * 1024)
                    t_x2 = Sc.add("dve", lambda q, x2b=x2b, xsl=xsl, c2=c2, i_=i_:
                                  q.scalar_tensor_tensor(
                                      x2b, arf(o_ln, o_ln + 1024), 1.0,
                                      arf(o_x + xsl * 1024, o_x + (xsl + 1) * 1024),
                                      op0=ALU.mult, op1=ALU.add,
                                      accum_out=small[:, c2 + i_:c2 + i_ + 1]),
                                  deps=later(t9, tok_x[t_], tok_x2free.get(x2sl),
                                             ln_guard.get(c2) if i_ == 0 else None))
                    tok_ln_free[0] = t_x2
                    tok_x2[t_] = t_x2
                    tok_xfree[xsl] = t_x2
                    x2bs.append(x2b)
                    tx2s.append(t_x2)
                fqs = []
                for i_ in range(2):
                    di2 = 0
                    dump_ctr[0] += 1
                    dmp2 = arbf(o_dump + di2 * 512, o_dump + (di2 + 1) * 512)
                    f2 = Sc.add("dve", lambda q, i_=i_, c2=c2, dmp2=dmp2:
                                q.scalar_tensor_tensor(
                                    dmp2, x2bs[i_], 1.0, x2bs[i_],
                                    op0=ALU.mult, op1=ALU.mult,
                                    accum_out=small[:, c2 + 2 + i_:c2 + 3 + i_]),
                                deps=later(tx2s[i_], tok_dump_free[di2]))
                    tok_dump_free[di2] = f2
                    fqs.append(f2)
                f3 = Sc.add("dve", lambda q, c2=c2: q.tensor_scalar(
                    small[:, c2 + 4:c2 + 6], small[:, c2:c2 + 2], -1.0 / D, None,
                    op0=ALU.mult), deps=[tx2s[0], tx2s[1]])
                f4 = Sc.add("dve", lambda q, c2=c2: q.scalar_tensor_tensor(
                    small[:, c2:c2 + 2], small[:, c2 + 4:c2 + 6], 1.0,
                    small[:, c2 + 4:c2 + 6], op0=ALU.mult, op1=ALU.mult), deps=[f3])
                f5 = Sc.add("dve", lambda q, c2=c2: q.scalar_tensor_tensor(
                    small[:, c2 + 2:c2 + 4], small[:, c2 + 2:c2 + 4], 1.0 / D,
                    small[:, c2:c2 + 2], op0=ALU.mult, op1=ALU.subtract),
                    deps=[fqs[0], fqs[1], f4])
                f6 = Sc.add("act", lambda q, c2=c2: q.activation(
                    small[:, c2:c2 + 2], small[:, c2 + 2:c2 + 4], AF.Ln,
                    bias=epsB[:]), deps=[f5])
                f7 = Sc.add("act", lambda q, c2=c2: q.activation(
                    small[:, c2 + 2:c2 + 4], small[:, c2:c2 + 2], AF.Exp,
                    scale=-0.5), deps=[f6])
                for i_, t_ in enumerate(ts):
                    f8 = Sc.add("dve", lambda q, i_=i_, c2=c2: q.tensor_scalar(
                        small[:, c2 + 6 + i_:c2 + 7 + i_],
                        small[:, c2 + 4 + i_:c2 + 5 + i_],
                        small[:, c2 + 2 + i_:c2 + 3 + i_], None, op0=ALU.mult),
                        deps=[f7])
                    jj = t_ % 2
                    f9 = Sc.add("dve", lambda q, i_=i_, c2=c2, jj=jj: q.tensor_scalar(
                        arbf(o_h + jj * 512, o_h + (jj + 1) * 512), x2bs[i_],
                        small[:, c2 + 2 + i_:c2 + 3 + i_],
                        small[:, c2 + 6 + i_:c2 + 7 + i_],
                        op0=ALU.mult, op1=ALU.add),
                        deps=later(f8, tok_h_free[jj]))
                    ln_guard[c2] = f9
                    h3tok[t_] = f9
            return t_fc, ln23_pair

        def emit_ffn(ta, tb, gidx):
            pidx = ta // 2
            ttoks = []
            for jj, t in enumerate((ta, tb)):
                hdst = arbf(o_h + jj * 512, o_h + (jj + 1) * 512)
                d0 = later(h3tok[t], t_idb, reg["b6"], qkv_mm_done.get(gidx))
                ttr = []
                for dc in range(8):
                    tt = Sc.add("pe", lambda q, dc=dc, hdst=hdst: q.transpose(
                        pab[:, 2 * TRB + dc * 128:2 * TRB + (dc + 1) * 128],
                        hdst[:, dc * 128:(dc + 1) * 128], id_b[:]),
                        deps=d0 if dc == 0 else [], inc=(dc == 7))
                    ttr.append(tt)
                te = Sc.add("act", lambda q, jj=jj: q.copy(
                    htv.rearrange("p (dc n) -> p dc n", n=256)[:, :, jj * 128:(jj + 1) * 128],
                    pab[:, 2 * TRB:2 * TRB + 1024].rearrange("p (dc n) -> p dc n", n=128)),
                    deps=[ttr[-1]])
                reg["b6"] = te
                tok_h_free[jj] = te
                ttoks.append(te)
            wb1 = [arbf(o_wf1 + i * 512, o_wf1 + (i + 1) * 512) for i in range(NWF)]
            g_toks = []
            gel_ps = [reg["b01"], reg["b01"]]
            for hc in range(32):
                wi = hc % NWF
                if hc < NWF and pidx in ffn_wl:
                    t_wl = ffn_wl[pidx][hc]
                else:
                    t_wl = Sc.add("gp", lambda q, hc=hc, wi=wi: q.dma_start(
                        out=wb1[wi].rearrange("p (dc n) -> p dc n", n=128),
                        in_=weff_h[hc].rearrange("dc pp n -> pp dc n")),
                        deps=later(tok_wf1_free[wi], *t_folds), dma="gf%d" % (wi % 4))
                pp = (hc % 2) * 512
                mms = []
                for dc in range(8):
                    tm = Sc.add("pe", lambda q, dc=dc, pp=pp, wi=wi: q.matmul(
                        pa[:, pp:pp + 256], wb1[wi][:, dc * 128:(dc + 1) * 128],
                        htv[:, dc * 256:(dc + 1) * 256], start=(dc == 0), stop=(dc == 7)),
                        deps=later(t_wl, *ttoks, gel_ps[hc % 2]) if dc == 0 else [],
                        inc=(dc == 7))
                    mms.append(tm)
                tok_wf1_free[wi] = mms[-1]
                t_g = Sc.add("act", lambda q, hc=hc, pp=pp: q.activation(
                    arbf(o_g, o_g + 4096).rearrange("p (ch n) -> p ch n", n=256)[:, hc, :],
                    pa[:, pp:pp + 256], AF.Gelu, bias=bgelu[:, hc:hc + 1]),
                    deps=later(mms[-1], t_bgelu_done,
                               tok_g_free[0] if hc == 0 else None))
                gel_ps[hc % 2] = t_g
                g_toks.append(t_g)
            reg["b01"] = g_toks[-1]
            wb2 = [arbf(o_wf2 + i * 512, o_wf2 + (i + 1) * 512) for i in range(NWF)]
            zlast = [None, None]
            for hc in range(32):
                wi = hc % NWF
                if hc < NWF and pidx in ffn_wl2:
                    t_wl = ffn_wl2[pidx][hc]
                else:
                    t_wl = Sc.add("sp", lambda q, hc=hc, wi=wi: q.dma_start(
                        out=wb2[wi][:], in_=weff2_h[hc * 128:(hc + 1) * 128, :]),
                        deps=later(tok_wf2_free[wi], *t_folds), dma="gg%d" % (wi % 4))
                for tg in range(2):
                    zp = Z0 if tg == 0 else Z1
                    rg = "b23" if tg == 0 else "b45"
                    for nf2 in range(2):
                        tm = Sc.add("pe", lambda q, hc=hc, tg=tg, nf2=nf2, zp=zp, wi=wi:
                                    q.matmul(pa[:, zp + nf2 * 512:zp + (nf2 + 1) * 512],
                                             arbf(o_g, o_g + 4096).rearrange(
                                                 "p (ch n) -> p ch n", n=256)
                                             [:, hc, tg * 128:(tg + 1) * 128],
                                             wb2[wi][:, nf2 * 512:(nf2 + 1) * 512],
                                             start=(hc == 0), stop=False),
                                    deps=later(t_wl, g_toks[hc],
                                               reg[rg] if (hc == 0 and nf2 == 0) else None,
                                               tok_z_free[tg] if (hc == 0 and nf2 == 0)
                                               else None), inc=(tg == 1 and nf2 == 1))
                        zlast[tg] = tm
                tok_wf2_free[wi] = zlast[1]
            z_done = [None, None]
            for tg in range(2):
                zp = Z0 if tg == 0 else Z1
                for nf2 in range(2):
                    tl = Sc.add("pe", lambda q, zp=zp, nf2=nf2: q.matmul(
                        pa[:, zp + nf2 * 512:zp + (nf2 + 1) * 512], ones_rowb[:],
                        brow_con[0:1, nf2 * 512:(nf2 + 1) * 512], start=False, stop=True),
                        deps=later(t_brz, t_ones2), inc=(nf2 == 1))
                z_done[tg] = tl
            tok_g_free[0] = z_done[1]
            c = 40 + 8 * (ln_set[0] % 6)
            ln_set[0] += 1
            t_zes = []
            for tg, t in enumerate((ta, tb)):
                zp = Z0 if tg == 0 else Z1
                rg = "b23" if tg == 0 else "b45"
                zsb = arf(o_z + tg * 1024, o_z + (tg + 1) * 1024)
                t_ze = Sc.add("dve", lambda q, zsb=zsb, zp=zp, c=c, tg=tg: q.tensor_scalar(
                    zsb, pa[:, zp:zp + 1024], 1.0, 0.0, op0=ALU.mult, op1=ALU.add,
                    accum_out=small[:, c + tg:c + tg + 1]),
                    deps=later(z_done[tg], tok_zsb_free[tg],
                               ln_guard.get(c) if tg == 0 else None))
                tok_z_free[tg] = t_ze
                reg[rg] = t_ze
                t_zes.append(t_ze)

            def ln4_cont(ta=ta, tb=tb, c=c, t_zes=t_zes):
                zsbs = [arf(o_z + tg * 1024, o_z + (tg + 1) * 1024) for tg in range(2)]
                sqs = []
                for tg in range(2):
                    di = 0
                    dump_ctr[0] += 1
                    dmp = arbf(o_dump + di * 512, o_dump + (di + 1) * 512)
                    t2 = Sc.add("dve", lambda q, tg=tg, c=c, dmp=dmp:
                                q.scalar_tensor_tensor(
                                    dmp, zsbs[tg], 1.0, zsbs[tg],
                                    op0=ALU.mult, op1=ALU.mult,
                                    accum_out=small[:, c + 2 + tg:c + 3 + tg]),
                                deps=later(t_zes[tg], tok_dump_free[di]))
                    tok_dump_free[di] = t2
                    sqs.append(t2)
                d1 = Sc.add("dve", lambda q, c=c: q.tensor_scalar(
                    small[:, c + 4:c + 6], small[:, c:c + 2], -1.0 / D, None,
                    op0=ALU.mult), deps=[t_zes[0], t_zes[1]])
                d2 = Sc.add("dve", lambda q, c=c: q.scalar_tensor_tensor(
                    small[:, c:c + 2], small[:, c + 4:c + 6], 1.0,
                    small[:, c + 4:c + 6], op0=ALU.mult, op1=ALU.mult), deps=[d1])
                d3 = Sc.add("dve", lambda q, c=c: q.scalar_tensor_tensor(
                    small[:, c + 2:c + 4], small[:, c + 2:c + 4], 1.0 / D,
                    small[:, c:c + 2], op0=ALU.mult, op1=ALU.subtract),
                    deps=[sqs[0], sqs[1], d2])
                a1 = Sc.add("act", lambda q, c=c: q.activation(
                    small[:, c:c + 2], small[:, c + 2:c + 4], AF.Ln, bias=epsB[:]),
                    deps=[d3])
                a2 = Sc.add("act", lambda q, c=c: q.activation(
                    small[:, c + 2:c + 4], small[:, c:c + 2], AF.Exp, scale=-0.5),
                    deps=[a1])
                t9l = None
                for tg, t in enumerate((ta, tb)):
                    a3 = Sc.add("dve", lambda q, tg=tg, c=c: q.tensor_scalar(
                        small[:, c + 6 + tg:c + 7 + tg], small[:, c + 4 + tg:c + 5 + tg],
                        small[:, c + 2 + tg:c + 3 + tg], None, op0=ALU.mult), deps=[a2])
                    t9 = Sc.add("dve", lambda q, tg=tg, c=c: q.tensor_scalar(
                        arf(o_ln, o_ln + 1024), zsbs[tg],
                        small[:, c + 2 + tg:c + 3 + tg],
                        small[:, c + 6 + tg:c + 7 + tg],
                        op0=ALU.mult, op1=ALU.add),
                        deps=later(a3, tok_ln_free[0]))
                    t9l = t9
                    x2sl = t % 6
                    t_o = Sc.add("dve", lambda q, x2sl=x2sl: q.tensor_add(
                        arf(o_outb, o_outb + 1024),
                        arf(o_ln, o_ln + 1024),
                        arf(o_x2 + x2sl * 1024, o_x2 + (x2sl + 1) * 1024)),
                        deps=later(t9, tok_out_free[0]))
                    tok_ln_free[0] = t_o
                    tok_x2free[x2sl] = t_o
                    tok_zsb_free[tg] = t_o
                    t_st = Sc.add("gp", lambda q, t=t: q.dma_start(
                        out=out_d[t * T:(t + 1) * T, :],
                        in_=arf(o_outb, o_outb + 1024)),
                        deps=[t_o], dma="sout0")
                    tok_out_free[0] = t_st
                    store_toks.append(t_st)
                ln_guard[c] = t9l
            pend_ln4.append(ln4_cont)
            return z_done[1]

        if stop_after == "fold":
            Sc.add("sp", lambda q: q.dma_start(out=out_d[0:128, :],
                                               in_=arf(o_outb, o_outb + 1024)),
                   deps=t_folds, dma="sout0")
            Sc.add("sp", lambda q: q.nop(),
                   deps=[("sout0", Sc.cnt["sout0"])], inc=False)
            _emit_engine_blocks()
            return nc
        ngl = NG
        if stop_after and stop_after.startswith("g"):
            ngl = int(stop_after[1:])
        for gidx in range(ngl):
            if gidx == 0:
                emit_pre_x(0)
            emit_group_pre(gidx)
            if gidx >= 2:
                ffn_mm_done[gidx - 2] = emit_ffn(2 * (gidx - 2), 2 * (gidx - 2) + 1, gidx - 1)
            emit_group(gidx)
            for _f in pend_ln4:
                _f()
            pend_ln4.clear()
            tfc = None
            first_t = True
            for t in [tt for tt in (gidx * GT - 1, gidx * GT) if 0 <= tt < NT]:
                tfc, _c2 = emit_attn(t)
                if _c2 is not None:
                    _c2()
                if first_t and gidx + 1 < ngl:
                    emit_pre_x(gidx + 1)
                first_t = False
            attn_done[gidx] = tfc
        if ngl == NG:
            emit_vslot(NT, tok_grp_evac[NG - 1])
            tfc, _c2 = emit_attn(NT - 1)
            attn_done[NG] = tfc
            if _c2 is not None:
                _c2()
            ffn_mm_done[NG - 2] = emit_ffn(NT - 4, NT - 3, NG - 1)
            for _f in pend_ln4:
                _f()
            pend_ln4.clear()
            emit_ffn(NT - 2, NT - 1, NG - 1)
            for _f in pend_ln4:
                _f()
            pend_ln4.clear()

        Sc.add("sp", lambda q: q.nop(),
               deps=[("sout0", Sc.cnt["sout0"])], inc=False)
        if True:
            pass

        _emit_engine_blocks()

    return nc


_CACHE = {}


def kernel(**inputs):
    g = _prep(inputs)
    if "nc" not in _CACHE:
        _CACHE["nc"] = build_program()
    nc = _CACHE["nc"]
    x = np.asarray(inputs["L0_attn_x"], dtype=np.float32)
    in_maps = []
    for b in range(8):
        m = dict(g)
        m["xb"] = np.ascontiguousarray(x[b])
        in_maps.append(m)
    res = run_bass_kernel_spmd(nc, in_maps, core_ids=list(range(8)))
    out = np.stack([np.asarray(res.results[b]["out"]) for b in range(8)], axis=0)
    return out.astype(np.float32)


# version 36
# speedup vs baseline: 1.0697x; 1.0697x over previous
import numpy as np
import ml_dtypes
import concourse.bass as bass
import concourse.mybir as mybir
from concourse.bass_utils import run_bass_kernel_spmd
from concourse.alu_op_type import AluOpType as ALU
from concourse.bass_isa import ReduceOp
from concourse.library_config import attn as LIB_ATTN
from contextlib import ExitStack

F32 = mybir.dt.float32
BF16 = mybir.dt.bfloat16
AF = mybir.ActivationFunctionType
AX = mybir.AxisListType
BF = ml_dtypes.bfloat16

S, D, H, HD, DH = 4096, 1024, 16, 64, 4096
T = 128
NT = S // T
GT = 2
NG = NT // GT
RING = 768
NSLOT = 4
EPS = 1e-5


def _stage_matrix(weights, s, dim=DH):
    w = np.asarray(weights[s], dtype=np.float32).reshape(dim // 2, 2, 2)
    s2 = 2 ** s
    M = np.zeros((dim, dim), dtype=np.float32)
    k = np.arange(dim // 2)
    q = k // s2
    r = k % s2
    ia = q * 2 * s2 + r
    ib = ia + s2
    M[ia, ia] = w[k, 0, 0]
    M[ib, ia] = w[k, 0, 1]
    M[ia, ib] = w[k, 1, 0]
    M[ib, ib] = w[k, 1, 1]
    return M


def _pi_perm(dim=DH):
    j = np.arange(dim)
    return (j & 127) * 32 + (j >> 7)


def _blocks(M):
    return np.stack([M[c * T:(c + 1) * T, c * T:(c + 1) * T] for c in range(32)])


def _conj_pi(M):
    pi = _pi_perm()
    out = np.zeros_like(M)
    out[np.ix_(pi, pi)] = M
    return out


def _fold_factor_arrays(up_w, down_w):
    Mu = [_stage_matrix(up_w, s) for s in range(12)]
    Md = [_stage_matrix(down_w, s) for s in range(12)]
    upA = np.stack([_blocks(Mu[6])] + [_blocks(Mu[s].T) for s in range(5, -1, -1)], axis=1)
    Nu = [_conj_pi(Mu[s]) for s in range(7, 12)]
    upB = np.stack([_blocks(Nu[4])] + [_blocks(Nu[s].T) for s in (3, 2, 1, 0)], axis=1)
    Nd = [_conj_pi(Md[s]) for s in range(7, 12)]
    dnB = np.stack([_blocks(Nd[0].T)] + [_blocks(Nd[s - 7]) for s in (8, 9, 10, 11)], axis=1)
    dnA = np.stack([_blocks(Md[0].T)] + [_blocks(Md[s]) for s in range(1, 7)], axis=1)
    return (upA.astype(BF), upB.astype(BF),
            dnB.astype(BF), dnA.astype(BF))


def _mask_D(kind):
    r = np.arange(128)
    c = np.arange(128)
    d0 = np.abs(c[None, :] - (r[:, None] - 64)).astype(np.float32)
    d1 = np.abs(c[None, :] - (r[:, None] + 64)).astype(np.float32)
    BIGD = np.float32(1e6)
    if kind == "first":
        d0 = d0.copy(); d0[r < 64, :] = BIGD
    if kind == "last":
        d1 = d1.copy(); d1[r >= 64, :] = BIGD
    return np.concatenate([d0, d1], axis=1)


def _prep(inputs):
    g = {}
    f32 = lambda a: np.ascontiguousarray(np.asarray(a), dtype=np.float32)
    bf = lambda a: np.ascontiguousarray(np.asarray(a).astype(BF))
    pi = _pi_perm()
    inv = np.argsort(pi)

    x = np.asarray(inputs["L0_attn_x"])
    g["alpha"] = f32(inputs["L0_attn_alpha"]).reshape(1, 1)
    g["tail_last"] = f32(x[:, S - 1, :])
    g["tail_prev"] = f32(x[:, S - 2, :])
    for nm, w in (("wq", "L0_attn_Wq"), ("wk", "L0_attn_Wk"),
                  ("wv", "L0_attn_Wv")):
        g[nm] = bf(np.asarray(inputs[w]).reshape(8, 128, 8, 128)
                   .transpose(1, 2, 0, 3).reshape(128, 8 * 1024))
    g["wfc"] = bf(inputs["L0_attn_Wfc"])
    g["wo"] = bf(inputs["L0_attn_W_o"])
    for nm, b in (("bq", "L0_attn_bq"), ("bk", "L0_attn_bk"), ("bv", "L0_attn_bv")):
        g[nm] = f32(np.asarray(inputs[b]).reshape(8, 128).T)
    g["bfc_row"] = bf(np.asarray(inputs["L0_attn_bfc"]).reshape(1, D))
    g["bcon_row"] = bf(np.asarray(inputs["L0_ffn_b_con"]).reshape(1, D))
    g["dbias_row"] = bf(np.asarray(inputs["L0_ffn_down_bias"]).reshape(1, D))
    g["wexpT_aug"] = bf(np.concatenate(
        [np.asarray(inputs["L0_ffn_W_exp"], dtype=np.float32).T,
         np.asarray(inputs["L0_ffn_b_exp"], dtype=np.float32).reshape(DH, 1)], axis=1))
    g["wcon_pi"] = bf(np.asarray(inputs["L0_ffn_W_con"])[inv, :])
    g["up_bias_pi"] = f32(np.asarray(inputs["L0_ffn_up_bias"])[inv].reshape(32, 128).T)
    upA, upB, dnB, dnA = _fold_factor_arrays(inputs["L0_ffn_up_weights"],
                                             inputs["L0_ffn_down_weights"])
    g["upA"], g["upB"], g["dnB"], g["dnA"] = upA, upB, dnB, dnA
    g["id_f32"] = np.eye(128, dtype=np.float32)
    g["id_bf"] = np.eye(128).astype(BF)
    g["dmask"] = np.stack([_mask_D("mid"), _mask_D("first"), _mask_D("last")])
    return g



DMA_SEMS = ("scst", "sfold0", "sfold1", "sfc0", "sfc1", "sfc2", "sfc3",
            "sff0", "sff1", "sff2", "sff3",
            "sx0", "sx1", "sx2", "sx3", "sout0", "sout1",
            "gw0", "gw1", "gw2", "gw3", "gf0", "gf1", "gf2", "gf3",
            "gg0", "gg1", "gg2", "gg3",
            "gfc0", "gfc1", "gfc2", "gfc3")


class Sched:
    ENG = ("pe", "act", "dve", "gp", "sp")

    def __init__(self):
        self.ops = {e: [] for e in self.ENG}
        self.cnt = {s: 0 for s in ("pe", "act", "dve", "pool") + DMA_SEMS}
        self.wm = {e: {s: 0 for s in self.cnt} for e in self.ENG}
        self.sems = {}
        self.final = {}

    def dep(self, eng, tok):
        if tok is None:
            return
        sem, v = tok
        fv = self.final.get(sem)
        if fv is not None and v < fv:
            v = fv
        if self.wm[eng][sem] >= v:
            return
        self.ops[eng].append(("w", sem, v))
        self.wm[eng][sem] = v

    def add(self, eng, fn, deps=(), inc=True, dma=False, base=0):
        for t in deps:
            self.dep(eng, t)
        if eng == "pe":
            if base != getattr(self, "_pe_base", 0):
                if not self.ops["pe"] or self.ops["pe"][-1][0] != "w":
                    self.ops["pe"].append(("o", lambda q: q.nop(), None, 0))
            self._pe_base = base
        if dma:
            sem = dma
            assert sem in DMA_SEMS, sem
            self.cnt[sem] += 16
            self.ops[eng].append(("o", fn, sem, 16))
            return (sem, self.cnt[sem])
        sem = "pool" if eng in ("gp", "sp") else eng
        if eng == "sp":
            inc = False
        if inc:
            self.cnt[sem] += 1
            self.ops[eng].append(("o", fn, sem, 1))
        else:
            self.ops[eng].append(("o", fn, None, 0))
        return (sem, self.cnt[sem])

    def replay(self, eng, q):
        for rec in self.ops[eng]:
            if rec[0] == "w":
                q.wait_ge(self.sems[rec[1]], rec[2])
            else:
                _, fn, sem, n = rec
                inst = fn(q)
                if sem is not None:
                    inst.then_inc(self.sems[sem], n)


def later(*toks):
    best = {}
    for t in toks:
        if t is None:
            continue
        s, v = t
        if best.get(s, -1) < v:
            best[s] = v
    return [(s, v) for s, v in best.items()]



def build_program(stop_after=None, attn_stop=None, grp_stop=None, vs_stop=None):
    nc = bass.Bass("TRN2", target_bir_lowering=False, debug=False)

    def din(name, shape, dt):
        return nc.dram_tensor(name, list(shape), dt, kind="ExternalInput").ap()

    xb = din("xb", (S, D), F32)
    tail_last = din("tail_last", (8, D), F32)
    tail_prev = din("tail_prev", (8, D), F32)
    alpha = din("alpha", (1, 1), F32)
    wq_d = din("wq", (128, 8 * D), BF16)
    wk_d = din("wk", (128, 8 * D), BF16)
    wv_d = din("wv", (128, 8 * D), BF16)
    wfc_d = din("wfc", (D, D), BF16)
    wo_d = din("wo", (H, HD, HD), BF16)
    bq_d = din("bq", (128, 8), F32)
    bk_d = din("bk", (128, 8), F32)
    bv_d = din("bv", (128, 8), F32)
    bfc_row_d = din("bfc_row", (1, D), BF16)
    bcon_row_d = din("bcon_row", (1, D), BF16)
    dbias_row_d = din("dbias_row", (1, D), BF16)
    wexpT_d = din("wexpT_aug", (DH, D + 1), BF16)
    wcon_d = din("wcon_pi", (DH, D), BF16)
    upbias_d = din("up_bias_pi", (128, 32), F32)
    upA_d = din("upA", (32, 7, 128, 128), BF16)
    upB_d = din("upB", (32, 5, 128, 128), BF16)
    dnB_d = din("dnB", (32, 5, 128, 128), BF16)
    dnA_d = din("dnA", (32, 7, 128, 128), BF16)
    idf_d = din("id_f32", (128, 128), F32)
    idb_d = din("id_bf", (128, 128), BF16)
    dmask_d = din("dmask", (3, 128, 256), F32)

    out_d = nc.dram_tensor("out", [S, D], F32, kind="ExternalOutput").ap()
    rt1 = nc.dram_tensor("rt1", [DH, D + 1], BF16, kind="Internal").ap()
    rt2 = nc.dram_tensor("rt2", [DH, D], BF16, kind="Internal").ap()
    weff_h = nc.dram_tensor("weff_h", [32, 8, 128, 128], BF16, kind="Internal").ap()
    weff2_h = nc.dram_tensor("weff2_h", [DH, D], BF16, kind="Internal").ap()

    Sc = Sched()

    with ExitStack() as st:
        sbuf = lambda n, sh, dt: st.enter_context(nc.sbuf_tensor(n, list(sh), dt))
        id_f = sbuf("id_f", (128, 128), F32)
        id_b = sbuf("id_b", (128, 128), BF16)
        ones_row = sbuf("ones_row", (1, 128), F32)
        ones_rowb = sbuf("ones_rowb", (1, 128), BF16)
        ones_colb = sbuf("ones_colb", (128, 1), BF16)
        zrow_b = sbuf("zrow_b", (1, 128), BF16)
        bq8 = sbuf("bq8", (128, 8), F32)
        bk_s = sbuf("bk_s", (128, 8), F32)
        bv_s = sbuf("bv_s", (128, 8), F32)
        brow_fc = sbuf("brow_fc", (1, D), BF16)
        brow_con = sbuf("brow_con", (1, D), BF16)
        brow_db = sbuf("brow_db", (1, D), BF16)
        wo_s = sbuf("wo_s", (128, 512), BF16)
        bgelu = sbuf("bgelu", (128, 32), F32)
        dm_s = sbuf("dm_s", (128, 3 * 256), F32)
        badd = sbuf("badd", (128, 3 * 512), BF16)
        vB = sbuf("vB", (128, 1), F32)
        epsB = sbuf("epsB", (128, 1), F32)
        small = sbuf("small", (128, 112), F32)
        rtmp = sbuf("rtmp", (128, 32), F32)

        AR_N = 44544
        ar = sbuf("arena", (128, AR_N), F32)
        A = ar.ap()
        Ab = A.bitcast(BF16)
        _off = [0]

        def carve(n):
            o = _off[0]
            _off[0] += n
            assert _off[0] <= AR_N, f"arena overflow {_off[0]}"
            return o

        o_ktr = carve(3072)
        o_vtr = carve(3072)
        o_qt = carve(2048)
        o_vp = carve(NSLOT * 520)
        o_x = carve(4 * 1024)
        o_x2 = carve(6 * 1024)
        o_h = carve(2 * 512)
        o_h1 = carve(2 * 512)
        o_ht = carve(1024)
        o_g = carve(4096)
        o_pt = carve(2 * 512)
        o_ot = carve(512)
        o_osb = carve(512)
        o_y = carve(2 * 1024)
        o_qkvw = carve(4 * 512)
        o_wf1 = carve(6 * 512)
        o_wf2 = carve(4 * 512)
        o_dump = carve(1024)
        o_z = carve(2 * 1024)
        o_ln = carve(1024)
        o_outb = carve(1024)
        fa_cb = [o_g, o_g + 1025, o_g + 2050, o_z]
        fa_fb = [o_z + 1025, o_wf1, o_wf1 + 896, o_outb]
        fa_t = [o_wf2, o_wf2 + 1024]
        fa_t2 = [o_wf2 + 1024, o_wf2 + 2048]
        fa_p = [o_wf1 + 1792, o_wf1 + 2048]

        def arf(c0, c1):
            return A[:, c0:c1]

        def arbf(c0, c1):
            return Ab[:, 2 * c0:2 * c1]

        tl_s = arf(o_x2 + 0, o_x2 + 1024)
        tp_s = arf(o_x2 + 1024, o_x2 + 2048)
        tsq = arf(o_x2 + 2048, o_x2 + 3072)
        lnl = arf(o_x2 + 3072, o_x2 + 4096)
        lnp = arf(o_x2 + 4096, o_x2 + 5120)

        pa = st.enter_context(nc.psum_tensor("pa", [128, 4096], F32))
        pab = pa.ap().bitcast(BF16)
        ST0, ST1 = 0, 1024
        VPP = 1024
        OP = 2048
        Z0, Z1 = 1024, 2048
        TRB = 3072
        RP = 3584
        NMP = 3200
        NMB = 3456
        FLP = 3800

        for s in Sc.cnt:
            Sc.sems[s] = st.enter_context(nc.semaphore(s))
        blk = st.enter_context(nc.Block())

        def _emit_engine_blocks():
            @blk.tensor
            def _(q):
                Sc.replay("pe", q)

            @blk.scalar
            def _(q):
                Sc.replay("act", q)

            @blk.vector
            def _(q):
                Sc.replay("dve", q)

            @blk.gpsimd
            def _(q):
                Sc.replay("gp", q)

            @blk.sync
            def _(q):
                Sc.replay("sp", q)

        reg = {"b01": None, "b23": None, "b45": None, "b6": None, "b7": None}

        t_idf = Sc.add("sp", lambda q: q.dma_start(out=id_f[:], in_=idf_d), dma="scst")
        t_idb = Sc.add("sp", lambda q: q.dma_start(out=id_b[:], in_=idb_d), dma="scst")
        t_bq = Sc.add("sp", lambda q: q.dma_start(out=bq8[:], in_=bq_d), dma="scst")
        t_bk = Sc.add("sp", lambda q: q.dma_start(out=bk_s[:], in_=bk_d), dma="scst")
        t_bv = Sc.add("sp", lambda q: q.dma_start(out=bv_s[:], in_=bv_d), dma="scst")
        t_br1 = Sc.add("sp", lambda q: q.dma_start(out=brow_fc[:], in_=bfc_row_d), dma="scst")
        t_br2 = Sc.add("sp", lambda q: q.dma_start(out=brow_con[:], in_=bcon_row_d), dma="scst")
        t_br3 = Sc.add("sp", lambda q: q.dma_start(out=brow_db[:], in_=dbias_row_d), dma="scst")
        t_brz = Sc.add("dve", lambda q: q.tensor_add(brow_con[:], brow_con[:],
                                                     brow_db[:]), deps=[t_br2, t_br3])
        t_wo = Sc.add("sp", lambda q: q.dma_start(
            out=wo_s[:].rearrange("p (hh n) -> p hh n", n=64),
            in_=wo_d.rearrange("(hh two) hd n -> (two hd) hh n", two=2)), dma="scst")
        t_dm = Sc.add("sp", lambda q: q.dma_start(
            out=dm_s[:].rearrange("p (m n) -> p m n", n=256),
            in_=dmask_d.rearrange("m p n -> p m n")), dma="scst")
        t_tl = Sc.add("sp", lambda q: q.dma_start(out=tl_s[0:8, :], in_=tail_last), dma="scst")
        t_tp = Sc.add("sp", lambda q: q.dma_start(out=tp_s[0:8, :], in_=tail_prev), dma="scst")
        t_al = Sc.add("sp", lambda q: q.dma_start(out=small[0:1, 0:1], in_=alpha), dma="scst")
        t_ub = Sc.add("sp", lambda q: q.dma_start(out=bgelu[:], in_=upbias_d), dma="scst")
        Sc.final["scst"] = Sc.cnt["scst"]

        t_ones = Sc.add("dve", lambda q: q.memset(ones_row[:], 1.0))
        t_ones2 = Sc.add("dve", lambda q: q.memset(ones_rowb[:], 1.0))
        t_ones3 = Sc.add("dve", lambda q: q.memset(ones_colb[:], 1.0))
        t_zrow = Sc.add("dve", lambda q: q.memset(zrow_b[:], 0.0))
        t_eps = Sc.add("dve", lambda q: q.memset(epsB[:], EPS))
        t_bq8 = Sc.add("dve", lambda q: q.tensor_scalar_mul(bq8[:], bq8[:], 0.125),
                       deps=[t_bq])
        t_oc8 = Sc.add("dve", lambda q: q.memset(small[0:8, 1:2], 1.0))

        def ln_rows(src, dst, tok, cbase):
            c = cbase
            t1 = Sc.add("dve", lambda q: q.reduce_sum(srowap(c), src[0:8, :], axis=AX.X),
                        deps=[tok])
            t3 = Sc.add("act", lambda q: q.activation(tsq[0:8, :], src[0:8, :], AF.Square,
                                                      accum_out=srowap(c + 1)), deps=[tok])
            t4 = Sc.add("dve", lambda q: q.tensor_scalar(srowap(c + 2), srowap(c),
                                                         1.0 / D, None, op0=ALU.mult),
                        deps=[t1])
            t5 = Sc.add("dve", lambda q: q.tensor_tensor(srowap(c + 3), srowap(c + 2),
                                                         srowap(c + 2), op=ALU.mult), deps=[t4])
            t6 = Sc.add("dve", lambda q: q.tensor_scalar(srowap(c + 4), srowap(c + 1),
                                                         1.0 / D, None, op0=ALU.mult), deps=[t3])
            t7 = Sc.add("dve", lambda q: q.tensor_tensor(srowap(c + 5), srowap(c + 4),
                                                         srowap(c + 3), op=ALU.subtract),
                        deps=[t5, t6])
            t8 = Sc.add("act", lambda q: q.activation(srowap(c + 6), srowap(c + 5), AF.Ln,
                                                      bias=epsB[0:8, :]), deps=[t7, t_eps])
            t9 = Sc.add("act", lambda q: q.activation(srowap(c + 7), srowap(c + 6), AF.Exp,
                                                      scale=-0.5), deps=[t8])
            ta = Sc.add("dve", lambda q: q.tensor_scalar(srowap(c + 8), srowap(c + 2),
                                                         srowap(c + 7), -1.0, op0=ALU.mult,
                                                         op1=ALU.mult), deps=[t9])
            tb = Sc.add("act", lambda q: q.activation(dst[0:8, :], src[0:8, :], AF.Identity,
                                                      bias=srowap(c + 8),
                                                      scale=srowap(c + 7)), deps=[ta])
            return tb

        def srowap(c):
            return small[0:8, c:c + 1]

        tk1 = ln_rows(tl_s, lnl, t_tl, 2)
        tk2 = ln_rows(tp_s, lnp, t_tp, 11)
        t_fl = Sc.add("dve", lambda q: q.tensor_sub(tsq[0:8, :], lnl[0:8, :], lnp[0:8, :]),
                      deps=[tk1, tk2])
        t_fa = Sc.add("act", lambda q: q.activation(lnl[0:8, :], tsq[0:8, :], AF.Abs,
                                                    accum_out=small[0:8, 20:21]), deps=[t_fl])
        t_fs = Sc.add("pe", lambda q: q.matmul(pa[0:1, FLP:FLP + 1], small[0:8, 20:21],
                                               small[0:8, 1:2], start=True, stop=True),
                      deps=[t_fa, t_oc8])
        reg["b7"] = t_fs
        t_fm = Sc.add("dve", lambda q: q.tensor_scalar(small[0:1, 21:22], pa[0:1, FLP:FLP + 1],
                                                       1.0 / (8 * D), None, op0=ALU.mult),
                      deps=[t_fs])
        reg["b7"] = t_fm
        t_a = Sc.add("act", lambda q: q.activation(small[0:1, 22:23], small[0:1, 0:1],
                                                   AF.Sigmoid), deps=[t_al])
        t_s1 = Sc.add("dve", lambda q: q.tensor_sub(small[0:1, 23:24], small[0:1, 21:22],
                                                    lnl[0:1, 0:1]), deps=[t_fm, t_fa])
        t_s2 = Sc.add("dve", lambda q: q.tensor_tensor(small[0:1, 24:25], small[0:1, 23:24],
                                                       small[0:1, 22:23], op=ALU.mult),
                      deps=[t_s1, t_a])
        t_s3 = Sc.add("dve", lambda q: q.tensor_add(small[0:1, 25:26], small[0:1, 24:25],
                                                    lnl[0:1, 0:1]), deps=[t_s2])
        t_s4 = Sc.add("act", lambda q: q.activation(small[0:1, 26:27], small[0:1, 25:26],
                                                    AF.Sigmoid), deps=[t_s3])
        t_s5 = Sc.add("dve", lambda q: q.tensor_scalar(small[0:1, 27:28], small[0:1, 26:27],
                                                       96.0, 24.0, op0=ALU.mult, op1=ALU.add),
                      deps=[t_s4])
        t_vb1 = Sc.add("pe", lambda q: q.matmul(pa[:, NMB:NMB + 1], ones_row[:],
                                                small[0:1, 27:28], start=True, stop=True),
                       deps=[t_s5, t_ones, reg["b7"]])
        t_vb = Sc.add("dve", lambda q: q.tensor_copy(vB[:], pa[:, NMB:NMB + 1]), deps=[t_vb1])
        reg["b7"] = t_vb
        t_b1 = Sc.add("dve", lambda q: q.tensor_scalar(dm_s[:], dm_s[:], 2.0, vB[:],
                                                       op0=ALU.mult, op1=ALU.subtract),
                      deps=[t_dm, t_vb])
        t_bg = Sc.add("dve", lambda q: q.tensor_single_scalar(dm_s[:], dm_s[:], 0.0,
                                                              op=ALU.is_gt), deps=[t_b1])
        t_b01 = None
        for rep in (0, 1):
            t_b01 = Sc.add("dve", lambda q, rep=rep: q.tensor_scalar(
                badd.ap().rearrange("p (k n) -> p k n", n=512)
                [:, :, rep * 256:(rep + 1) * 256],
                dm_s[:].rearrange("p (k n) -> p k n", n=256),
                -1e30, None, op0=ALU.mult), deps=[t_bg])
        phase0_sbuf_done = t_b01

        PF0, PF1 = 0, 512
        PCH = 1024

        def fold_products(fbuf, nf, t_fac, extra):
            f0 = fbuf[0]
            fac = lambda i: Ab[:, 2 * f0 + i * 128:2 * f0 + (i + 1) * 128]
            cur = fac(0)
            tok = t_fac
            dst0b = Ab[:, 2 * fa_p[0]:2 * fa_p[0] + 128]
            dst1b = Ab[:, 2 * fa_p[0] + 128:2 * fa_p[0] + 256]
            for i in range(1, nf):
                pf = PF0 if (i % 2) else PF1
                tm = Sc.add("pe", lambda q, i=i, cur=cur, pf=pf, fac=fac:
                            q.matmul(pa[:, pf:pf + 128], fac(i), cur,
                                     start=True, stop=True),
                            deps=later(tok, reg["b01"], *extra))
                dst = dst0b if (i % 2) else dst1b
                te = Sc.add("dve", lambda q, dst=dst, pf=pf:
                            q.tensor_copy(dst, pa[:, pf:pf + 128]), deps=[tm])
                reg["b01"] = te
                cur = dst
                tok = te
            return cur, tok

        def fold_apply(lhsT, w, rhs_ap, deps):
            toks = []
            c0 = 0
            dd = list(deps)
            while c0 < w:
                cw = min(512, w - c0)
                tm = Sc.add("pe", lambda q, c0=c0, cw=cw, lhsT=lhsT, rhs_ap=rhs_ap:
                            q.matmul(pa[:, PCH + c0:PCH + c0 + cw], lhsT,
                                     rhs_ap[:, c0:c0 + cw], start=True, stop=True),
                            deps=dd)
                dd = []
                toks.append(tm)
                c0 += cw
            return toks

        rt1v = rt1.rearrange("(p cc) n -> p cc n", cc=32)
        rt2v = rt2.rearrange("(l mm) n -> mm l n", mm=128)
        w2v = weff2_h.rearrange("(p cc) n -> p cc n", cc=32)
        prev_w = None
        prev_ev = None
        prevw2 = [None, None]
        buf_free = [None] * 4
        fac_free = [None] * 4

        def fold_pass(load_src, fac_src, nfac, width, out_fn, pass_id):
            nonlocal prev_w, prev_ev
            pass_start = later(prevw2[0], prevw2[1])
            pass_start = pass_start if pass_start else [None]
            pass_start0 = list(pass_start)
            for c in range(32):
                parity = c % 4
                cb = [fa_cb[parity]]
                fb = [fa_fb[parity]]
                t_l = Sc.add("sp", lambda q, c=c, cb=cb: q.dma_start(
                    out=Ab[:, 2 * cb[0]:2 * cb[0] + width],
                    in_=load_src(c)), deps=later(buf_free[parity], *pass_start0),
                    dma="sfc%d" % parity)
                t_f = Sc.add("gp", lambda q, c=c, fb=fb, nfac=nfac: q.dma_start(
                    out=Ab[:, 2 * fb[0]:2 * fb[0] + nfac * 128]
                    .rearrange("p (f n) -> p f n", n=128),
                    in_=fac_src(c)), deps=later(fac_free[parity]), dma="sff%d" % parity)
                lhsT, tokp = fold_products(fb, nfac, t_f, [])
                fac_free[parity] = tokp
                toks = fold_apply(lhsT, width, Ab[:, 2 * cb[0]:2 * cb[0] + width],
                                  later(tokp, t_l, reg["b23"], reg["b45"]))
                buf_free[parity] = toks[-1]
                prev_w, prev_ev = out_fn(c, toks)
                reg["b23"] = prev_ev
                reg["b45"] = prev_ev

        def outA(c, toks):
            ft = fa_t if c % 2 == 0 else fa_t2
            e1 = Ab[:, 2 * ft[0]:2 * ft[0] + 1025]
            t1 = Sc.add("dve", lambda q, e1=e1: q.tensor_copy(e1, pa[:, PCH:PCH + 1025]),
                        deps=toks + later(prevw2[c % 2]))
            w1 = Sc.add("sp", lambda q, c=c, e1=e1: q.dma_start(
                out=rt1v[:, c, :], in_=e1), deps=[t1], dma="sfold%d" % (c % 2))
            prevw2[c % 2] = w1
            return w1, t1

        fold_pass(lambda c: wexpT_d[c * 128:(c + 1) * 128, :],
                  lambda c: upA_d[c].rearrange("f pp n -> pp f n"), 7, 1025, outA, 0)

        t_rt1_done = prev_w

        def outB(c, toks):
            ft = fa_t if c % 2 == 0 else fa_t2
            tb = Sc.add("dve", lambda q, c=c: q.tensor_add(
                bgelu[:, c:c + 1], pa[:, PCH + 1024:PCH + 1025], bgelu[:, c:c + 1]),
                deps=toks + [t_ub])
            evb = arbf(ft[0], ft[0] + 512)
            e1 = Sc.add("act", lambda q, evb=evb: q.copy(evb, pa[:, PCH:PCH + 1024]),
                        deps=toks + later(prevw2[c % 2]))
            ttr = []
            dep0 = later(e1, t_idb, reg["b6"])
            for dc in range(8):
                tt = Sc.add("pe", lambda q, dc=dc, evb=evb: q.transpose(
                    pab[:, 2 * TRB + dc * 128:2 * TRB + (dc + 1) * 128],
                    evb[:, dc * 128:(dc + 1) * 128], id_b[:]),
                    deps=dep0 if dc == 0 else [], inc=(dc == 7))
                ttr.append(tt)
            e2 = Sc.add("act", lambda q, ft=ft: q.copy(arbf(ft[0] + 512, ft[0] + 1024),
                                                       pab[:, 2 * TRB:2 * TRB + 1024]),
                        deps=[ttr[-1]])
            reg["b6"] = e2
            w1 = Sc.add("sp", lambda q, c=c, ft=ft: q.dma_start(
                out=weff_h[c].rearrange("dc pp n -> pp dc n"),
                in_=arbf(ft[0] + 512, ft[0] + 1024).rearrange(
                    "p (dc n) -> p dc n", n=128)), deps=[e2], dma="sfold%d" % (c % 2))
            prevw2[c % 2] = w1
            return w1, tb

        fold_pass(lambda c: rt1[c * 128:(c + 1) * 128, :],
                  lambda c: upB_d[c].rearrange("f pp n -> pp f n"), 5, 1025, outB, 1)
        t_bgelu_done = prev_ev

        def outDB(c, toks):
            ft = fa_t if c % 2 == 0 else fa_t2
            e1 = Ab[:, 2 * ft[0]:2 * ft[0] + 1024]
            t1 = Sc.add("dve", lambda q, e1=e1: q.tensor_copy(e1, pa[:, PCH:PCH + 1024]),
                        deps=toks + later(prevw2[c % 2]))
            w1 = t1
            for i in range(4):
                w1 = Sc.add("sp", lambda q, c=c, e1=e1, i=i: q.dma_start(
                    out=rt2v[4 * c + i], in_=e1[32 * i:32 * (i + 1), :]),
                    deps=[t1], dma="sfold%d" % (c % 2))
            prevw2[c % 2] = w1
            return w1, t1

        fold_pass(lambda c: wcon_d[c * 128:(c + 1) * 128, :],
                  lambda c: dnB_d[c].rearrange("f pp n -> pp f n"), 5, 1024, outDB, 2)

        def outDA(c, toks):
            ft = fa_t if c % 2 == 0 else fa_t2
            evb = arbf(ft[0], ft[0] + 512)
            e1 = Sc.add("act", lambda q, evb=evb: q.copy(evb, pa[:, PCH:PCH + 1024]),
                        deps=toks + later(prevw2[c % 2]))
            w1 = Sc.add("sp", lambda q, c=c, evb=evb: q.dma_start(
                out=w2v[:, c, :], in_=evb), deps=[e1], dma="sfold%d" % (c % 2))
            prevw2[c % 2] = w1
            return w1, e1

        fold_pass(lambda c: rt2[c * 128:(c + 1) * 128, :],
                  lambda c: dnA_d[c].rearrange("f pp n -> pp f n"), 7, 1024, outDA, 3)
        t_folds = [prevw2[0], prevw2[1]]

        tok_x = {}
        tok_x2 = {}
        tok_xfree = {}
        tok_x2free = {}
        tok_vp = {}
        tok_grp_evac = {}
        attn_done = {}
        qkv_mm_done = {}
        ffn_mm_done = {}
        NWQ = 4
        NWF = 6
        tok_qt_free = [None, None]
        tok_wq_free = [None] * NWQ
        tok_wf1_free = [None] * NWF
        tok_wf2_free = [None] * NWQ
        tok_vslot_free = {}
        st_free = [None, None]
        pt_free = [None, None]
        negm_col_free = [None] * 4
        cm_col_free = [None] * 4
        ar_col_free = [None] * 4
        rtmp_free = [None]
        tok_dump_free = [None, None]
        dump_ctr = [0]
        tok_h_free = [None, None]
        tok_h1_free = [None, None]
        h3tok = {}
        grp_h1 = {}
        grp_wl = {}
        ffn_wl = {}
        ffn_wl2 = {}
        tok_ht_writer = [None]
        tok_ht_readers = [None]
        tok_osb_free = [None]
        tok_ot_free = [None]
        tok_y_free = [None, None]
        tok_ln_free = [None]
        tok_z_free = [None, None]
        tok_zsb_free = [None, None]
        tok_out_free = [None, None]
        tok_g_free = [t_folds[0] if t_folds[0] else None]
        for _tf in t_folds:
            tok_g_free[0] = _tf if _tf else tok_g_free[0]
        for _sl5 in range(5):
            tok_x2free[_sl5] = phase0_sbuf_done
        tok_out_free[0] = t_folds[1] if len(t_folds) > 1 and t_folds[1] else t_folds[0]
        tok_zsb_free[0] = tok_zsb_free[1] = tok_g_free[0]
        gel_free = [None, None]
        store_toks = []
        ln_set = [0]
        pend_ln2 = []
        pend_ln4 = []
        pair_acc = {"c": None, "ts": [], "yes": []}

        def emit_ln(src_ap, dst_ap, deps_src):
            c = 40 + 8 * (ln_set[0] % 6)
            ln_set[0] += 1
            di = dump_ctr[0] % 2
            dump_ctr[0] += 1
            dmp = arbf(o_dump + di * 512, o_dump + (di + 1) * 512)
            sets_prev = ln_guard.get(c)
            t1 = Sc.add("dve", lambda q: q.tensor_scalar(dmp, src_ap, 1.0, 0.0,
                                                         op0=ALU.mult, op1=ALU.add,
                                                         accum_out=small[:, c:c + 1]),
                        deps=later(*deps_src, tok_dump_free[di], sets_prev))
            t2 = Sc.add("dve", lambda q: q.scalar_tensor_tensor(
                dmp, src_ap, 1.0, src_ap, op0=ALU.mult, op1=ALU.mult,
                accum_out=small[:, c + 1:c + 2]), deps=[t1])
            tok_dump_free[di] = t2
            t3 = Sc.add("dve", lambda q: q.tensor_scalar(small[:, c + 2:c + 3],
                                                         small[:, c:c + 1], 1.0 / D, None,
                                                         op0=ALU.mult), deps=[t1])
            t4 = Sc.add("dve", lambda q: q.tensor_tensor(small[:, c + 3:c + 4],
                                                         small[:, c + 2:c + 3],
                                                         small[:, c + 2:c + 3],
                                                         op=ALU.mult), deps=[t3])
            t5 = Sc.add("dve", lambda q: q.tensor_scalar(small[:, c + 4:c + 5],
                                                         small[:, c + 1:c + 2], 1.0 / D,
                                                         small[:, c + 3:c + 4],
                                                         op0=ALU.mult, op1=ALU.subtract),
                        deps=[t2, t4])
            t6 = Sc.add("act", lambda q: q.activation(small[:, c + 5:c + 6],
                                                      small[:, c + 4:c + 5], AF.Ln,
                                                      bias=epsB[:]), deps=[t5])
            t7 = Sc.add("act", lambda q: q.activation(small[:, c + 6:c + 7],
                                                      small[:, c + 5:c + 6], AF.Exp,
                                                      scale=-0.5), deps=[t6])
            t8 = Sc.add("dve", lambda q: q.tensor_scalar(small[:, c + 7:c + 8],
                                                         small[:, c + 2:c + 3],
                                                         small[:, c + 6:c + 7], -1.0,
                                                         op0=ALU.mult, op1=ALU.mult),
                        deps=[t7])
            t9 = Sc.add("act", lambda q: q.activation(dst_ap, src_ap, AF.Identity,
                                                      bias=small[:, c + 7:c + 8],
                                                      scale=small[:, c + 6:c + 7]),
                        deps=[t8])
            ln_guard[c] = t9
            return t9

        ln_guard = {}

        ktv = arbf(o_ktr, o_ktr + 3072).rearrange("p (ch n) -> p ch n", n=RING)
        vtv = arbf(o_vtr, o_vtr + 3072).rearrange("p (ch n) -> p ch n", n=RING)
        htv = arbf(o_ht, o_ht + 1024)

        def emit_vslot(w, gdep):
            sl = w % NSLOT
            vp0 = o_vp + sl * 520
            if w == 0:
                segs = [(64, 0, 64)]
            elif w == NT:
                segs = [(0, w * T - 64, 64)]
            else:
                segs = [(0, w * T - 64, 128)]
            vjobs = []
            for h in list(range(0, H, 2)) + list(range(1, H, 2)):
                for (pb, ts0, cnt) in segs:
                    p0 = ts0 % RING
                    take = min(cnt, RING - p0)
                    parts = ((pb, take, p0),) if take == cnt else \
                        ((pb, take, p0), (pb + take, cnt - take, 0))
                    for (mb, cnt2, rp) in parts:
                        vjobs.append((h, mb, cnt2, rp))
            vbanks = [(h % 2, mb) for (h, mb, _, _) in vjobs]
            last = None
            for ji, (h, mb, cnt2, rp) in enumerate(vjobs):
                b = vbanks[ji]
                vs = (vbanks.index(b) == ji,
                      (len(vbanks) - 1 - vbanks[::-1].index(b)) == ji)
                dd = later(gdep, t_wo, reg["b23"],
                           tok_vslot_free.get(sl)) if ji == 0 else []
                vc = (h % 2) * 512 + (h // 2) * 64
                last = Sc.add("pe", lambda q, h=h, mb=mb, cnt2=cnt2, rp=rp, vs=vs, vc=vc:
                              q.matmul(pa[mb:mb + cnt2,
                                          VPP + vc:VPP + vc + 64],
                                       vtv[(h % 2) * 64:(h % 2) * 64 + 64, h // 2,
                                           rp:rp + cnt2],
                                       wo_s[(h % 2) * 64:(h % 2) * 64 + 64,
                                            (h // 2) * 64:(h // 2) * 64 + 64],
                                       start=vs[0], stop=vs[1]),
                              deps=dd, inc=(ji == len(vjobs) - 1),
                              base=(h % 2) * 64)
            if vs_stop == "mm":
                tok_vp[w] = last
                return
            if w == 0:
                evrows = (64, 128)
            elif w == NT:
                evrows = (0, 64)
            else:
                evrows = (0, 128)
            t_ev = Sc.add("act", lambda q, vp0=vp0, evrows=evrows: q.copy(
                arbf(vp0, vp0 + 512)[evrows[0]:evrows[1], :],
                pa[evrows[0]:evrows[1], VPP:VPP + 1024]), deps=[last])
            reg["b23"] = t_ev
            if w == 0 or w == NT:
                z0, z1 = (0, 64) if w == 0 else (64, 128)
                t_ev = Sc.add("dve", lambda q, vp0=vp0, z0=z0, z1=z1: q.memset(
                    arbf(vp0, vp0 + 512)[z0:z1, :], 0.0), deps=[t_ev])
            if vs_stop == "ev":
                tok_vp[w] = t_ev
                return
            tok_vp[w] = t_ev

        def emit_pre_x(gidx):
            g0 = gidx * GT
            c = 40 + 8 * (ln_set[0] % 6)
            ln_set[0] += 1
            tx = []
            srcs = []
            for ii in range(GT):
                ti = g0 + ii
                sl = ti % 4
                t_x = Sc.add("sp", lambda q, ti=ti, sl=sl: q.dma_start(
                    out=arf(o_x + sl * 1024, o_x + (sl + 1) * 1024),
                    in_=xb[ti * T:(ti + 1) * T, :]),
                    deps=later(tok_xfree.get(sl)), dma="sx%d" % sl)
                tok_x[ti] = t_x
                tx.append(t_x)
                srcs.append(arf(o_x + sl * 1024, o_x + (sl + 1) * 1024))
            sets_prev = ln_guard.get(c)
            tss = []
            for ii in range(GT):
                di = dump_ctr[0] % 2
                dump_ctr[0] += 1
                dmp = arbf(o_dump + di * 512, o_dump + (di + 1) * 512)
                t1 = Sc.add("dve", lambda q, ii=ii, c=c, dmp=dmp: q.tensor_scalar(
                    dmp, srcs[ii], 1.0, 0.0, op0=ALU.mult, op1=ALU.add,
                    accum_out=small[:, c + ii:c + ii + 1]),
                    deps=later(tx[ii], tok_dump_free[di],
                               sets_prev if ii == 0 else None))
                tok_dump_free[di] = t1
                di = dump_ctr[0] % 2
                dump_ctr[0] += 1
                dmp = arbf(o_dump + di * 512, o_dump + (di + 1) * 512)
                t2 = Sc.add("dve", lambda q, ii=ii, c=c, dmp=dmp:
                            q.scalar_tensor_tensor(
                                dmp, srcs[ii], 1.0, srcs[ii],
                                op0=ALU.mult, op1=ALU.mult,
                                accum_out=small[:, c + 2 + ii:c + 3 + ii]),
                            deps=later(tx[ii], tok_dump_free[di]))
                tok_dump_free[di] = t2
                tss.append((t1, t2))
            d1 = Sc.add("dve", lambda q, c=c: q.tensor_scalar(
                small[:, c + 4:c + 6], small[:, c:c + 2], -1.0 / D, None,
                op0=ALU.mult), deps=[tss[0][0], tss[1][0]])
            d2 = Sc.add("dve", lambda q, c=c: q.scalar_tensor_tensor(
                small[:, c:c + 2], small[:, c + 4:c + 6], 1.0,
                small[:, c + 4:c + 6], op0=ALU.mult, op1=ALU.mult), deps=[d1])
            d3 = Sc.add("dve", lambda q, c=c: q.scalar_tensor_tensor(
                small[:, c + 2:c + 4], small[:, c + 2:c + 4], 1.0 / D,
                small[:, c:c + 2], op0=ALU.mult, op1=ALU.subtract),
                deps=[tss[0][1], tss[1][1], d2])
            a1 = Sc.add("act", lambda q, c=c: q.activation(
                small[:, c:c + 2], small[:, c + 2:c + 4], AF.Ln, bias=epsB[:]),
                deps=[d3])
            a2 = Sc.add("act", lambda q, c=c: q.activation(
                small[:, c + 2:c + 4], small[:, c:c + 2], AF.Exp, scale=-0.5),
                deps=[a1])
            toks = []
            for ii in range(GT):
                a3 = Sc.add("dve", lambda q, ii=ii, c=c: q.tensor_scalar(
                    small[:, c + 6 + ii:c + 7 + ii], small[:, c + 4 + ii:c + 5 + ii],
                    small[:, c + 2 + ii:c + 3 + ii], None, op0=ALU.mult), deps=[a2])
                hdst = arbf(o_h1 + ii * 512, o_h1 + (ii + 1) * 512)
                t9 = Sc.add("dve", lambda q, ii=ii, c=c, hdst=hdst: q.tensor_scalar(
                    hdst, srcs[ii],
                    small[:, c + 2 + ii:c + 3 + ii],
                    small[:, c + 6 + ii:c + 7 + ii],
                    op0=ALU.mult, op1=ALU.add),
                    deps=later(a3, tok_h1_free[ii]))
                toks.append(t9)
            ln_guard[c] = toks[-1]
            grp_h1[gidx] = toks

        def emit_group_pre(gidx):
            wl = []
            for nf in range(NWQ):
                wi = nf
                t_wl = Sc.add("sp", lambda q, nf=nf, wi=wi: q.dma_start(
                    out=arbf(o_qkvw + wi * 512, o_qkvw + (wi + 1) * 512),
                    in_=wq_d[:, nf * 1024:(nf + 1) * 1024]),
                    deps=later(tok_wq_free[wi]), dma="gw%d" % (wi % 4))
                wl.append(t_wl)
            grp_wl[gidx] = wl
            if gidx >= 2:
                p2 = gidx - 2
                wl2 = []
                for hc in range(NWQ):
                    wi = hc
                    t_wl = Sc.add("sp", lambda q, hc=hc, wi=wi: q.dma_start(
                        out=arbf(o_wf2 + wi * 512, o_wf2 + (wi + 1) * 512),
                        in_=weff2_h[hc * 128:(hc + 1) * 128, :]),
                        deps=later(tok_wf2_free[wi], *t_folds), dma="gg%d" % (wi % 4))
                    wl2.append(t_wl)
                ffn_wl2[p2] = wl2
                wl1 = []
                for hc in range(NWF):
                    wi = hc
                    t_wl = Sc.add("gp", lambda q, hc=hc, wi=wi: q.dma_start(
                        out=arbf(o_wf1 + wi * 512, o_wf1 + (wi + 1) * 512).rearrange(
                            "p (dc n) -> p dc n", n=128),
                        in_=weff_h[hc].rearrange("dc pp n -> pp dc n")),
                        deps=later(tok_wf1_free[wi], *t_folds), dma="gf%d" % (wi % 4))
                    wl1.append(t_wl)
                ffn_wl[p2] = wl1

        def emit_group(gidx):
            g0 = gidx * GT
            pos = (g0 * T) % RING
            ht_deps = later(ffn_mm_done.get(gidx - 2))
            toks_ht = []
            for ii in range(GT):
                hdst = arbf(o_h1 + ii * 512, o_h1 + (ii + 1) * 512)
                t_h = grp_h1[gidx][ii]
                ttr = []
                d0 = later(t_h, t_idb, reg["b6"]) + ht_deps
                for dc in range(8):
                    tt = Sc.add("pe", lambda q, dc=dc, hdst=hdst: q.transpose(
                        pab[:, 2 * TRB + dc * 128:2 * TRB + (dc + 1) * 128],
                        hdst[:, dc * 128:(dc + 1) * 128], id_b[:]),
                        deps=d0 if dc == 0 else [], inc=(dc == 7))
                    ttr.append(tt)
                te = Sc.add("act", lambda q, ii=ii: q.copy(
                    htv.rearrange("p (dc n) -> p dc n", n=256)[:, :, ii * 128:(ii + 1) * 128],
                    pab[:, 2 * TRB:2 * TRB + 1024].rearrange("p (dc n) -> p dc n", n=128)),
                    deps=[ttr[-1]])
                reg["b6"] = te
                tok_h1_free[ii] = te
                toks_ht.append(te)
            if grp_stop == "ln":
                tok_grp_evac[gidx] = toks_ht[-1]
                qkv_mm_done[gidx] = None
                return
            wb = [arbf(o_qkvw + i * 512, o_qkvw + (i + 1) * 512) for i in range(NWQ)]
            qkv_ps_free = [reg["b01"], reg["b01"]]
            nw = [0]
            last_ev = None
            last_mm = None
            for (wd, kindp) in ((wq_d, "q"), (wk_d, "k"), (wv_d, "v")):
                for nf in range(8):
                    wi = nw[0] % NWQ
                    if nw[0] < NWQ:
                        t_wl = grp_wl[gidx][nf]
                    else:
                        t_wl = Sc.add("sp", lambda q, wd=wd, nf=nf, wi=wi: q.dma_start(
                            out=arbf(o_qkvw + wi * 512, o_qkvw + (wi + 1) * 512),
                            in_=wd[:, nf * 1024:(nf + 1) * 1024]),
                            deps=later(tok_wq_free[wi]), dma="gw%d" % (wi % 4))
                    nw[0] += 1
                    pp = (nf % 2) * 512
                    mms = []
                    for dc in range(8):
                        tm = Sc.add("pe", lambda q, dc=dc, pp=pp, wi=wi: q.matmul(
                            pa[:, pp:pp + 256], wb[wi][:, dc * 128:(dc + 1) * 128],
                            htv[:, dc * 256:(dc + 1) * 256],
                            start=(dc == 0), stop=(dc == 7)),
                            deps=later(t_wl, *toks_ht, qkv_ps_free[nf % 2])
                            if dc == 0 else [], inc=(dc == 7))
                        mms.append(tm)
                    last_mm = mms[-1]
                    tok_wq_free[wi] = mms[-1]
                    if kindp == "q":
                        dst = arbf(o_qt + (gidx % 2) * 1024,
                                   o_qt + (gidx % 2) * 1024 + 1024)[:, nf * 256:(nf + 1) * 256]
                        te = Sc.add("dve", lambda q, dst=dst, pp=pp, nf=nf:
                                    q.tensor_scalar(dst, pa[:, pp:pp + 256], 0.125,
                                                    bq8[:, nf:nf + 1],
                                                    op0=ALU.mult, op1=ALU.add),
                                    deps=later(mms[-1], t_bq8,
                                               attn_done.get(gidx - 1)))
                    else:
                        rv = ktv if kindp == "k" else vtv
                        te = Sc.add("dve", lambda q, rv=rv, pp=pp, nf=nf, pos=pos,
                                    bb=(bk_s if kindp == "k" else bv_s):
                                    q.tensor_scalar(rv[:, nf, pos:pos + 256],
                                                    pa[:, pp:pp + 256],
                                                    bb[:, nf:nf + 1], None, op0=ALU.add),
                                    deps=later(mms[-1], t_bk, t_bv,
                                               attn_done.get(gidx - 2)))
                    qkv_ps_free[nf % 2] = te
                    last_ev = te
            reg["b01"] = last_ev
            qkv_mm_done[gidx] = last_mm
            tok_grp_evac[gidx] = last_ev
            if grp_stop == "qkv":
                return
            for w in (g0, g0 + 1):
                emit_vslot(w, last_ev)

        def emit_attn(t):
            if attn_stop == "skip":
                return None, (lambda: None)
            mk = 1 if t == 0 else (2 if t == NT - 1 else 0)
            b01t = badd.ap()[:, mk * 512:(mk + 1) * 512]
            gq = t // GT
            qtb = arbf(o_qt + (gq % 2) * 1024, o_qt + (gq % 2) * 1024 + 1024)
            qoff = (t % GT) * 128
            gready = tok_grp_evac.get((t + 1) // GT if (t + 1) < NT else t // GT)
            wave_state = {}
            av_last_l = [None]

            def av_wave(wv):
                t_exp_a, t_exp_b, ptb = wave_state[wv]
                avfirst = True
                bfirst = True
                last = None
                for hh in (0, 2, 1, 3):
                    h = wv * 4 + hh
                    for kt in range(2):
                        w = t + kt
                        vp0 = o_vp + (w % NSLOT) * 520
                        if avfirst:
                            dd = later(t_exp_a, tok_vp[t], tok_vp[t + 1],
                                       reg["b45"] if wv == 0 else None,
                                       reg["b7"] if wv == 0 else None)
                        elif hh in (1, 3) and bfirst:
                            dd = later(t_exp_b)
                            bfirst = False
                        else:
                            dd = []
                        avfirst = False
                        o_start = (h % 8 == 0 and kt == 0)
                        o_stop = (h % 8 == 7 and kt == 1)
                        r_start = (h == 0 and kt == 0)
                        r_stop = (h == 15 and kt == 1)
                        sb_ = (hh % 2) * 2 + hh // 2
                        vs_ = (h % 2) * 8 + h // 2
                        Sc.add("pe", lambda q, sb_=sb_, vs_=vs_, h=h, kt=kt, vp0=vp0,
                               ptb=ptb, o_start=o_start, o_stop=o_stop:
                               q.matmul(pa[:, OP + h * 64:OP + h * 64 + 64],
                                        ptb[:, sb_ * 256 + kt * 128:sb_ * 256 + kt * 128 + 128],
                                        arbf(vp0, vp0 + 520)[:, vs_ * 64:vs_ * 64 + 64],
                                        start=o_start, stop=o_stop),
                               deps=dd, inc=False)
                        last = Sc.add("pe", lambda q, sb_=sb_, h=h, kt=kt, ptb=ptb,
                                      r_start=r_start, r_stop=r_stop:
                                      q.matmul(pa[:, RP + h:RP + h + 1],
                                               ptb[:, sb_ * 256 + kt * 128:
                                                   sb_ * 256 + kt * 128 + 128],
                                               ones_colb[:], start=r_start, stop=r_stop),
                                      deps=[t_ones3] if (wv == 0 and hh == 0 and kt == 0)
                                      else [], inc=(hh == 3 and kt == 1))
                pt_free[wv % 2] = last
                av_last_l[0] = last

            for wv in range(4):
                stp = ST0 if (wv % 2 == 0) else ST1
                regk = "b01" if (wv % 2 == 0) else "b23"
                jobs = []
                job_bases_l = []
                wraps = any(((t * T - 64 + kk * 128) % RING) + 128 > RING
                            for kk in range(2))
                split_mode = bool(mk) or wraps
                bank_regions = ({}, {})
                for hh in (0, 2, 1, 3):
                    h = wv * 4 + hh
                    for kt in range(2):
                        k0 = t * T - 64 + kt * 128
                        col = stp + (hh % 2) * 512 + (hh // 2) * 256 + kt * 128
                        if split_mode:
                            segs = []
                            for half in range(2):
                                kk = k0 + half * 64
                                if 0 <= kk and kk + 64 <= S:
                                    segs.append((half * 64, kk, 64))
                        else:
                            segs = [(0, k0, 128)]
                        for (mb, ts0, cnt) in segs:
                            p0 = ts0 % RING
                            take = min(cnt, RING - p0)
                            parts = ((mb, take, p0),) if take == cnt else \
                                ((mb, take, p0), (mb + take, cnt - take, 0))
                            for (mb2, cnt2, rp) in parts:
                                jobs.append((lambda q, h=h, mb2=mb2, cnt2=cnt2, rp=rp,
                                             col=col, qoff=qoff, qtb=qtb, ss=None:
                                             q.matmul(
                                    pa[mb2:mb2 + cnt2, col:col + 128],
                                    ktv[(h % 2) * 64:(h % 2) * 64 + 64, h // 2, rp:rp + cnt2],
                                    qtb[(h % 2) * 64:(h % 2) * 64 + 64,
                                        (h // 2) * 256 + qoff:(h // 2) * 256 + qoff + 128],
                                    start=ss[0], stop=ss[1]), None, col, mb2))
                                job_bases_l.append((h % 2) * 64)
                                br = bank_regions[hh % 2]
                                br[mb2] = max(br.get(mb2, 0), cnt2)
                        if split_mode:
                            have = {mb for (mb, _, _) in segs}
                            for pb in (0, 64):
                                if pb not in have:
                                    br = bank_regions[hh % 2]
                                    br[pb] = max(br.get(pb, 0), 64)
                for bank in (0, 1):
                    for mb2 in sorted(bank_regions[bank]):
                        cnt2 = bank_regions[bank][mb2]
                        jobs.append((lambda q, bank=bank, mb2=mb2, cnt2=cnt2,
                                     stp=stp, b01t=b01t, ss=None: q.matmul(
                            pa[mb2:mb2 + cnt2, stp + bank * 512:stp + bank * 512 + 512],
                            id_b[:, mb2:mb2 + cnt2], b01t,
                            start=ss[0], stop=ss[1]), None, stp + bank * 512, mb2))
                        job_bases_l.append(0)
                keys = [((c - stp) // 512, mb) for (_, _, c, mb) in jobs]
                job_bases = job_bases_l
                t_mmw = None
                for ji, (fn, extra, col, mb) in enumerate(jobs):
                    k = keys[ji]
                    ss = (keys.index(k) == ji,
                          (len(keys) - 1 - keys[::-1].index(k)) == ji)
                    fn2 = (lambda q, fn=fn, ss=ss: fn(q, ss=ss))
                    dd = later(gready, tok_grp_evac[t // GT], st_free[wv % 2],
                               reg[regk], extra, t_b01, t_idb) if ji == 0 else \
                        later(extra)
                    t_mmw = Sc.add("pe", fn2, deps=dd, inc=(ji == len(jobs) - 1),
                                   base=job_bases[ji])
                if wv - 1 in wave_state:
                    av_wave(wv - 1)
                if attn_stop in ("st", "cm", "negm"):
                    st_free[wv % 2] = t_mmw
                    av_last_l[0] = t_mmw
                    continue
                ptb = arbf(o_pt + (wv % 2) * 512, o_pt + (wv % 2) * 512 + 512)
                t_exp_a = Sc.add("act", lambda q, stp=stp, ptb=ptb: q.activation(
                    ptb[:, 0:512], pa[:, stp:stp + 512], AF.Exp),
                    deps=later(t_mmw, pt_free[wv % 2]))
                t_exp_b = Sc.add("act", lambda q, stp=stp, ptb=ptb: q.activation(
                    ptb[:, 512:1024], pa[:, stp + 512:stp + 1024], AF.Exp),
                    deps=[t_exp_a])
                st_free[wv % 2] = t_exp_b
                reg[regk] = t_exp_b
                if attn_stop == "exp":
                    av_last_l[0] = t_exp_b
                    pt_free[wv % 2] = t_exp_b
                    continue
                wave_state[wv] = (t_exp_a, t_exp_b, ptb)
            if 3 in wave_state:
                av_wave(3)
            if attn_stop == "waves":
                return av_last_l[0], (lambda: None)
            t_r2 = Sc.add("dve", lambda q: q.reciprocal(rtmp[:, 16:32], pa[:, RP:RP + 16]),
                          deps=later(av_last_l[0], rtmp_free[0]))
            reg["b7"] = t_r2
            rin = bass.AP(rtmp.ap().tensor, 16, [[32, 128], [1, 16], [0, 64]])
            t_oe = Sc.add("dve", lambda q, rin=rin: q.tensor_tensor(
                arbf(o_osb, o_osb + 512).rearrange("p (hh n) -> p hh n", n=64),
                pa[:, OP:OP + 1024].rearrange("p (hh n) -> p hh n", n=64),
                rin, op=ALU.mult), deps=later(t_r2, tok_osb_free[0]))
            rtmp_free[0] = t_oe
            reg["b45"] = t_oe
            if attn_stop == "rinv":
                return t_oe, (lambda: None)
            ttr = []
            d0 = later(t_oe, t_idb, reg["b6"])
            for dc in range(8):
                tt = Sc.add("pe", lambda q, dc=dc: q.transpose(
                    pab[:, 2 * TRB + dc * 128:2 * TRB + (dc + 1) * 128],
                    arbf(o_osb, o_osb + 512)[:, dc * 128:(dc + 1) * 128], id_b[:]),
                    deps=d0 if dc == 0 else [], inc=(dc == 7))
                ttr.append(tt)
            t_ote = Sc.add("dve", lambda q: q.tensor_copy(arbf(o_ot, o_ot + 512),
                                                          pab[:, 2 * TRB:2 * TRB + 1024]),
                           deps=later(ttr[-1], tok_ot_free[0]))
            reg["b6"] = t_ote
            tok_osb_free[0] = t_ote
            if attn_stop == "otr":
                return t_ote, (lambda: None)
            wb = [arbf(o_qkvw + i * 512, o_qkvw + (i + 1) * 512) for i in range(NWQ)]
            mms = []
            for dc in range(8):
                wi = dc % NWQ
                t_wl = Sc.add("sp", lambda q, dc=dc, wi=wi: q.dma_start(
                    out=wb[wi][:], in_=wfc_d[dc * 128:(dc + 1) * 128, :]),
                    deps=later(tok_wq_free[wi]), dma="gfc%d" % (wi % 4))
                for nf2 in range(2):
                    tm = Sc.add("pe", lambda q, dc=dc, nf2=nf2, wi=wi: q.matmul(
                        pa[:, OP + nf2 * 512:OP + (nf2 + 1) * 512],
                        arbf(o_ot, o_ot + 512)[:, dc * 128:(dc + 1) * 128],
                        wb[wi][:, nf2 * 512:(nf2 + 1) * 512],
                        start=(dc == 0), stop=False),
                        deps=later(t_wl, t_ote,
                                   reg["b45"] if (dc == 0 and nf2 == 0) else None,
                                   tok_y_free[t % 2] if (dc == 0 and nf2 == 0) else None),
                        inc=(nf2 == 1))
                    mms.append(tm)
                tok_wq_free[wi] = mms[-1]
            for nf2 in range(2):
                tm = Sc.add("pe", lambda q, nf2=nf2: q.matmul(
                    pa[:, OP + nf2 * 512:OP + (nf2 + 1) * 512], ones_rowb[:],
                    brow_fc[0:1, nf2 * 512:(nf2 + 1) * 512], start=False, stop=True),
                    deps=later(t_ones2, t_br1), inc=(nf2 == 1))
                mms.append(tm)
            t_fc = mms[-1]
            si = t % 2
            if t % 2 == 0:
                pair_acc["c"] = 40 + 8 * (ln_set[0] % 6)
                ln_set[0] += 1
                pair_acc["ts"] = []
                pair_acc["yes"] = []
            c = pair_acc["c"]
            idx = t % 2
            ysb = arf(o_y + si * 1024, o_y + (si + 1) * 1024)
            t_ye = Sc.add("dve", lambda q, c=c, idx=idx, ysb=ysb: q.tensor_scalar(
                ysb, pa[:, OP:OP + 1024], 1.0, 0.0, op0=ALU.mult, op1=ALU.add,
                accum_out=small[:, c + idx:c + idx + 1]),
                deps=later(t_fc, tok_y_free[si],
                           ln_guard.get(c) if idx == 0 else None))
            reg["b45"] = t_ye
            pair_acc["ts"].append(t)
            pair_acc["yes"].append(t_ye)
            if t % 2 == 0:
                return t_fc, None

            def ln23_pair(c=c, ts=tuple(pair_acc["ts"]), yes=tuple(pair_acc["yes"])):
                ysbs = [arf(o_y + (t_ % 2) * 1024, o_y + (t_ % 2 + 1) * 1024)
                        for t_ in ts]
                sqs = []
                for i_ in range(2):
                    di = dump_ctr[0] % 2
                    dump_ctr[0] += 1
                    dmp = arbf(o_dump + di * 512, o_dump + (di + 1) * 512)
                    t2 = Sc.add("dve", lambda q, i_=i_, c=c, dmp=dmp:
                                q.scalar_tensor_tensor(
                                    dmp, ysbs[i_], 1.0, ysbs[i_],
                                    op0=ALU.mult, op1=ALU.mult,
                                    accum_out=small[:, c + 2 + i_:c + 3 + i_]),
                                deps=later(yes[i_], tok_dump_free[di]))
                    tok_dump_free[di] = t2
                    sqs.append(t2)
                d1 = Sc.add("dve", lambda q, c=c: q.tensor_scalar(
                    small[:, c + 4:c + 6], small[:, c:c + 2], -1.0 / D, None,
                    op0=ALU.mult), deps=[yes[0], yes[1]])
                d2 = Sc.add("dve", lambda q, c=c: q.scalar_tensor_tensor(
                    small[:, c:c + 2], small[:, c + 4:c + 6], 1.0,
                    small[:, c + 4:c + 6], op0=ALU.mult, op1=ALU.mult), deps=[d1])
                d3 = Sc.add("dve", lambda q, c=c: q.scalar_tensor_tensor(
                    small[:, c + 2:c + 4], small[:, c + 2:c + 4], 1.0 / D,
                    small[:, c:c + 2], op0=ALU.mult, op1=ALU.subtract),
                    deps=[sqs[0], sqs[1], d2])
                a1 = Sc.add("act", lambda q, c=c: q.activation(
                    small[:, c:c + 2], small[:, c + 2:c + 4], AF.Ln, bias=epsB[:]),
                    deps=[d3])
                a2 = Sc.add("act", lambda q, c=c: q.activation(
                    small[:, c + 2:c + 4], small[:, c:c + 2], AF.Exp, scale=-0.5),
                    deps=[a1])
                c2 = 40 + 8 * (ln_set[0] % 6)
                ln_set[0] += 1
                x2bs = []
                tx2s = []
                for i_, t_ in enumerate(ts):
                    a3 = Sc.add("dve", lambda q, i_=i_, c=c: q.tensor_scalar(
                        small[:, c + 6 + i_:c + 7 + i_], small[:, c + 4 + i_:c + 5 + i_],
                        small[:, c + 2 + i_:c + 3 + i_], None, op0=ALU.mult), deps=[a2])
                    t9 = Sc.add("dve", lambda q, i_=i_, c=c: q.tensor_scalar(
                        arf(o_ln, o_ln + 1024), ysbs[i_],
                        small[:, c + 2 + i_:c + 3 + i_],
                        small[:, c + 6 + i_:c + 7 + i_],
                        op0=ALU.mult, op1=ALU.add),
                        deps=later(a3, tok_ln_free[0]))
                    ln_guard[c] = t9
                    tok_y_free[t_ % 2] = t9
                    x2sl = t_ % 6
                    xsl = t_ % 4
                    x2b = arf(o_x2 + x2sl * 1024, o_x2 + (x2sl + 1) * 1024)
                    t_x2 = Sc.add("dve", lambda q, x2b=x2b, xsl=xsl, c2=c2, i_=i_:
                                  q.scalar_tensor_tensor(
                                      x2b, arf(o_ln, o_ln + 1024), 1.0,
                                      arf(o_x + xsl * 1024, o_x + (xsl + 1) * 1024),
                                      op0=ALU.mult, op1=ALU.add,
                                      accum_out=small[:, c2 + i_:c2 + i_ + 1]),
                                  deps=later(t9, tok_x[t_], tok_x2free.get(x2sl),
                                             ln_guard.get(c2) if i_ == 0 else None))
                    tok_ln_free[0] = t_x2
                    tok_x2[t_] = t_x2
                    tok_xfree[xsl] = t_x2
                    x2bs.append(x2b)
                    tx2s.append(t_x2)
                fqs = []
                for i_ in range(2):
                    di2 = dump_ctr[0] % 2
                    dump_ctr[0] += 1
                    dmp2 = arbf(o_dump + di2 * 512, o_dump + (di2 + 1) * 512)
                    f2 = Sc.add("dve", lambda q, i_=i_, c2=c2, dmp2=dmp2:
                                q.scalar_tensor_tensor(
                                    dmp2, x2bs[i_], 1.0, x2bs[i_],
                                    op0=ALU.mult, op1=ALU.mult,
                                    accum_out=small[:, c2 + 2 + i_:c2 + 3 + i_]),
                                deps=later(tx2s[i_], tok_dump_free[di2]))
                    tok_dump_free[di2] = f2
                    fqs.append(f2)
                f3 = Sc.add("dve", lambda q, c2=c2: q.tensor_scalar(
                    small[:, c2 + 4:c2 + 6], small[:, c2:c2 + 2], -1.0 / D, None,
                    op0=ALU.mult), deps=[tx2s[0], tx2s[1]])
                f4 = Sc.add("dve", lambda q, c2=c2: q.scalar_tensor_tensor(
                    small[:, c2:c2 + 2], small[:, c2 + 4:c2 + 6], 1.0,
                    small[:, c2 + 4:c2 + 6], op0=ALU.mult, op1=ALU.mult), deps=[f3])
                f5 = Sc.add("dve", lambda q, c2=c2: q.scalar_tensor_tensor(
                    small[:, c2 + 2:c2 + 4], small[:, c2 + 2:c2 + 4], 1.0 / D,
                    small[:, c2:c2 + 2], op0=ALU.mult, op1=ALU.subtract),
                    deps=[fqs[0], fqs[1], f4])
                f6 = Sc.add("act", lambda q, c2=c2: q.activation(
                    small[:, c2:c2 + 2], small[:, c2 + 2:c2 + 4], AF.Ln,
                    bias=epsB[:]), deps=[f5])
                f7 = Sc.add("act", lambda q, c2=c2: q.activation(
                    small[:, c2 + 2:c2 + 4], small[:, c2:c2 + 2], AF.Exp,
                    scale=-0.5), deps=[f6])
                for i_, t_ in enumerate(ts):
                    f8 = Sc.add("dve", lambda q, i_=i_, c2=c2: q.tensor_scalar(
                        small[:, c2 + 6 + i_:c2 + 7 + i_],
                        small[:, c2 + 4 + i_:c2 + 5 + i_],
                        small[:, c2 + 2 + i_:c2 + 3 + i_], None, op0=ALU.mult),
                        deps=[f7])
                    jj = t_ % 2
                    f9 = Sc.add("dve", lambda q, i_=i_, c2=c2, jj=jj: q.tensor_scalar(
                        arbf(o_h + jj * 512, o_h + (jj + 1) * 512), x2bs[i_],
                        small[:, c2 + 2 + i_:c2 + 3 + i_],
                        small[:, c2 + 6 + i_:c2 + 7 + i_],
                        op0=ALU.mult, op1=ALU.add),
                        deps=later(f8, tok_h_free[jj]))
                    ln_guard[c2] = f9
                    h3tok[t_] = f9
            return t_fc, ln23_pair

        def emit_ffn(ta, tb, gidx):
            pidx = ta // 2
            ttoks = []
            for jj, t in enumerate((ta, tb)):
                hdst = arbf(o_h + jj * 512, o_h + (jj + 1) * 512)
                d0 = later(h3tok[t], t_idb, reg["b6"], qkv_mm_done.get(gidx))
                ttr = []
                for dc in range(8):
                    tt = Sc.add("pe", lambda q, dc=dc, hdst=hdst: q.transpose(
                        pab[:, 2 * TRB + dc * 128:2 * TRB + (dc + 1) * 128],
                        hdst[:, dc * 128:(dc + 1) * 128], id_b[:]),
                        deps=d0 if dc == 0 else [], inc=(dc == 7))
                    ttr.append(tt)
                te = Sc.add("act", lambda q, jj=jj: q.copy(
                    htv.rearrange("p (dc n) -> p dc n", n=256)[:, :, jj * 128:(jj + 1) * 128],
                    pab[:, 2 * TRB:2 * TRB + 1024].rearrange("p (dc n) -> p dc n", n=128)),
                    deps=[ttr[-1]])
                reg["b6"] = te
                tok_h_free[jj] = te
                ttoks.append(te)
            wb1 = [arbf(o_wf1 + i * 512, o_wf1 + (i + 1) * 512) for i in range(NWF)]
            g_toks = []
            gel_ps = [reg["b01"], reg["b01"]]
            for hc in range(32):
                wi = hc % NWF
                if hc < NWF and pidx in ffn_wl:
                    t_wl = ffn_wl[pidx][hc]
                else:
                    t_wl = Sc.add("gp", lambda q, hc=hc, wi=wi: q.dma_start(
                        out=wb1[wi].rearrange("p (dc n) -> p dc n", n=128),
                        in_=weff_h[hc].rearrange("dc pp n -> pp dc n")),
                        deps=later(tok_wf1_free[wi], *t_folds), dma="gf%d" % (wi % 4))
                pp = (hc % 2) * 512
                mms = []
                for dc in range(8):
                    tm = Sc.add("pe", lambda q, dc=dc, pp=pp, wi=wi: q.matmul(
                        pa[:, pp:pp + 256], wb1[wi][:, dc * 128:(dc + 1) * 128],
                        htv[:, dc * 256:(dc + 1) * 256], start=(dc == 0), stop=(dc == 7)),
                        deps=later(t_wl, *ttoks, gel_ps[hc % 2]) if dc == 0 else [],
                        inc=(dc == 7))
                    mms.append(tm)
                tok_wf1_free[wi] = mms[-1]
                t_g = Sc.add("act", lambda q, hc=hc, pp=pp: q.activation(
                    arbf(o_g, o_g + 4096).rearrange("p (ch n) -> p ch n", n=256)[:, hc, :],
                    pa[:, pp:pp + 256], AF.Gelu, bias=bgelu[:, hc:hc + 1]),
                    deps=later(mms[-1], t_bgelu_done,
                               tok_g_free[0] if hc == 0 else None))
                gel_ps[hc % 2] = t_g
                g_toks.append(t_g)
            reg["b01"] = g_toks[-1]
            wb2 = [arbf(o_wf2 + i * 512, o_wf2 + (i + 1) * 512) for i in range(NWQ)]
            zlast = [None, None]
            for hc in range(32):
                wi = hc % NWQ
                if hc < NWQ and pidx in ffn_wl2:
                    t_wl = ffn_wl2[pidx][hc]
                else:
                    t_wl = Sc.add("sp", lambda q, hc=hc, wi=wi: q.dma_start(
                        out=wb2[wi][:], in_=weff2_h[hc * 128:(hc + 1) * 128, :]),
                        deps=later(tok_wf2_free[wi], *t_folds), dma="gg%d" % (wi % 4))
                for tg in range(2):
                    zp = Z0 if tg == 0 else Z1
                    rg = "b23" if tg == 0 else "b45"
                    for nf2 in range(2):
                        tm = Sc.add("pe", lambda q, hc=hc, tg=tg, nf2=nf2, zp=zp, wi=wi:
                                    q.matmul(pa[:, zp + nf2 * 512:zp + (nf2 + 1) * 512],
                                             arbf(o_g, o_g + 4096).rearrange(
                                                 "p (ch n) -> p ch n", n=256)
                                             [:, hc, tg * 128:(tg + 1) * 128],
                                             wb2[wi][:, nf2 * 512:(nf2 + 1) * 512],
                                             start=(hc == 0), stop=False),
                                    deps=later(t_wl, g_toks[hc],
                                               reg[rg] if (hc == 0 and nf2 == 0) else None,
                                               tok_z_free[tg] if (hc == 0 and nf2 == 0)
                                               else None), inc=(tg == 1 and nf2 == 1))
                        zlast[tg] = tm
                tok_wf2_free[wi] = zlast[1]
            z_done = [None, None]
            for tg in range(2):
                zp = Z0 if tg == 0 else Z1
                for nf2 in range(2):
                    tl = Sc.add("pe", lambda q, zp=zp, nf2=nf2: q.matmul(
                        pa[:, zp + nf2 * 512:zp + (nf2 + 1) * 512], ones_rowb[:],
                        brow_con[0:1, nf2 * 512:(nf2 + 1) * 512], start=False, stop=True),
                        deps=later(t_brz, t_ones2), inc=(nf2 == 1))
                z_done[tg] = tl
            tok_g_free[0] = z_done[1]
            c = 40 + 8 * (ln_set[0] % 6)
            ln_set[0] += 1
            t_zes = []
            for tg, t in enumerate((ta, tb)):
                zp = Z0 if tg == 0 else Z1
                rg = "b23" if tg == 0 else "b45"
                zsb = arf(o_z + tg * 1024, o_z + (tg + 1) * 1024)
                t_ze = Sc.add("dve", lambda q, zsb=zsb, zp=zp, c=c, tg=tg: q.tensor_scalar(
                    zsb, pa[:, zp:zp + 1024], 1.0, 0.0, op0=ALU.mult, op1=ALU.add,
                    accum_out=small[:, c + tg:c + tg + 1]),
                    deps=later(z_done[tg], tok_zsb_free[tg],
                               ln_guard.get(c) if tg == 0 else None))
                tok_z_free[tg] = t_ze
                reg[rg] = t_ze
                t_zes.append(t_ze)

            def ln4_cont(ta=ta, tb=tb, c=c, t_zes=t_zes):
                zsbs = [arf(o_z + tg * 1024, o_z + (tg + 1) * 1024) for tg in range(2)]
                sqs = []
                for tg in range(2):
                    di = dump_ctr[0] % 2
                    dump_ctr[0] += 1
                    dmp = arbf(o_dump + di * 512, o_dump + (di + 1) * 512)
                    t2 = Sc.add("dve", lambda q, tg=tg, c=c, dmp=dmp:
                                q.scalar_tensor_tensor(
                                    dmp, zsbs[tg], 1.0, zsbs[tg],
                                    op0=ALU.mult, op1=ALU.mult,
                                    accum_out=small[:, c + 2 + tg:c + 3 + tg]),
                                deps=later(t_zes[tg], tok_dump_free[di]))
                    tok_dump_free[di] = t2
                    sqs.append(t2)
                d1 = Sc.add("dve", lambda q, c=c: q.tensor_scalar(
                    small[:, c + 4:c + 6], small[:, c:c + 2], -1.0 / D, None,
                    op0=ALU.mult), deps=[t_zes[0], t_zes[1]])
                d2 = Sc.add("dve", lambda q, c=c: q.scalar_tensor_tensor(
                    small[:, c:c + 2], small[:, c + 4:c + 6], 1.0,
                    small[:, c + 4:c + 6], op0=ALU.mult, op1=ALU.mult), deps=[d1])
                d3 = Sc.add("dve", lambda q, c=c: q.scalar_tensor_tensor(
                    small[:, c + 2:c + 4], small[:, c + 2:c + 4], 1.0 / D,
                    small[:, c:c + 2], op0=ALU.mult, op1=ALU.subtract),
                    deps=[sqs[0], sqs[1], d2])
                a1 = Sc.add("act", lambda q, c=c: q.activation(
                    small[:, c:c + 2], small[:, c + 2:c + 4], AF.Ln, bias=epsB[:]),
                    deps=[d3])
                a2 = Sc.add("act", lambda q, c=c: q.activation(
                    small[:, c + 2:c + 4], small[:, c:c + 2], AF.Exp, scale=-0.5),
                    deps=[a1])
                t9l = None
                for tg, t in enumerate((ta, tb)):
                    a3 = Sc.add("dve", lambda q, tg=tg, c=c: q.tensor_scalar(
                        small[:, c + 6 + tg:c + 7 + tg], small[:, c + 4 + tg:c + 5 + tg],
                        small[:, c + 2 + tg:c + 3 + tg], None, op0=ALU.mult), deps=[a2])
                    t9 = Sc.add("dve", lambda q, tg=tg, c=c: q.tensor_scalar(
                        arf(o_ln, o_ln + 1024), zsbs[tg],
                        small[:, c + 2 + tg:c + 3 + tg],
                        small[:, c + 6 + tg:c + 7 + tg],
                        op0=ALU.mult, op1=ALU.add),
                        deps=later(a3, tok_ln_free[0]))
                    t9l = t9
                    x2sl = t % 6
                    t_o = Sc.add("dve", lambda q, x2sl=x2sl: q.tensor_add(
                        arf(o_outb, o_outb + 1024),
                        arf(o_ln, o_ln + 1024),
                        arf(o_x2 + x2sl * 1024, o_x2 + (x2sl + 1) * 1024)),
                        deps=later(t9, tok_out_free[0]))
                    tok_ln_free[0] = t_o
                    tok_x2free[x2sl] = t_o
                    tok_zsb_free[tg] = t_o
                    t_st = Sc.add("gp", lambda q, t=t: q.dma_start(
                        out=out_d[t * T:(t + 1) * T, :],
                        in_=arf(o_outb, o_outb + 1024)),
                        deps=[t_o], dma="sout0")
                    tok_out_free[0] = t_st
                    store_toks.append(t_st)
                ln_guard[c] = t9l
            pend_ln4.append(ln4_cont)
            return z_done[1]

        if stop_after == "fold":
            Sc.add("sp", lambda q: q.dma_start(out=out_d[0:128, :],
                                               in_=arf(o_outb, o_outb + 1024)),
                   deps=t_folds, dma="sout0")
            Sc.add("sp", lambda q: q.nop(),
                   deps=[("sout0", Sc.cnt["sout0"])], inc=False)
            _emit_engine_blocks()
            return nc
        ngl = NG
        if stop_after and stop_after.startswith("g"):
            ngl = int(stop_after[1:])
        for gidx in range(ngl):
            if gidx == 0:
                emit_pre_x(0)
            emit_group_pre(gidx)
            if gidx >= 2:
                ffn_mm_done[gidx - 2] = emit_ffn(2 * (gidx - 2), 2 * (gidx - 2) + 1, gidx - 1)
            emit_group(gidx)
            for _f in pend_ln4:
                _f()
            pend_ln4.clear()
            tfc = None
            first_t = True
            for t in [tt for tt in (gidx * GT - 1, gidx * GT) if 0 <= tt < NT]:
                tfc, _c2 = emit_attn(t)
                if _c2 is not None:
                    _c2()
                if first_t and gidx + 1 < ngl:
                    emit_pre_x(gidx + 1)
                first_t = False
            attn_done[gidx] = tfc
        if ngl == NG:
            emit_vslot(NT, tok_grp_evac[NG - 1])
            tfc, _c2 = emit_attn(NT - 1)
            attn_done[NG] = tfc
            if _c2 is not None:
                _c2()
            ffn_mm_done[NG - 2] = emit_ffn(NT - 4, NT - 3, NG - 1)
            for _f in pend_ln4:
                _f()
            pend_ln4.clear()
            emit_ffn(NT - 2, NT - 1, NG - 1)
            for _f in pend_ln4:
                _f()
            pend_ln4.clear()

        Sc.add("sp", lambda q: q.nop(),
               deps=[("sout0", Sc.cnt["sout0"])], inc=False)
        if True:
            pass

        _emit_engine_blocks()

    return nc


_CACHE = {}


def kernel(**inputs):
    g = _prep(inputs)
    if "nc" not in _CACHE:
        _CACHE["nc"] = build_program()
    nc = _CACHE["nc"]
    x = np.asarray(inputs["L0_attn_x"], dtype=np.float32)
    in_maps = []
    for b in range(8):
        m = dict(g)
        m["xb"] = np.ascontiguousarray(x[b])
        in_maps.append(m)
    res = run_bass_kernel_spmd(nc, in_maps, core_ids=list(range(8)))
    out = np.stack([np.asarray(res.results[b]["out"]) for b in range(8)], axis=0)
    return out.astype(np.float32)


# version 37
# speedup vs baseline: 1.1023x; 1.0304x over previous
import numpy as np
import ml_dtypes
import concourse.bass as bass
import concourse.mybir as mybir
from concourse.bass_utils import run_bass_kernel_spmd
from concourse.alu_op_type import AluOpType as ALU
from concourse.bass_isa import ReduceOp
from concourse.library_config import attn as LIB_ATTN
from contextlib import ExitStack

F32 = mybir.dt.float32
BF16 = mybir.dt.bfloat16
AF = mybir.ActivationFunctionType
AX = mybir.AxisListType
BF = ml_dtypes.bfloat16

S, D, H, HD, DH = 4096, 1024, 16, 64, 4096
T = 128
NT = S // T
GT = 2
NG = NT // GT
RING = 768
NSLOT = 4
EPS = 1e-5


def _stage_matrix(weights, s, dim=DH):
    w = np.asarray(weights[s], dtype=np.float32).reshape(dim // 2, 2, 2)
    s2 = 2 ** s
    M = np.zeros((dim, dim), dtype=np.float32)
    k = np.arange(dim // 2)
    q = k // s2
    r = k % s2
    ia = q * 2 * s2 + r
    ib = ia + s2
    M[ia, ia] = w[k, 0, 0]
    M[ib, ia] = w[k, 0, 1]
    M[ia, ib] = w[k, 1, 0]
    M[ib, ib] = w[k, 1, 1]
    return M


def _pi_perm(dim=DH):
    j = np.arange(dim)
    return (j & 127) * 32 + (j >> 7)


def _blocks(M):
    return np.stack([M[c * T:(c + 1) * T, c * T:(c + 1) * T] for c in range(32)])


def _conj_pi(M):
    pi = _pi_perm()
    out = np.zeros_like(M)
    out[np.ix_(pi, pi)] = M
    return out


def _fold_factor_arrays(up_w, down_w):
    Mu = [_stage_matrix(up_w, s) for s in range(12)]
    Md = [_stage_matrix(down_w, s) for s in range(12)]
    upA = np.stack([_blocks(Mu[6])] + [_blocks(Mu[s].T) for s in range(5, -1, -1)], axis=1)
    Nu = [_conj_pi(Mu[s]) for s in range(7, 12)]
    upB = np.stack([_blocks(Nu[4])] + [_blocks(Nu[s].T) for s in (3, 2, 1, 0)], axis=1)
    Nd = [_conj_pi(Md[s]) for s in range(7, 12)]
    dnB = np.stack([_blocks(Nd[0].T)] + [_blocks(Nd[s - 7]) for s in (8, 9, 10, 11)], axis=1)
    dnA = np.stack([_blocks(Md[0].T)] + [_blocks(Md[s]) for s in range(1, 7)], axis=1)
    return (upA.astype(BF), upB.astype(BF),
            dnB.astype(BF), dnA.astype(BF))


def _mask_D(kind):
    r = np.arange(128)
    c = np.arange(128)
    d0 = np.abs(c[None, :] - (r[:, None] - 64)).astype(np.float32)
    d1 = np.abs(c[None, :] - (r[:, None] + 64)).astype(np.float32)
    BIGD = np.float32(1e6)
    if kind == "first":
        d0 = d0.copy(); d0[r < 64, :] = BIGD
    if kind == "last":
        d1 = d1.copy(); d1[r >= 64, :] = BIGD
    return np.concatenate([d0, d1], axis=1)


def _prep(inputs):
    g = {}
    f32 = lambda a: np.ascontiguousarray(np.asarray(a), dtype=np.float32)
    bf = lambda a: np.ascontiguousarray(np.asarray(a).astype(BF))
    pi = _pi_perm()
    inv = np.argsort(pi)

    x = np.asarray(inputs["L0_attn_x"])
    g["alpha"] = f32(inputs["L0_attn_alpha"]).reshape(1, 1)
    g["tail_last"] = f32(x[:, S - 1, :])
    g["tail_prev"] = f32(x[:, S - 2, :])
    for nm, w in (("wq", "L0_attn_Wq"), ("wk", "L0_attn_Wk"),
                  ("wv", "L0_attn_Wv")):
        g[nm] = bf(np.asarray(inputs[w]).reshape(8, 128, 8, 128)
                   .transpose(1, 2, 0, 3).reshape(128, 8 * 1024))
    g["wfc"] = bf(inputs["L0_attn_Wfc"])
    g["wo"] = bf(inputs["L0_attn_W_o"])
    for nm, b in (("bq", "L0_attn_bq"), ("bk", "L0_attn_bk"), ("bv", "L0_attn_bv")):
        g[nm] = f32(np.asarray(inputs[b]).reshape(8, 128).T)
    g["bfc_row"] = bf(np.asarray(inputs["L0_attn_bfc"]).reshape(1, D))
    g["bcon_row"] = bf(np.asarray(inputs["L0_ffn_b_con"]).reshape(1, D))
    g["dbias_row"] = bf(np.asarray(inputs["L0_ffn_down_bias"]).reshape(1, D))
    g["wexpT_aug"] = bf(np.concatenate(
        [np.asarray(inputs["L0_ffn_W_exp"], dtype=np.float32).T,
         np.asarray(inputs["L0_ffn_b_exp"], dtype=np.float32).reshape(DH, 1)], axis=1))
    g["wcon_pi"] = bf(np.asarray(inputs["L0_ffn_W_con"])[inv, :])
    g["up_bias_pi"] = f32(np.asarray(inputs["L0_ffn_up_bias"])[inv].reshape(32, 128).T)
    upA, upB, dnB, dnA = _fold_factor_arrays(inputs["L0_ffn_up_weights"],
                                             inputs["L0_ffn_down_weights"])
    g["upA"], g["upB"], g["dnB"], g["dnA"] = upA, upB, dnB, dnA
    g["id_f32"] = np.eye(128, dtype=np.float32)
    g["id_bf"] = np.eye(128).astype(BF)
    g["dmask"] = np.stack([_mask_D("mid"), _mask_D("first"), _mask_D("last")])
    return g



DMA_SEMS = ("scst", "sfold0", "sfold1", "sfc0", "sfc1", "sfc2", "sfc3",
            "sff0", "sff1", "sff2", "sff3",
            "sx0", "sx1", "sx2", "sx3", "sout0", "sout1",
            "gw0", "gw1", "gw2", "gw3", "gf0", "gf1", "gf2", "gf3",
            "gg0", "gg1", "gg2", "gg3",
            "gfc0", "gfc1", "gfc2", "gfc3")


class Sched:
    ENG = ("pe", "act", "dve", "gp", "sp")

    def __init__(self):
        self.ops = {e: [] for e in self.ENG}
        self.cnt = {s: 0 for s in ("pe", "act", "dve", "pool") + DMA_SEMS}
        self.wm = {e: {s: 0 for s in self.cnt} for e in self.ENG}
        self.sems = {}
        self.final = {}

    def dep(self, eng, tok):
        if tok is None:
            return
        sem, v = tok
        fv = self.final.get(sem)
        if fv is not None and v < fv:
            v = fv
        if self.wm[eng][sem] >= v:
            return
        self.ops[eng].append(("w", sem, v))
        self.wm[eng][sem] = v

    def add(self, eng, fn, deps=(), inc=True, dma=False, base=0):
        for t in deps:
            self.dep(eng, t)
        if eng == "pe":
            if base != getattr(self, "_pe_base", 0):
                if not self.ops["pe"] or self.ops["pe"][-1][0] != "w":
                    self.ops["pe"].append(("o", lambda q: q.nop(), None, 0))
            self._pe_base = base
        if dma:
            sem = dma
            assert sem in DMA_SEMS, sem
            self.cnt[sem] += 16
            self.ops[eng].append(("o", fn, sem, 16))
            return (sem, self.cnt[sem])
        sem = "pool" if eng in ("gp", "sp") else eng
        if eng == "sp":
            inc = False
        if inc:
            self.cnt[sem] += 1
            self.ops[eng].append(("o", fn, sem, 1))
        else:
            self.ops[eng].append(("o", fn, None, 0))
        return (sem, self.cnt[sem])

    def replay(self, eng, q):
        for rec in self.ops[eng]:
            if rec[0] == "w":
                q.wait_ge(self.sems[rec[1]], rec[2])
            else:
                _, fn, sem, n = rec
                inst = fn(q)
                if sem is not None:
                    inst.then_inc(self.sems[sem], n)


def later(*toks):
    best = {}
    for t in toks:
        if t is None:
            continue
        s, v = t
        if best.get(s, -1) < v:
            best[s] = v
    return [(s, v) for s, v in best.items()]



def build_program(stop_after=None, attn_stop=None, grp_stop=None, vs_stop=None):
    nc = bass.Bass("TRN2", target_bir_lowering=False, debug=False)

    def din(name, shape, dt):
        return nc.dram_tensor(name, list(shape), dt, kind="ExternalInput").ap()

    xb = din("xb", (S, D), F32)
    tail_last = din("tail_last", (8, D), F32)
    tail_prev = din("tail_prev", (8, D), F32)
    alpha = din("alpha", (1, 1), F32)
    wq_d = din("wq", (128, 8 * D), BF16)
    wk_d = din("wk", (128, 8 * D), BF16)
    wv_d = din("wv", (128, 8 * D), BF16)
    wfc_d = din("wfc", (D, D), BF16)
    wo_d = din("wo", (H, HD, HD), BF16)
    bq_d = din("bq", (128, 8), F32)
    bk_d = din("bk", (128, 8), F32)
    bv_d = din("bv", (128, 8), F32)
    bfc_row_d = din("bfc_row", (1, D), BF16)
    bcon_row_d = din("bcon_row", (1, D), BF16)
    dbias_row_d = din("dbias_row", (1, D), BF16)
    wexpT_d = din("wexpT_aug", (DH, D + 1), BF16)
    wcon_d = din("wcon_pi", (DH, D), BF16)
    upbias_d = din("up_bias_pi", (128, 32), F32)
    upA_d = din("upA", (32, 7, 128, 128), BF16)
    upB_d = din("upB", (32, 5, 128, 128), BF16)
    dnB_d = din("dnB", (32, 5, 128, 128), BF16)
    dnA_d = din("dnA", (32, 7, 128, 128), BF16)
    idf_d = din("id_f32", (128, 128), F32)
    idb_d = din("id_bf", (128, 128), BF16)
    dmask_d = din("dmask", (3, 128, 256), F32)

    out_d = nc.dram_tensor("out", [S, D], F32, kind="ExternalOutput").ap()
    rt1 = nc.dram_tensor("rt1", [DH, D + 1], BF16, kind="Internal").ap()
    rt2 = nc.dram_tensor("rt2", [DH, D], BF16, kind="Internal").ap()
    weff_h = nc.dram_tensor("weff_h", [32, 8, 128, 128], BF16, kind="Internal").ap()
    weff2_h = nc.dram_tensor("weff2_h", [DH, D], BF16, kind="Internal").ap()

    Sc = Sched()

    with ExitStack() as st:
        sbuf = lambda n, sh, dt: st.enter_context(nc.sbuf_tensor(n, list(sh), dt))
        id_f = sbuf("id_f", (128, 128), F32)
        id_b = sbuf("id_b", (128, 128), BF16)
        ones_row = sbuf("ones_row", (1, 128), F32)
        ones_rowb = sbuf("ones_rowb", (1, 128), BF16)
        ones_colb = sbuf("ones_colb", (128, 1), BF16)
        zrow_b = sbuf("zrow_b", (1, 128), BF16)
        bq8 = sbuf("bq8", (128, 8), F32)
        bk_s = sbuf("bk_s", (128, 8), F32)
        bv_s = sbuf("bv_s", (128, 8), F32)
        brow_fc = sbuf("brow_fc", (1, D), BF16)
        brow_con = sbuf("brow_con", (1, D), BF16)
        brow_db = sbuf("brow_db", (1, D), BF16)
        wo_s = sbuf("wo_s", (128, 512), BF16)
        bgelu = sbuf("bgelu", (128, 32), F32)
        dm_s = sbuf("dm_s", (128, 3 * 256), F32)
        badd = sbuf("badd", (128, 3 * 512), BF16)
        vB = sbuf("vB", (128, 1), F32)
        epsB = sbuf("epsB", (128, 1), F32)
        small = sbuf("small", (128, 112), F32)
        rtmp = sbuf("rtmp", (128, 32), F32)

        AR_N = 44544
        ar = sbuf("arena", (128, AR_N), F32)
        A = ar.ap()
        Ab = A.bitcast(BF16)
        _off = [0]

        def carve(n):
            o = _off[0]
            _off[0] += n
            assert _off[0] <= AR_N, f"arena overflow {_off[0]}"
            return o

        o_ktr = carve(3072)
        o_vtr = carve(3072)
        o_qt = carve(2048)
        o_vp = carve(NSLOT * 520)
        o_x = carve(4 * 1024)
        o_x2 = carve(6 * 1024)
        o_h = carve(2 * 512)
        o_h1 = carve(2 * 512)
        o_ht = carve(1024)
        o_g = carve(4096)
        o_pt = carve(2 * 512)
        o_ot = carve(512)
        o_osb = carve(512)
        o_y = carve(2 * 1024)
        o_qkvw = carve(4 * 512)
        o_wf1 = carve(6 * 512)
        o_wf2 = carve(4 * 512)
        o_dump = carve(1024)
        o_z = carve(2 * 1024)
        o_ln = carve(1024)
        o_outb = carve(1024)
        fa_cb = [o_g, o_g + 1025, o_g + 2050, o_z]
        fa_fb = [o_z + 1025, o_wf1, o_wf1 + 896, o_outb]
        fa_t = [o_wf2, o_wf2 + 1024]
        fa_t2 = [o_wf2 + 1024, o_wf2 + 2048]
        fa_p = [o_wf1 + 1792, o_wf1 + 2048]

        def arf(c0, c1):
            return A[:, c0:c1]

        def arbf(c0, c1):
            return Ab[:, 2 * c0:2 * c1]

        tl_s = arf(o_x2 + 0, o_x2 + 1024)
        tp_s = arf(o_x2 + 1024, o_x2 + 2048)
        tsq = arf(o_x2 + 2048, o_x2 + 3072)
        lnl = arf(o_x2 + 3072, o_x2 + 4096)
        lnp = arf(o_x2 + 4096, o_x2 + 5120)

        pa = st.enter_context(nc.psum_tensor("pa", [128, 4096], F32))
        pab = pa.ap().bitcast(BF16)
        ST0, ST1 = 0, 1024
        VPP = 1024
        OP = 2048
        Z0, Z1 = 1024, 2048
        TRB = 3072
        RP = 3584
        NMP = 3200
        NMB = 3456
        FLP = 3800

        for s in Sc.cnt:
            Sc.sems[s] = st.enter_context(nc.semaphore(s))
        blk = st.enter_context(nc.Block())

        def _emit_engine_blocks():
            @blk.tensor
            def _(q):
                Sc.replay("pe", q)

            @blk.scalar
            def _(q):
                Sc.replay("act", q)

            @blk.vector
            def _(q):
                Sc.replay("dve", q)

            @blk.gpsimd
            def _(q):
                Sc.replay("gp", q)

            @blk.sync
            def _(q):
                Sc.replay("sp", q)

        reg = {"b01": None, "b23": None, "b45": None, "b6": None, "b7": None}

        t_idf = Sc.add("sp", lambda q: q.dma_start(out=id_f[:], in_=idf_d), dma="scst")
        t_idb = Sc.add("sp", lambda q: q.dma_start(out=id_b[:], in_=idb_d), dma="scst")
        t_bq = Sc.add("sp", lambda q: q.dma_start(out=bq8[:], in_=bq_d), dma="scst")
        t_bk = Sc.add("sp", lambda q: q.dma_start(out=bk_s[:], in_=bk_d), dma="scst")
        t_bv = Sc.add("sp", lambda q: q.dma_start(out=bv_s[:], in_=bv_d), dma="scst")
        t_br1 = Sc.add("sp", lambda q: q.dma_start(out=brow_fc[:], in_=bfc_row_d), dma="scst")
        t_br2 = Sc.add("sp", lambda q: q.dma_start(out=brow_con[:], in_=bcon_row_d), dma="scst")
        t_br3 = Sc.add("sp", lambda q: q.dma_start(out=brow_db[:], in_=dbias_row_d), dma="scst")
        t_brz = Sc.add("dve", lambda q: q.tensor_add(brow_con[:], brow_con[:],
                                                     brow_db[:]), deps=[t_br2, t_br3])
        t_wo = Sc.add("sp", lambda q: q.dma_start(
            out=wo_s[:].rearrange("p (hh n) -> p hh n", n=64),
            in_=wo_d.rearrange("(hh two) hd n -> (two hd) hh n", two=2)), dma="scst")
        t_dm = Sc.add("sp", lambda q: q.dma_start(
            out=dm_s[:].rearrange("p (m n) -> p m n", n=256),
            in_=dmask_d.rearrange("m p n -> p m n")), dma="scst")
        t_tl = Sc.add("sp", lambda q: q.dma_start(out=tl_s[0:8, :], in_=tail_last), dma="scst")
        t_tp = Sc.add("sp", lambda q: q.dma_start(out=tp_s[0:8, :], in_=tail_prev), dma="scst")
        t_al = Sc.add("sp", lambda q: q.dma_start(out=small[0:1, 0:1], in_=alpha), dma="scst")
        t_ub = Sc.add("sp", lambda q: q.dma_start(out=bgelu[:], in_=upbias_d), dma="scst")
        Sc.final["scst"] = Sc.cnt["scst"]

        t_ones = Sc.add("dve", lambda q: q.memset(ones_row[:], 1.0))
        t_ones2 = Sc.add("dve", lambda q: q.memset(ones_rowb[:], 1.0))
        t_ones3 = Sc.add("dve", lambda q: q.memset(ones_colb[:], 1.0))
        t_zrow = Sc.add("dve", lambda q: q.memset(zrow_b[:], 0.0))
        t_eps = Sc.add("dve", lambda q: q.memset(epsB[:], EPS))
        t_bq8 = Sc.add("dve", lambda q: q.tensor_scalar_mul(bq8[:], bq8[:], 0.125),
                       deps=[t_bq])
        t_oc8 = Sc.add("dve", lambda q: q.memset(small[0:8, 1:2], 1.0))

        def ln_rows(src, dst, tok, cbase):
            c = cbase
            t1 = Sc.add("dve", lambda q: q.reduce_sum(srowap(c), src[0:8, :], axis=AX.X),
                        deps=[tok])
            t3 = Sc.add("act", lambda q: q.activation(tsq[0:8, :], src[0:8, :], AF.Square,
                                                      accum_out=srowap(c + 1)), deps=[tok])
            t4 = Sc.add("dve", lambda q: q.tensor_scalar(srowap(c + 2), srowap(c),
                                                         1.0 / D, None, op0=ALU.mult),
                        deps=[t1])
            t5 = Sc.add("dve", lambda q: q.tensor_tensor(srowap(c + 3), srowap(c + 2),
                                                         srowap(c + 2), op=ALU.mult), deps=[t4])
            t6 = Sc.add("dve", lambda q: q.tensor_scalar(srowap(c + 4), srowap(c + 1),
                                                         1.0 / D, None, op0=ALU.mult), deps=[t3])
            t7 = Sc.add("dve", lambda q: q.tensor_tensor(srowap(c + 5), srowap(c + 4),
                                                         srowap(c + 3), op=ALU.subtract),
                        deps=[t5, t6])
            t8 = Sc.add("act", lambda q: q.activation(srowap(c + 6), srowap(c + 5), AF.Ln,
                                                      bias=epsB[0:8, :]), deps=[t7, t_eps])
            t9 = Sc.add("act", lambda q: q.activation(srowap(c + 7), srowap(c + 6), AF.Exp,
                                                      scale=-0.5), deps=[t8])
            ta = Sc.add("dve", lambda q: q.tensor_scalar(srowap(c + 8), srowap(c + 2),
                                                         srowap(c + 7), -1.0, op0=ALU.mult,
                                                         op1=ALU.mult), deps=[t9])
            tb = Sc.add("act", lambda q: q.activation(dst[0:8, :], src[0:8, :], AF.Identity,
                                                      bias=srowap(c + 8),
                                                      scale=srowap(c + 7)), deps=[ta])
            return tb

        def srowap(c):
            return small[0:8, c:c + 1]

        tk1 = ln_rows(tl_s, lnl, t_tl, 2)
        tk2 = ln_rows(tp_s, lnp, t_tp, 11)
        t_fl = Sc.add("dve", lambda q: q.tensor_sub(tsq[0:8, :], lnl[0:8, :], lnp[0:8, :]),
                      deps=[tk1, tk2])
        t_fa = Sc.add("act", lambda q: q.activation(lnl[0:8, :], tsq[0:8, :], AF.Abs,
                                                    accum_out=small[0:8, 20:21]), deps=[t_fl])
        t_fs = Sc.add("pe", lambda q: q.matmul(pa[0:1, FLP:FLP + 1], small[0:8, 20:21],
                                               small[0:8, 1:2], start=True, stop=True),
                      deps=[t_fa, t_oc8])
        reg["b7"] = t_fs
        t_fm = Sc.add("dve", lambda q: q.tensor_scalar(small[0:1, 21:22], pa[0:1, FLP:FLP + 1],
                                                       1.0 / (8 * D), None, op0=ALU.mult),
                      deps=[t_fs])
        reg["b7"] = t_fm
        t_a = Sc.add("act", lambda q: q.activation(small[0:1, 22:23], small[0:1, 0:1],
                                                   AF.Sigmoid), deps=[t_al])
        t_s1 = Sc.add("dve", lambda q: q.tensor_sub(small[0:1, 23:24], small[0:1, 21:22],
                                                    lnl[0:1, 0:1]), deps=[t_fm, t_fa])
        t_s2 = Sc.add("dve", lambda q: q.tensor_tensor(small[0:1, 24:25], small[0:1, 23:24],
                                                       small[0:1, 22:23], op=ALU.mult),
                      deps=[t_s1, t_a])
        t_s3 = Sc.add("dve", lambda q: q.tensor_add(small[0:1, 25:26], small[0:1, 24:25],
                                                    lnl[0:1, 0:1]), deps=[t_s2])
        t_s4 = Sc.add("act", lambda q: q.activation(small[0:1, 26:27], small[0:1, 25:26],
                                                    AF.Sigmoid), deps=[t_s3])
        t_s5 = Sc.add("dve", lambda q: q.tensor_scalar(small[0:1, 27:28], small[0:1, 26:27],
                                                       96.0, 24.0, op0=ALU.mult, op1=ALU.add),
                      deps=[t_s4])
        t_vb1 = Sc.add("pe", lambda q: q.matmul(pa[:, NMB:NMB + 1], ones_row[:],
                                                small[0:1, 27:28], start=True, stop=True),
                       deps=[t_s5, t_ones, reg["b7"]])
        t_vb = Sc.add("dve", lambda q: q.tensor_copy(vB[:], pa[:, NMB:NMB + 1]), deps=[t_vb1])
        reg["b7"] = t_vb
        t_b1 = Sc.add("dve", lambda q: q.tensor_scalar(dm_s[:], dm_s[:], 2.0, vB[:],
                                                       op0=ALU.mult, op1=ALU.subtract),
                      deps=[t_dm, t_vb])
        t_bg = Sc.add("dve", lambda q: q.tensor_single_scalar(dm_s[:], dm_s[:], 0.0,
                                                              op=ALU.is_gt), deps=[t_b1])
        t_b01 = None
        for rep in (0, 1):
            t_b01 = Sc.add("dve", lambda q, rep=rep: q.tensor_scalar(
                badd.ap().rearrange("p (k n) -> p k n", n=512)
                [:, :, rep * 256:(rep + 1) * 256],
                dm_s[:].rearrange("p (k n) -> p k n", n=256),
                -1e30, None, op0=ALU.mult), deps=[t_bg])
        phase0_sbuf_done = t_b01

        PF0, PF1 = 0, 512
        PCH = 1024

        def fold_products(fbuf, nf, t_fac, extra):
            f0 = fbuf[0]
            fac = lambda i: Ab[:, 2 * f0 + i * 128:2 * f0 + (i + 1) * 128]
            cur = fac(0)
            tok = t_fac
            dst0b = Ab[:, 2 * fa_p[0]:2 * fa_p[0] + 128]
            dst1b = Ab[:, 2 * fa_p[0] + 128:2 * fa_p[0] + 256]
            for i in range(1, nf):
                pf = PF0 if (i % 2) else PF1
                tm = Sc.add("pe", lambda q, i=i, cur=cur, pf=pf, fac=fac:
                            q.matmul(pa[:, pf:pf + 128], fac(i), cur,
                                     start=True, stop=True),
                            deps=later(tok, reg["b01"], *extra))
                dst = dst0b if (i % 2) else dst1b
                te = Sc.add("dve", lambda q, dst=dst, pf=pf:
                            q.tensor_copy(dst, pa[:, pf:pf + 128]), deps=[tm])
                reg["b01"] = te
                cur = dst
                tok = te
            return cur, tok

        def fold_apply(lhsT, w, rhs_ap, deps):
            toks = []
            c0 = 0
            dd = list(deps)
            while c0 < w:
                cw = min(512, w - c0)
                tm = Sc.add("pe", lambda q, c0=c0, cw=cw, lhsT=lhsT, rhs_ap=rhs_ap:
                            q.matmul(pa[:, PCH + c0:PCH + c0 + cw], lhsT,
                                     rhs_ap[:, c0:c0 + cw], start=True, stop=True),
                            deps=dd)
                dd = []
                toks.append(tm)
                c0 += cw
            return toks

        rt1v = rt1.rearrange("(p cc) n -> p cc n", cc=32)
        rt2v = rt2.rearrange("(l mm) n -> mm l n", mm=128)
        w2v = weff2_h.rearrange("(p cc) n -> p cc n", cc=32)
        prev_w = None
        prev_ev = None
        prevw2 = [None, None]
        buf_free = [None] * 4
        fac_free = [None] * 4

        def fold_pass(load_src, fac_src, nfac, width, out_fn, pass_id):
            nonlocal prev_w, prev_ev
            pass_start = later(prevw2[0], prevw2[1])
            pass_start = pass_start if pass_start else [None]
            pass_start0 = list(pass_start)
            for c in range(32):
                parity = c % 4
                cb = [fa_cb[parity]]
                fb = [fa_fb[parity]]
                t_l = Sc.add("sp", lambda q, c=c, cb=cb: q.dma_start(
                    out=Ab[:, 2 * cb[0]:2 * cb[0] + width],
                    in_=load_src(c)), deps=later(buf_free[parity], *pass_start0),
                    dma="sfc%d" % parity)
                t_f = Sc.add("gp", lambda q, c=c, fb=fb, nfac=nfac: q.dma_start(
                    out=Ab[:, 2 * fb[0]:2 * fb[0] + nfac * 128]
                    .rearrange("p (f n) -> p f n", n=128),
                    in_=fac_src(c)), deps=later(fac_free[parity]), dma="sff%d" % parity)
                lhsT, tokp = fold_products(fb, nfac, t_f, [])
                fac_free[parity] = tokp
                toks = fold_apply(lhsT, width, Ab[:, 2 * cb[0]:2 * cb[0] + width],
                                  later(tokp, t_l, reg["b23"], reg["b45"]))
                buf_free[parity] = toks[-1]
                prev_w, prev_ev = out_fn(c, toks)
                reg["b23"] = prev_ev
                reg["b45"] = prev_ev

        def outA(c, toks):
            ft = fa_t if c % 2 == 0 else fa_t2
            e1 = Ab[:, 2 * ft[0]:2 * ft[0] + 1025]
            t1 = Sc.add("dve", lambda q, e1=e1: q.tensor_copy(e1, pa[:, PCH:PCH + 1025]),
                        deps=toks + later(prevw2[c % 2]))
            w1 = Sc.add("sp", lambda q, c=c, e1=e1: q.dma_start(
                out=rt1v[:, c, :], in_=e1), deps=[t1], dma="sfold%d" % (c % 2))
            prevw2[c % 2] = w1
            return w1, t1

        fold_pass(lambda c: wexpT_d[c * 128:(c + 1) * 128, :],
                  lambda c: upA_d[c].rearrange("f pp n -> pp f n"), 7, 1025, outA, 0)

        t_rt1_done = prev_w

        def outB(c, toks):
            ft = fa_t if c % 2 == 0 else fa_t2
            tb = Sc.add("dve", lambda q, c=c: q.tensor_add(
                bgelu[:, c:c + 1], pa[:, PCH + 1024:PCH + 1025], bgelu[:, c:c + 1]),
                deps=toks + [t_ub])
            evb = arbf(ft[0], ft[0] + 512)
            e1 = Sc.add("act", lambda q, evb=evb: q.copy(evb, pa[:, PCH:PCH + 1024]),
                        deps=toks + later(prevw2[c % 2]))
            ttr = []
            dep0 = later(e1, t_idb, reg["b6"])
            for dc in range(8):
                tt = Sc.add("pe", lambda q, dc=dc, evb=evb: q.transpose(
                    pab[:, 2 * TRB + dc * 128:2 * TRB + (dc + 1) * 128],
                    evb[:, dc * 128:(dc + 1) * 128], id_b[:]),
                    deps=dep0 if dc == 0 else [], inc=(dc == 7))
                ttr.append(tt)
            e2 = Sc.add("act", lambda q, ft=ft: q.copy(arbf(ft[0] + 512, ft[0] + 1024),
                                                       pab[:, 2 * TRB:2 * TRB + 1024]),
                        deps=[ttr[-1]])
            reg["b6"] = e2
            w1 = Sc.add("sp", lambda q, c=c, ft=ft: q.dma_start(
                out=weff_h[c].rearrange("dc pp n -> pp dc n"),
                in_=arbf(ft[0] + 512, ft[0] + 1024).rearrange(
                    "p (dc n) -> p dc n", n=128)), deps=[e2], dma="sfold%d" % (c % 2))
            prevw2[c % 2] = w1
            return w1, tb

        fold_pass(lambda c: rt1[c * 128:(c + 1) * 128, :],
                  lambda c: upB_d[c].rearrange("f pp n -> pp f n"), 5, 1025, outB, 1)
        t_bgelu_done = prev_ev

        def outDB(c, toks):
            ft = fa_t if c % 2 == 0 else fa_t2
            e1 = Ab[:, 2 * ft[0]:2 * ft[0] + 1024]
            t1 = Sc.add("dve", lambda q, e1=e1: q.tensor_copy(e1, pa[:, PCH:PCH + 1024]),
                        deps=toks + later(prevw2[c % 2]))
            w1 = t1
            for i in range(4):
                w1 = Sc.add("sp", lambda q, c=c, e1=e1, i=i: q.dma_start(
                    out=rt2v[4 * c + i], in_=e1[32 * i:32 * (i + 1), :]),
                    deps=[t1], dma="sfold%d" % (c % 2))
            prevw2[c % 2] = w1
            return w1, t1

        fold_pass(lambda c: wcon_d[c * 128:(c + 1) * 128, :],
                  lambda c: dnB_d[c].rearrange("f pp n -> pp f n"), 5, 1024, outDB, 2)

        def outDA(c, toks):
            ft = fa_t if c % 2 == 0 else fa_t2
            evb = arbf(ft[0], ft[0] + 512)
            e1 = Sc.add("act", lambda q, evb=evb: q.copy(evb, pa[:, PCH:PCH + 1024]),
                        deps=toks + later(prevw2[c % 2]))
            w1 = Sc.add("sp", lambda q, c=c, evb=evb: q.dma_start(
                out=w2v[:, c, :], in_=evb), deps=[e1], dma="sfold%d" % (c % 2))
            prevw2[c % 2] = w1
            return w1, e1

        fold_pass(lambda c: rt2[c * 128:(c + 1) * 128, :],
                  lambda c: dnA_d[c].rearrange("f pp n -> pp f n"), 7, 1024, outDA, 3)
        t_folds = [prevw2[0], prevw2[1]]

        tok_x = {}
        tok_x2 = {}
        tok_xfree = {}
        tok_x2free = {}
        tok_vp = {}
        tok_grp_evac = {}
        attn_done = {}
        qkv_mm_done = {}
        ffn_mm_done = {}
        NWQ = 4
        NWF = 6
        tok_qt_free = [None, None]
        fc_slot_tok = [None] * 4
        tok_wq_free = [None] * NWQ
        tok_wf1_free = [None] * NWF
        tok_wf2_free = [None] * NWQ
        tok_vslot_free = {}
        st_free = [None, None]
        pt_free = [None, None]
        negm_col_free = [None] * 4
        cm_col_free = [None] * 4
        ar_col_free = [None] * 4
        rtmp_free = [None]
        tok_dump_free = [None, None]
        dump_ctr = [0]
        tok_h_free = [None, None]
        tok_h1_free = [None, None]
        h3tok = {}
        grp_h1 = {}
        grp_wl = {}
        ffn_wl = {}
        ffn_wl2 = {}
        tok_ht_writer = [None]
        tok_ht_readers = [None]
        tok_osb_free = [None]
        tok_ot_free = [None]
        tok_y_free = [None, None]
        tok_ln_free = [None]
        tok_z_free = [None, None]
        tok_zsb_free = [None, None]
        tok_out_free = [None, None]
        tok_g_free = [t_folds[0] if t_folds[0] else None]
        for _tf in t_folds:
            tok_g_free[0] = _tf if _tf else tok_g_free[0]
        for _sl5 in range(5):
            tok_x2free[_sl5] = phase0_sbuf_done
        tok_out_free[0] = t_folds[1] if len(t_folds) > 1 and t_folds[1] else t_folds[0]
        tok_zsb_free[0] = tok_zsb_free[1] = tok_g_free[0]
        gel_free = [None, None]
        store_toks = []
        ln_set = [0]
        pend_ln2 = []
        pend_ln4 = []
        pair_acc = {"c": None, "ts": [], "yes": []}

        def emit_ln(src_ap, dst_ap, deps_src):
            c = 40 + 8 * (ln_set[0] % 6)
            ln_set[0] += 1
            di = dump_ctr[0] % 2
            dump_ctr[0] += 1
            dmp = arbf(o_dump + di * 512, o_dump + (di + 1) * 512)
            sets_prev = ln_guard.get(c)
            t1 = Sc.add("dve", lambda q: q.tensor_scalar(dmp, src_ap, 1.0, 0.0,
                                                         op0=ALU.mult, op1=ALU.add,
                                                         accum_out=small[:, c:c + 1]),
                        deps=later(*deps_src, tok_dump_free[di], sets_prev))
            t2 = Sc.add("dve", lambda q: q.scalar_tensor_tensor(
                dmp, src_ap, 1.0, src_ap, op0=ALU.mult, op1=ALU.mult,
                accum_out=small[:, c + 1:c + 2]), deps=[t1])
            tok_dump_free[di] = t2
            t3 = Sc.add("dve", lambda q: q.tensor_scalar(small[:, c + 2:c + 3],
                                                         small[:, c:c + 1], 1.0 / D, None,
                                                         op0=ALU.mult), deps=[t1])
            t4 = Sc.add("dve", lambda q: q.tensor_tensor(small[:, c + 3:c + 4],
                                                         small[:, c + 2:c + 3],
                                                         small[:, c + 2:c + 3],
                                                         op=ALU.mult), deps=[t3])
            t5 = Sc.add("dve", lambda q: q.tensor_scalar(small[:, c + 4:c + 5],
                                                         small[:, c + 1:c + 2], 1.0 / D,
                                                         small[:, c + 3:c + 4],
                                                         op0=ALU.mult, op1=ALU.subtract),
                        deps=[t2, t4])
            t6 = Sc.add("act", lambda q: q.activation(small[:, c + 5:c + 6],
                                                      small[:, c + 4:c + 5], AF.Ln,
                                                      bias=epsB[:]), deps=[t5])
            t7 = Sc.add("act", lambda q: q.activation(small[:, c + 6:c + 7],
                                                      small[:, c + 5:c + 6], AF.Exp,
                                                      scale=-0.5), deps=[t6])
            t8 = Sc.add("dve", lambda q: q.tensor_scalar(small[:, c + 7:c + 8],
                                                         small[:, c + 2:c + 3],
                                                         small[:, c + 6:c + 7], -1.0,
                                                         op0=ALU.mult, op1=ALU.mult),
                        deps=[t7])
            t9 = Sc.add("act", lambda q: q.activation(dst_ap, src_ap, AF.Identity,
                                                      bias=small[:, c + 7:c + 8],
                                                      scale=small[:, c + 6:c + 7]),
                        deps=[t8])
            ln_guard[c] = t9
            return t9

        ln_guard = {}

        ktv = arbf(o_ktr, o_ktr + 3072).rearrange("p (ch n) -> p ch n", n=RING)
        vtv = arbf(o_vtr, o_vtr + 3072).rearrange("p (ch n) -> p ch n", n=RING)
        htv = arbf(o_ht, o_ht + 1024)

        def emit_vslot(w, gdep):
            sl = w % NSLOT
            vp0 = o_vp + sl * 520
            if w == 0:
                segs = [(64, 0, 64)]
            elif w == NT:
                segs = [(0, w * T - 64, 64)]
            else:
                segs = [(0, w * T - 64, 128)]
            vjobs = []
            for h in list(range(0, H, 2)) + list(range(1, H, 2)):
                for (pb, ts0, cnt) in segs:
                    p0 = ts0 % RING
                    take = min(cnt, RING - p0)
                    parts = ((pb, take, p0),) if take == cnt else \
                        ((pb, take, p0), (pb + take, cnt - take, 0))
                    for (mb, cnt2, rp) in parts:
                        vjobs.append((h, mb, cnt2, rp))
            vbanks = [(h % 2, mb) for (h, mb, _, _) in vjobs]
            last = None
            for ji, (h, mb, cnt2, rp) in enumerate(vjobs):
                b = vbanks[ji]
                vs = (vbanks.index(b) == ji,
                      (len(vbanks) - 1 - vbanks[::-1].index(b)) == ji)
                dd = later(gdep, t_wo, reg["b23"],
                           tok_vslot_free.get(sl)) if ji == 0 else []
                vc = (h % 2) * 512 + (h // 2) * 64
                last = Sc.add("pe", lambda q, h=h, mb=mb, cnt2=cnt2, rp=rp, vs=vs, vc=vc:
                              q.matmul(pa[mb:mb + cnt2,
                                          VPP + vc:VPP + vc + 64],
                                       vtv[(h % 2) * 64:(h % 2) * 64 + 64, h // 2,
                                           rp:rp + cnt2],
                                       wo_s[(h % 2) * 64:(h % 2) * 64 + 64,
                                            (h // 2) * 64:(h // 2) * 64 + 64],
                                       start=vs[0], stop=vs[1]),
                              deps=dd, inc=(ji == len(vjobs) - 1),
                              base=(h % 2) * 64)
            if vs_stop == "mm":
                tok_vp[w] = last
                return
            if w == 0:
                evrows = (64, 128)
            elif w == NT:
                evrows = (0, 64)
            else:
                evrows = (0, 128)
            t_ev = Sc.add("act", lambda q, vp0=vp0, evrows=evrows: q.copy(
                arbf(vp0, vp0 + 512)[evrows[0]:evrows[1], :],
                pa[evrows[0]:evrows[1], VPP:VPP + 1024]), deps=[last])
            reg["b23"] = t_ev
            if w == 0 or w == NT:
                z0, z1 = (0, 64) if w == 0 else (64, 128)
                t_ev = Sc.add("dve", lambda q, vp0=vp0, z0=z0, z1=z1: q.memset(
                    arbf(vp0, vp0 + 512)[z0:z1, :], 0.0), deps=[t_ev])
            if vs_stop == "ev":
                tok_vp[w] = t_ev
                return
            tok_vp[w] = t_ev

        def emit_pre_x(gidx):
            g0 = gidx * GT
            c = 40 + 8 * (ln_set[0] % 6)
            ln_set[0] += 1
            tx = []
            srcs = []
            for ii in range(GT):
                ti = g0 + ii
                sl = ti % 4
                t_x = Sc.add("sp", lambda q, ti=ti, sl=sl: q.dma_start(
                    out=arf(o_x + sl * 1024, o_x + (sl + 1) * 1024),
                    in_=xb[ti * T:(ti + 1) * T, :]),
                    deps=later(tok_xfree.get(sl)), dma="sx%d" % sl)
                tok_x[ti] = t_x
                tx.append(t_x)
                srcs.append(arf(o_x + sl * 1024, o_x + (sl + 1) * 1024))
            sets_prev = ln_guard.get(c)
            tss = []
            for ii in range(GT):
                di = dump_ctr[0] % 2
                dump_ctr[0] += 1
                dmp = arbf(o_dump + di * 512, o_dump + (di + 1) * 512)
                t1 = Sc.add("dve", lambda q, ii=ii, c=c, dmp=dmp: q.tensor_scalar(
                    dmp, srcs[ii], 1.0, 0.0, op0=ALU.mult, op1=ALU.add,
                    accum_out=small[:, c + ii:c + ii + 1]),
                    deps=later(tx[ii], tok_dump_free[di],
                               sets_prev if ii == 0 else None))
                tok_dump_free[di] = t1
                di = dump_ctr[0] % 2
                dump_ctr[0] += 1
                dmp = arbf(o_dump + di * 512, o_dump + (di + 1) * 512)
                t2 = Sc.add("dve", lambda q, ii=ii, c=c, dmp=dmp:
                            q.scalar_tensor_tensor(
                                dmp, srcs[ii], 1.0, srcs[ii],
                                op0=ALU.mult, op1=ALU.mult,
                                accum_out=small[:, c + 2 + ii:c + 3 + ii]),
                            deps=later(tx[ii], tok_dump_free[di]))
                tok_dump_free[di] = t2
                tss.append((t1, t2))
            d1 = Sc.add("dve", lambda q, c=c: q.tensor_scalar(
                small[:, c + 4:c + 6], small[:, c:c + 2], -1.0 / D, None,
                op0=ALU.mult), deps=[tss[0][0], tss[1][0]])
            d2 = Sc.add("dve", lambda q, c=c: q.scalar_tensor_tensor(
                small[:, c:c + 2], small[:, c + 4:c + 6], 1.0,
                small[:, c + 4:c + 6], op0=ALU.mult, op1=ALU.mult), deps=[d1])
            d3 = Sc.add("dve", lambda q, c=c: q.scalar_tensor_tensor(
                small[:, c + 2:c + 4], small[:, c + 2:c + 4], 1.0 / D,
                small[:, c:c + 2], op0=ALU.mult, op1=ALU.subtract),
                deps=[tss[0][1], tss[1][1], d2])
            a1 = Sc.add("act", lambda q, c=c: q.activation(
                small[:, c:c + 2], small[:, c + 2:c + 4], AF.Ln, bias=epsB[:]),
                deps=[d3])
            a2 = Sc.add("act", lambda q, c=c: q.activation(
                small[:, c + 2:c + 4], small[:, c:c + 2], AF.Exp, scale=-0.5),
                deps=[a1])
            toks = []
            for ii in range(GT):
                a3 = Sc.add("dve", lambda q, ii=ii, c=c: q.tensor_scalar(
                    small[:, c + 6 + ii:c + 7 + ii], small[:, c + 4 + ii:c + 5 + ii],
                    small[:, c + 2 + ii:c + 3 + ii], None, op0=ALU.mult), deps=[a2])
                hdst = arbf(o_h1 + ii * 512, o_h1 + (ii + 1) * 512)
                t9 = Sc.add("dve", lambda q, ii=ii, c=c, hdst=hdst: q.tensor_scalar(
                    hdst, srcs[ii],
                    small[:, c + 2 + ii:c + 3 + ii],
                    small[:, c + 6 + ii:c + 7 + ii],
                    op0=ALU.mult, op1=ALU.add),
                    deps=later(a3, tok_h1_free[ii]))
                toks.append(t9)
            ln_guard[c] = toks[-1]
            grp_h1[gidx] = toks

        def emit_group_pre(gidx):
            wl = []
            for nf in range(NWQ):
                wi = nf
                t_wl = Sc.add("sp", lambda q, nf=nf, wi=wi: q.dma_start(
                    out=arbf(o_qkvw + wi * 512, o_qkvw + (wi + 1) * 512),
                    in_=wq_d[:, nf * 1024:(nf + 1) * 1024]),
                    deps=later(tok_wq_free[wi]), dma="gw%d" % (wi % 4))
                wl.append(t_wl)
            grp_wl[gidx] = wl
            if gidx >= 2:
                p2 = gidx - 2
                wl2 = []
                for hc in range(NWQ):
                    wi = hc
                    t_wl = Sc.add("sp", lambda q, hc=hc, wi=wi: q.dma_start(
                        out=arbf(o_wf2 + wi * 512, o_wf2 + (wi + 1) * 512),
                        in_=weff2_h[hc * 128:(hc + 1) * 128, :]),
                        deps=later(tok_wf2_free[wi], *t_folds), dma="gg%d" % (wi % 4))
                    wl2.append(t_wl)
                ffn_wl2[p2] = wl2
                wl1 = []
                for hc in range(NWF):
                    wi = hc
                    t_wl = Sc.add("gp", lambda q, hc=hc, wi=wi: q.dma_start(
                        out=arbf(o_wf1 + wi * 512, o_wf1 + (wi + 1) * 512).rearrange(
                            "p (dc n) -> p dc n", n=128),
                        in_=weff_h[hc].rearrange("dc pp n -> pp dc n")),
                        deps=later(tok_wf1_free[wi], *t_folds), dma="gf%d" % (wi % 4))
                    wl1.append(t_wl)
                ffn_wl[p2] = wl1

        def emit_group(gidx):
            g0 = gidx * GT
            pos = (g0 * T) % RING
            ht_deps = later(ffn_mm_done.get(gidx - 2))
            toks_ht = []
            for ii in range(GT):
                hdst = arbf(o_h1 + ii * 512, o_h1 + (ii + 1) * 512)
                t_h = grp_h1[gidx][ii]
                ttr = []
                d0 = later(t_h, t_idb, reg["b6"]) + ht_deps
                for dc in range(8):
                    tt = Sc.add("pe", lambda q, dc=dc, hdst=hdst: q.transpose(
                        pab[:, 2 * TRB + dc * 128:2 * TRB + (dc + 1) * 128],
                        hdst[:, dc * 128:(dc + 1) * 128], id_b[:]),
                        deps=d0 if dc == 0 else [], inc=(dc == 7))
                    ttr.append(tt)
                te = Sc.add("act", lambda q, ii=ii: q.copy(
                    htv.rearrange("p (dc n) -> p dc n", n=256)[:, :, ii * 128:(ii + 1) * 128],
                    pab[:, 2 * TRB:2 * TRB + 1024].rearrange("p (dc n) -> p dc n", n=128)),
                    deps=[ttr[-1]])
                reg["b6"] = te
                tok_h1_free[ii] = te
                toks_ht.append(te)
            if grp_stop == "ln":
                tok_grp_evac[gidx] = toks_ht[-1]
                qkv_mm_done[gidx] = None
                return
            wb = [arbf(o_qkvw + i * 512, o_qkvw + (i + 1) * 512) for i in range(NWQ)]
            qkv_ps_free = [reg["b01"], reg["b01"]]
            nw = [0]
            last_ev = None
            last_mm = None
            for (wd, kindp) in ((wq_d, "q"), (wk_d, "k"), (wv_d, "v")):
                for nf in range(8):
                    wi = nw[0] % NWQ
                    if nw[0] < NWQ:
                        t_wl = grp_wl[gidx][nf]
                    else:
                        t_wl = Sc.add("sp", lambda q, wd=wd, nf=nf, wi=wi: q.dma_start(
                            out=arbf(o_qkvw + wi * 512, o_qkvw + (wi + 1) * 512),
                            in_=wd[:, nf * 1024:(nf + 1) * 1024]),
                            deps=later(tok_wq_free[wi]), dma="gw%d" % (wi % 4))
                    nw[0] += 1
                    pp = (nf % 2) * 512
                    mms = []
                    for dc in range(8):
                        tm = Sc.add("pe", lambda q, dc=dc, pp=pp, wi=wi: q.matmul(
                            pa[:, pp:pp + 256], wb[wi][:, dc * 128:(dc + 1) * 128],
                            htv[:, dc * 256:(dc + 1) * 256],
                            start=(dc == 0), stop=(dc == 7)),
                            deps=later(t_wl, *toks_ht, qkv_ps_free[nf % 2])
                            if dc == 0 else [], inc=(dc == 7))
                        mms.append(tm)
                    last_mm = mms[-1]
                    tok_wq_free[wi] = mms[-1]
                    if kindp == "q":
                        dst = arbf(o_qt + (gidx % 2) * 1024,
                                   o_qt + (gidx % 2) * 1024 + 1024)[:, nf * 256:(nf + 1) * 256]
                        te = Sc.add("dve", lambda q, dst=dst, pp=pp, nf=nf:
                                    q.tensor_scalar(dst, pa[:, pp:pp + 256], 0.125,
                                                    bq8[:, nf:nf + 1],
                                                    op0=ALU.mult, op1=ALU.add),
                                    deps=later(mms[-1], t_bq8,
                                               attn_done.get(gidx - 1)))
                    else:
                        rv = ktv if kindp == "k" else vtv
                        te = Sc.add("dve", lambda q, rv=rv, pp=pp, nf=nf, pos=pos,
                                    bb=(bk_s if kindp == "k" else bv_s):
                                    q.tensor_scalar(rv[:, nf, pos:pos + 256],
                                                    pa[:, pp:pp + 256],
                                                    bb[:, nf:nf + 1], None, op0=ALU.add),
                                    deps=later(mms[-1], t_bk, t_bv,
                                               attn_done.get(gidx - 2)))
                    qkv_ps_free[nf % 2] = te
                    last_ev = te
            reg["b01"] = last_ev
            qkv_mm_done[gidx] = last_mm
            tok_grp_evac[gidx] = last_ev
            if grp_stop == "qkv":
                return
            for w in (g0, g0 + 1):
                emit_vslot(w, last_ev)

        def emit_attn(t):
            if attn_stop == "skip":
                return None, (lambda: None)
            mk = 1 if t == 0 else (2 if t == NT - 1 else 0)
            b01t = badd.ap()[:, mk * 512:(mk + 1) * 512]
            gq = t // GT
            qtb = arbf(o_qt + (gq % 2) * 1024, o_qt + (gq % 2) * 1024 + 1024)
            qoff = (t % GT) * 128
            gready = tok_grp_evac.get((t + 1) // GT if (t + 1) < NT else t // GT)
            wave_state = {}
            av_last_l = [None]

            def av_wave(wv):
                t_msk, ptb = wave_state[wv]
                avfirst = True
                last = None
                for hh in range(4):
                    h = wv * 4 + hh
                    for kt in range(2):
                        w = t + kt
                        vp0 = o_vp + (w % NSLOT) * 520
                        dd = later(t_msk, tok_vp[t], tok_vp[t + 1],
                                   reg["b45"] if wv == 0 else None,
                                   reg["b7"] if wv == 0 else None) if avfirst else \
                            later(t_msk)
                        avfirst = False
                        o_start = (h % 8 == 0 and kt == 0)
                        o_stop = (h % 8 == 7 and kt == 1)
                        r_start = (h == 0 and kt == 0)
                        r_stop = (h == 15 and kt == 1)
                        sb_ = (hh % 2) * 2 + hh // 2
                        vs_ = (h % 2) * 8 + h // 2
                        Sc.add("pe", lambda q, sb_=sb_, vs_=vs_, h=h, kt=kt, vp0=vp0,
                               ptb=ptb, o_start=o_start, o_stop=o_stop:
                               q.matmul(pa[:, OP + h * 64:OP + h * 64 + 64],
                                        ptb[:, sb_ * 256 + kt * 128:sb_ * 256 + kt * 128 + 128],
                                        arbf(vp0, vp0 + 520)[:, vs_ * 64:vs_ * 64 + 64],
                                        start=o_start, stop=o_stop),
                               deps=dd, inc=False)
                        last = Sc.add("pe", lambda q, sb_=sb_, h=h, kt=kt, ptb=ptb,
                                      r_start=r_start, r_stop=r_stop:
                                      q.matmul(pa[:, RP + h:RP + h + 1],
                                               ptb[:, sb_ * 256 + kt * 128:
                                                   sb_ * 256 + kt * 128 + 128],
                                               ones_colb[:], start=r_start, stop=r_stop),
                                      deps=[t_ones3] if (wv == 0 and hh == 0 and kt == 0)
                                      else [], inc=(hh == 3 and kt == 1))
                pt_free[wv % 2] = last
                av_last_l[0] = last

            for wv in range(4):
                stp = ST0 if (wv % 2 == 0) else ST1
                regk = "b01" if (wv % 2 == 0) else "b23"
                jobs = []
                job_bases_l = []
                wraps = any(((t * T - 64 + kk * 128) % RING) + 128 > RING
                            for kk in range(2))
                split_mode = bool(mk) or wraps
                bank_regions = ({}, {})
                for hh in (0, 2, 1, 3):
                    h = wv * 4 + hh
                    for kt in range(2):
                        k0 = t * T - 64 + kt * 128
                        col = stp + (hh % 2) * 512 + (hh // 2) * 256 + kt * 128
                        if split_mode:
                            segs = []
                            for half in range(2):
                                kk = k0 + half * 64
                                if 0 <= kk and kk + 64 <= S:
                                    segs.append((half * 64, kk, 64))
                        else:
                            segs = [(0, k0, 128)]
                        for (mb, ts0, cnt) in segs:
                            p0 = ts0 % RING
                            take = min(cnt, RING - p0)
                            parts = ((mb, take, p0),) if take == cnt else \
                                ((mb, take, p0), (mb + take, cnt - take, 0))
                            for (mb2, cnt2, rp) in parts:
                                jobs.append((lambda q, h=h, mb2=mb2, cnt2=cnt2, rp=rp,
                                             col=col, qoff=qoff, qtb=qtb, ss=None:
                                             q.matmul(
                                    pa[mb2:mb2 + cnt2, col:col + 128],
                                    ktv[(h % 2) * 64:(h % 2) * 64 + 64, h // 2, rp:rp + cnt2],
                                    qtb[(h % 2) * 64:(h % 2) * 64 + 64,
                                        (h // 2) * 256 + qoff:(h // 2) * 256 + qoff + 128],
                                    start=ss[0], stop=ss[1]), None, col, mb2))
                                job_bases_l.append((h % 2) * 64)
                                br = bank_regions[hh % 2]
                                br[mb2] = max(br.get(mb2, 0), cnt2)
                        if split_mode:
                            have = {mb for (mb, _, _) in segs}
                            for pb in (0, 64):
                                if pb not in have:
                                    br = bank_regions[hh % 2]
                                    br[pb] = max(br.get(pb, 0), 64)
                for bank in (0, 1):
                    for mb2 in sorted(bank_regions[bank]):
                        cnt2 = bank_regions[bank][mb2]
                        jobs.append((lambda q, bank=bank, mb2=mb2, cnt2=cnt2,
                                     stp=stp, b01t=b01t, ss=None: q.matmul(
                            pa[mb2:mb2 + cnt2, stp + bank * 512:stp + bank * 512 + 512],
                            id_b[:, mb2:mb2 + cnt2], b01t,
                            start=ss[0], stop=ss[1]), None, stp + bank * 512, mb2))
                        job_bases_l.append(0)
                keys = [((c - stp) // 512, mb) for (_, _, c, mb) in jobs]
                job_bases = job_bases_l
                t_mmw = None
                for ji, (fn, extra, col, mb) in enumerate(jobs):
                    k = keys[ji]
                    ss = (keys.index(k) == ji,
                          (len(keys) - 1 - keys[::-1].index(k)) == ji)
                    fn2 = (lambda q, fn=fn, ss=ss: fn(q, ss=ss))
                    dd = later(gready, tok_grp_evac[t // GT], st_free[wv % 2],
                               reg[regk], extra, t_b01, t_idb) if ji == 0 else \
                        later(extra)
                    t_mmw = Sc.add("pe", fn2, deps=dd, inc=(ji == len(jobs) - 1),
                                   base=job_bases[ji])
                if wv - 1 in wave_state:
                    av_wave(wv - 1)
                if attn_stop in ("st", "cm", "negm"):
                    st_free[wv % 2] = t_mmw
                    av_last_l[0] = t_mmw
                    continue
                ptb = arbf(o_pt + (wv % 2) * 512, o_pt + (wv % 2) * 512 + 512)
                t_exp = Sc.add("act", lambda q, stp=stp, ptb=ptb: q.activation(
                    ptb, pa[:, stp:stp + 1024], AF.Exp),
                    deps=later(t_mmw, pt_free[wv % 2]))
                st_free[wv % 2] = t_exp
                reg[regk] = t_exp
                if attn_stop == "exp":
                    av_last_l[0] = t_exp
                    pt_free[wv % 2] = t_exp
                    continue
                wave_state[wv] = (t_exp, ptb)
            if 3 in wave_state:
                av_wave(3)
            if attn_stop == "waves":
                return av_last_l[0], (lambda: None)
            t_r2 = Sc.add("dve", lambda q: q.reciprocal(rtmp[:, 16:32], pa[:, RP:RP + 16]),
                          deps=later(av_last_l[0], rtmp_free[0]))
            reg["b7"] = t_r2
            rin = bass.AP(rtmp.ap().tensor, 16, [[32, 128], [1, 16], [0, 64]])
            t_oe = Sc.add("dve", lambda q, rin=rin: q.tensor_tensor(
                arbf(o_osb, o_osb + 512).rearrange("p (hh n) -> p hh n", n=64),
                pa[:, OP:OP + 1024].rearrange("p (hh n) -> p hh n", n=64),
                rin, op=ALU.mult), deps=later(t_r2, tok_osb_free[0]))
            rtmp_free[0] = t_oe
            reg["b45"] = t_oe
            if attn_stop == "rinv":
                return t_oe, (lambda: None)
            ttr = []
            d0 = later(t_oe, t_idb, reg["b6"])
            for dc in range(8):
                tt = Sc.add("pe", lambda q, dc=dc: q.transpose(
                    pab[:, 2 * TRB + dc * 128:2 * TRB + (dc + 1) * 128],
                    arbf(o_osb, o_osb + 512)[:, dc * 128:(dc + 1) * 128], id_b[:]),
                    deps=d0 if dc == 0 else [], inc=(dc == 7))
                ttr.append(tt)
            t_ote = Sc.add("dve", lambda q: q.tensor_copy(arbf(o_ot, o_ot + 512),
                                                          pab[:, 2 * TRB:2 * TRB + 1024]),
                           deps=later(ttr[-1], tok_ot_free[0]))
            reg["b6"] = t_ote
            tok_osb_free[0] = t_ote
            if attn_stop == "otr":
                return t_ote, (lambda: None)
            wb = [arbf(o_qkvw + i * 512, o_qkvw + (i + 1) * 512) for i in range(NWQ)]
            mms = []
            if t == NT - 1:
                dcs = [0, 1, 2, 3, 4, 5, 6, 7]
                n_res = 4
            elif t % 2 == 0 and t > 0:
                dcs = [4, 5, 6, 7, 0, 1, 2, 3]
                n_res = 4
            else:
                dcs = list(range(8))
                n_res = 0
            for j, dc in enumerate(dcs):
                wi = dc % NWQ
                if j < n_res:
                    t_wl = fc_slot_tok[wi]
                else:
                    t_wl = Sc.add("sp", lambda q, dc=dc, wi=wi: q.dma_start(
                        out=wb[wi][:], in_=wfc_d[dc * 128:(dc + 1) * 128, :]),
                        deps=later(tok_wq_free[wi]), dma="gfc%d" % (wi % 4))
                    fc_slot_tok[wi] = t_wl
                for nf2 in range(2):
                    tm = Sc.add("pe", lambda q, dc=dc, nf2=nf2, wi=wi, j=j: q.matmul(
                        pa[:, OP + nf2 * 512:OP + (nf2 + 1) * 512],
                        arbf(o_ot, o_ot + 512)[:, dc * 128:(dc + 1) * 128],
                        wb[wi][:, nf2 * 512:(nf2 + 1) * 512],
                        start=(j == 0), stop=False),
                        deps=later(t_wl, t_ote,
                                   reg["b45"] if (j == 0 and nf2 == 0) else None,
                                   tok_y_free[t % 2] if (j == 0 and nf2 == 0) else None),
                        inc=(nf2 == 1))
                    mms.append(tm)
                tok_wq_free[wi] = mms[-1]
            for nf2 in range(2):
                tm = Sc.add("pe", lambda q, nf2=nf2: q.matmul(
                    pa[:, OP + nf2 * 512:OP + (nf2 + 1) * 512], ones_rowb[:],
                    brow_fc[0:1, nf2 * 512:(nf2 + 1) * 512], start=False, stop=True),
                    deps=later(t_ones2, t_br1), inc=(nf2 == 1))
                mms.append(tm)
            t_fc = mms[-1]
            si = t % 2
            if t % 2 == 0:
                pair_acc["c"] = 40 + 8 * (ln_set[0] % 6)
                ln_set[0] += 1
                pair_acc["ts"] = []
                pair_acc["yes"] = []
            c = pair_acc["c"]
            idx = t % 2
            ysb = arf(o_y + si * 1024, o_y + (si + 1) * 1024)
            t_ye = Sc.add("dve", lambda q, c=c, idx=idx, ysb=ysb: q.tensor_scalar(
                ysb, pa[:, OP:OP + 1024], 1.0, 0.0, op0=ALU.mult, op1=ALU.add,
                accum_out=small[:, c + idx:c + idx + 1]),
                deps=later(t_fc, tok_y_free[si],
                           ln_guard.get(c) if idx == 0 else None))
            reg["b45"] = t_ye
            pair_acc["ts"].append(t)
            pair_acc["yes"].append(t_ye)
            if t % 2 == 0:
                return t_fc, None

            def ln23_pair(c=c, ts=tuple(pair_acc["ts"]), yes=tuple(pair_acc["yes"])):
                ysbs = [arf(o_y + (t_ % 2) * 1024, o_y + (t_ % 2 + 1) * 1024)
                        for t_ in ts]
                sqs = []
                for i_ in range(2):
                    di = dump_ctr[0] % 2
                    dump_ctr[0] += 1
                    dmp = arbf(o_dump + di * 512, o_dump + (di + 1) * 512)
                    t2 = Sc.add("dve", lambda q, i_=i_, c=c, dmp=dmp:
                                q.scalar_tensor_tensor(
                                    dmp, ysbs[i_], 1.0, ysbs[i_],
                                    op0=ALU.mult, op1=ALU.mult,
                                    accum_out=small[:, c + 2 + i_:c + 3 + i_]),
                                deps=later(yes[i_], tok_dump_free[di]))
                    tok_dump_free[di] = t2
                    sqs.append(t2)
                d1 = Sc.add("dve", lambda q, c=c: q.tensor_scalar(
                    small[:, c + 4:c + 6], small[:, c:c + 2], -1.0 / D, None,
                    op0=ALU.mult), deps=[yes[0], yes[1]])
                d2 = Sc.add("dve", lambda q, c=c: q.scalar_tensor_tensor(
                    small[:, c:c + 2], small[:, c + 4:c + 6], 1.0,
                    small[:, c + 4:c + 6], op0=ALU.mult, op1=ALU.mult), deps=[d1])
                d3 = Sc.add("dve", lambda q, c=c: q.scalar_tensor_tensor(
                    small[:, c + 2:c + 4], small[:, c + 2:c + 4], 1.0 / D,
                    small[:, c:c + 2], op0=ALU.mult, op1=ALU.subtract),
                    deps=[sqs[0], sqs[1], d2])
                a1 = Sc.add("act", lambda q, c=c: q.activation(
                    small[:, c:c + 2], small[:, c + 2:c + 4], AF.Ln, bias=epsB[:]),
                    deps=[d3])
                a2 = Sc.add("act", lambda q, c=c: q.activation(
                    small[:, c + 2:c + 4], small[:, c:c + 2], AF.Exp, scale=-0.5),
                    deps=[a1])
                c2 = 40 + 8 * (ln_set[0] % 6)
                ln_set[0] += 1
                x2bs = []
                tx2s = []
                for i_, t_ in enumerate(ts):
                    a3 = Sc.add("dve", lambda q, i_=i_, c=c: q.tensor_scalar(
                        small[:, c + 6 + i_:c + 7 + i_], small[:, c + 4 + i_:c + 5 + i_],
                        small[:, c + 2 + i_:c + 3 + i_], None, op0=ALU.mult), deps=[a2])
                    t9 = Sc.add("dve", lambda q, i_=i_, c=c: q.tensor_scalar(
                        arf(o_ln, o_ln + 1024), ysbs[i_],
                        small[:, c + 2 + i_:c + 3 + i_],
                        small[:, c + 6 + i_:c + 7 + i_],
                        op0=ALU.mult, op1=ALU.add),
                        deps=later(a3, tok_ln_free[0]))
                    ln_guard[c] = t9
                    tok_y_free[t_ % 2] = t9
                    x2sl = t_ % 6
                    xsl = t_ % 4
                    x2b = arf(o_x2 + x2sl * 1024, o_x2 + (x2sl + 1) * 1024)
                    t_x2 = Sc.add("dve", lambda q, x2b=x2b, xsl=xsl, c2=c2, i_=i_:
                                  q.scalar_tensor_tensor(
                                      x2b, arf(o_ln, o_ln + 1024), 1.0,
                                      arf(o_x + xsl * 1024, o_x + (xsl + 1) * 1024),
                                      op0=ALU.mult, op1=ALU.add,
                                      accum_out=small[:, c2 + i_:c2 + i_ + 1]),
                                  deps=later(t9, tok_x[t_], tok_x2free.get(x2sl),
                                             ln_guard.get(c2) if i_ == 0 else None))
                    tok_ln_free[0] = t_x2
                    tok_x2[t_] = t_x2
                    tok_xfree[xsl] = t_x2
                    x2bs.append(x2b)
                    tx2s.append(t_x2)
                fqs = []
                for i_ in range(2):
                    di2 = dump_ctr[0] % 2
                    dump_ctr[0] += 1
                    dmp2 = arbf(o_dump + di2 * 512, o_dump + (di2 + 1) * 512)
                    f2 = Sc.add("dve", lambda q, i_=i_, c2=c2, dmp2=dmp2:
                                q.scalar_tensor_tensor(
                                    dmp2, x2bs[i_], 1.0, x2bs[i_],
                                    op0=ALU.mult, op1=ALU.mult,
                                    accum_out=small[:, c2 + 2 + i_:c2 + 3 + i_]),
                                deps=later(tx2s[i_], tok_dump_free[di2]))
                    tok_dump_free[di2] = f2
                    fqs.append(f2)
                f3 = Sc.add("dve", lambda q, c2=c2: q.tensor_scalar(
                    small[:, c2 + 4:c2 + 6], small[:, c2:c2 + 2], -1.0 / D, None,
                    op0=ALU.mult), deps=[tx2s[0], tx2s[1]])
                f4 = Sc.add("dve", lambda q, c2=c2: q.scalar_tensor_tensor(
                    small[:, c2:c2 + 2], small[:, c2 + 4:c2 + 6], 1.0,
                    small[:, c2 + 4:c2 + 6], op0=ALU.mult, op1=ALU.mult), deps=[f3])
                f5 = Sc.add("dve", lambda q, c2=c2: q.scalar_tensor_tensor(
                    small[:, c2 + 2:c2 + 4], small[:, c2 + 2:c2 + 4], 1.0 / D,
                    small[:, c2:c2 + 2], op0=ALU.mult, op1=ALU.subtract),
                    deps=[fqs[0], fqs[1], f4])
                f6 = Sc.add("act", lambda q, c2=c2: q.activation(
                    small[:, c2:c2 + 2], small[:, c2 + 2:c2 + 4], AF.Ln,
                    bias=epsB[:]), deps=[f5])
                f7 = Sc.add("act", lambda q, c2=c2: q.activation(
                    small[:, c2 + 2:c2 + 4], small[:, c2:c2 + 2], AF.Exp,
                    scale=-0.5), deps=[f6])
                for i_, t_ in enumerate(ts):
                    f8 = Sc.add("dve", lambda q, i_=i_, c2=c2: q.tensor_scalar(
                        small[:, c2 + 6 + i_:c2 + 7 + i_],
                        small[:, c2 + 4 + i_:c2 + 5 + i_],
                        small[:, c2 + 2 + i_:c2 + 3 + i_], None, op0=ALU.mult),
                        deps=[f7])
                    jj = t_ % 2
                    f9 = Sc.add("dve", lambda q, i_=i_, c2=c2, jj=jj: q.tensor_scalar(
                        arbf(o_h + jj * 512, o_h + (jj + 1) * 512), x2bs[i_],
                        small[:, c2 + 2 + i_:c2 + 3 + i_],
                        small[:, c2 + 6 + i_:c2 + 7 + i_],
                        op0=ALU.mult, op1=ALU.add),
                        deps=later(f8, tok_h_free[jj]))
                    ln_guard[c2] = f9
                    h3tok[t_] = f9
            return t_fc, ln23_pair

        def emit_ffn(ta, tb, gidx):
            pidx = ta // 2
            ttoks = []
            for jj, t in enumerate((ta, tb)):
                hdst = arbf(o_h + jj * 512, o_h + (jj + 1) * 512)
                d0 = later(h3tok[t], t_idb, reg["b6"], qkv_mm_done.get(gidx))
                ttr = []
                for dc in range(8):
                    tt = Sc.add("pe", lambda q, dc=dc, hdst=hdst: q.transpose(
                        pab[:, 2 * TRB + dc * 128:2 * TRB + (dc + 1) * 128],
                        hdst[:, dc * 128:(dc + 1) * 128], id_b[:]),
                        deps=d0 if dc == 0 else [], inc=(dc == 7))
                    ttr.append(tt)
                te = Sc.add("act", lambda q, jj=jj: q.copy(
                    htv.rearrange("p (dc n) -> p dc n", n=256)[:, :, jj * 128:(jj + 1) * 128],
                    pab[:, 2 * TRB:2 * TRB + 1024].rearrange("p (dc n) -> p dc n", n=128)),
                    deps=[ttr[-1]])
                reg["b6"] = te
                tok_h_free[jj] = te
                ttoks.append(te)
            wb1 = [arbf(o_wf1 + i * 512, o_wf1 + (i + 1) * 512) for i in range(NWF)]
            g_toks = []
            gel_ps = [reg["b01"], reg["b01"]]
            for hc in range(32):
                wi = hc % NWF
                if hc < NWF and pidx in ffn_wl:
                    t_wl = ffn_wl[pidx][hc]
                else:
                    t_wl = Sc.add("gp", lambda q, hc=hc, wi=wi: q.dma_start(
                        out=wb1[wi].rearrange("p (dc n) -> p dc n", n=128),
                        in_=weff_h[hc].rearrange("dc pp n -> pp dc n")),
                        deps=later(tok_wf1_free[wi], *t_folds), dma="gf%d" % (wi % 4))
                pp = (hc % 2) * 512
                mms = []
                for dc in range(8):
                    tm = Sc.add("pe", lambda q, dc=dc, pp=pp, wi=wi: q.matmul(
                        pa[:, pp:pp + 256], wb1[wi][:, dc * 128:(dc + 1) * 128],
                        htv[:, dc * 256:(dc + 1) * 256], start=(dc == 0), stop=(dc == 7)),
                        deps=later(t_wl, *ttoks, gel_ps[hc % 2]) if dc == 0 else [],
                        inc=(dc == 7))
                    mms.append(tm)
                tok_wf1_free[wi] = mms[-1]
                t_g = Sc.add("act", lambda q, hc=hc, pp=pp: q.activation(
                    arbf(o_g, o_g + 4096).rearrange("p (ch n) -> p ch n", n=256)[:, hc, :],
                    pa[:, pp:pp + 256], AF.Gelu, bias=bgelu[:, hc:hc + 1]),
                    deps=later(mms[-1], t_bgelu_done,
                               tok_g_free[0] if hc == 0 else None))
                gel_ps[hc % 2] = t_g
                g_toks.append(t_g)
            reg["b01"] = g_toks[-1]
            wb2 = [arbf(o_wf2 + i * 512, o_wf2 + (i + 1) * 512) for i in range(NWQ)]
            zlast = [None, None]
            for hc in range(32):
                wi = hc % NWQ
                if hc < NWQ and pidx in ffn_wl2:
                    t_wl = ffn_wl2[pidx][hc]
                else:
                    t_wl = Sc.add("sp", lambda q, hc=hc, wi=wi: q.dma_start(
                        out=wb2[wi][:], in_=weff2_h[hc * 128:(hc + 1) * 128, :]),
                        deps=later(tok_wf2_free[wi], *t_folds), dma="gg%d" % (wi % 4))
                for tg in range(2):
                    zp = Z0 if tg == 0 else Z1
                    rg = "b23" if tg == 0 else "b45"
                    for nf2 in range(2):
                        tm = Sc.add("pe", lambda q, hc=hc, tg=tg, nf2=nf2, zp=zp, wi=wi:
                                    q.matmul(pa[:, zp + nf2 * 512:zp + (nf2 + 1) * 512],
                                             arbf(o_g, o_g + 4096).rearrange(
                                                 "p (ch n) -> p ch n", n=256)
                                             [:, hc, tg * 128:(tg + 1) * 128],
                                             wb2[wi][:, nf2 * 512:(nf2 + 1) * 512],
                                             start=(hc == 0), stop=False),
                                    deps=later(t_wl, g_toks[hc],
                                               reg[rg] if (hc == 0 and nf2 == 0) else None,
                                               tok_z_free[tg] if (hc == 0 and nf2 == 0)
                                               else None), inc=(tg == 1 and nf2 == 1))
                        zlast[tg] = tm
                tok_wf2_free[wi] = zlast[1]
            z_done = [None, None]
            for tg in range(2):
                zp = Z0 if tg == 0 else Z1
                for nf2 in range(2):
                    tl = Sc.add("pe", lambda q, zp=zp, nf2=nf2: q.matmul(
                        pa[:, zp + nf2 * 512:zp + (nf2 + 1) * 512], ones_rowb[:],
                        brow_con[0:1, nf2 * 512:(nf2 + 1) * 512], start=False, stop=True),
                        deps=later(t_brz, t_ones2), inc=(nf2 == 1))
                z_done[tg] = tl
            tok_g_free[0] = z_done[1]
            c = 40 + 8 * (ln_set[0] % 6)
            ln_set[0] += 1
            t_zes = []
            for tg, t in enumerate((ta, tb)):
                zp = Z0 if tg == 0 else Z1
                rg = "b23" if tg == 0 else "b45"
                zsb = arf(o_z + tg * 1024, o_z + (tg + 1) * 1024)
                t_ze = Sc.add("dve", lambda q, zsb=zsb, zp=zp, c=c, tg=tg: q.tensor_scalar(
                    zsb, pa[:, zp:zp + 1024], 1.0, 0.0, op0=ALU.mult, op1=ALU.add,
                    accum_out=small[:, c + tg:c + tg + 1]),
                    deps=later(z_done[tg], tok_zsb_free[tg],
                               ln_guard.get(c) if tg == 0 else None))
                tok_z_free[tg] = t_ze
                reg[rg] = t_ze
                t_zes.append(t_ze)

            def ln4_cont(ta=ta, tb=tb, c=c, t_zes=t_zes):
                zsbs = [arf(o_z + tg * 1024, o_z + (tg + 1) * 1024) for tg in range(2)]
                sqs = []
                for tg in range(2):
                    di = dump_ctr[0] % 2
                    dump_ctr[0] += 1
                    dmp = arbf(o_dump + di * 512, o_dump + (di + 1) * 512)
                    t2 = Sc.add("dve", lambda q, tg=tg, c=c, dmp=dmp:
                                q.scalar_tensor_tensor(
                                    dmp, zsbs[tg], 1.0, zsbs[tg],
                                    op0=ALU.mult, op1=ALU.mult,
                                    accum_out=small[:, c + 2 + tg:c + 3 + tg]),
                                deps=later(t_zes[tg], tok_dump_free[di]))
                    tok_dump_free[di] = t2
                    sqs.append(t2)
                d1 = Sc.add("dve", lambda q, c=c: q.tensor_scalar(
                    small[:, c + 4:c + 6], small[:, c:c + 2], -1.0 / D, None,
                    op0=ALU.mult), deps=[t_zes[0], t_zes[1]])
                d2 = Sc.add("dve", lambda q, c=c: q.scalar_tensor_tensor(
                    small[:, c:c + 2], small[:, c + 4:c + 6], 1.0,
                    small[:, c + 4:c + 6], op0=ALU.mult, op1=ALU.mult), deps=[d1])
                d3 = Sc.add("dve", lambda q, c=c: q.scalar_tensor_tensor(
                    small[:, c + 2:c + 4], small[:, c + 2:c + 4], 1.0 / D,
                    small[:, c:c + 2], op0=ALU.mult, op1=ALU.subtract),
                    deps=[sqs[0], sqs[1], d2])
                a1 = Sc.add("act", lambda q, c=c: q.activation(
                    small[:, c:c + 2], small[:, c + 2:c + 4], AF.Ln, bias=epsB[:]),
                    deps=[d3])
                a2 = Sc.add("act", lambda q, c=c: q.activation(
                    small[:, c + 2:c + 4], small[:, c:c + 2], AF.Exp, scale=-0.5),
                    deps=[a1])
                t9l = None
                for tg, t in enumerate((ta, tb)):
                    a3 = Sc.add("dve", lambda q, tg=tg, c=c: q.tensor_scalar(
                        small[:, c + 6 + tg:c + 7 + tg], small[:, c + 4 + tg:c + 5 + tg],
                        small[:, c + 2 + tg:c + 3 + tg], None, op0=ALU.mult), deps=[a2])
                    t9 = Sc.add("dve", lambda q, tg=tg, c=c: q.tensor_scalar(
                        arf(o_ln, o_ln + 1024), zsbs[tg],
                        small[:, c + 2 + tg:c + 3 + tg],
                        small[:, c + 6 + tg:c + 7 + tg],
                        op0=ALU.mult, op1=ALU.add),
                        deps=later(a3, tok_ln_free[0]))
                    t9l = t9
                    x2sl = t % 6
                    t_o = Sc.add("dve", lambda q, x2sl=x2sl: q.tensor_add(
                        arf(o_outb, o_outb + 1024),
                        arf(o_ln, o_ln + 1024),
                        arf(o_x2 + x2sl * 1024, o_x2 + (x2sl + 1) * 1024)),
                        deps=later(t9, tok_out_free[0]))
                    tok_ln_free[0] = t_o
                    tok_x2free[x2sl] = t_o
                    tok_zsb_free[tg] = t_o
                    t_st = Sc.add("gp", lambda q, t=t: q.dma_start(
                        out=out_d[t * T:(t + 1) * T, :],
                        in_=arf(o_outb, o_outb + 1024)),
                        deps=[t_o], dma="sout0")
                    tok_out_free[0] = t_st
                    store_toks.append(t_st)
                ln_guard[c] = t9l
            pend_ln4.append(ln4_cont)
            return z_done[1]

        if stop_after == "fold":
            Sc.add("sp", lambda q: q.dma_start(out=out_d[0:128, :],
                                               in_=arf(o_outb, o_outb + 1024)),
                   deps=t_folds, dma="sout0")
            Sc.add("sp", lambda q: q.nop(),
                   deps=[("sout0", Sc.cnt["sout0"])], inc=False)
            _emit_engine_blocks()
            return nc
        ngl = NG
        if stop_after and stop_after.startswith("g"):
            ngl = int(stop_after[1:])
        for gidx in range(ngl):
            if gidx == 0:
                emit_pre_x(0)
            emit_group_pre(gidx)
            if gidx >= 2:
                ffn_mm_done[gidx - 2] = emit_ffn(2 * (gidx - 2), 2 * (gidx - 2) + 1, gidx - 1)
            emit_group(gidx)
            for _f in pend_ln4:
                _f()
            pend_ln4.clear()
            tfc = None
            first_t = True
            for t in [tt for tt in (gidx * GT - 1, gidx * GT) if 0 <= tt < NT]:
                tfc, _c2 = emit_attn(t)
                if _c2 is not None:
                    _c2()
                if first_t and gidx + 1 < ngl:
                    emit_pre_x(gidx + 1)
                first_t = False
            attn_done[gidx] = tfc
        if ngl == NG:
            emit_vslot(NT, tok_grp_evac[NG - 1])
            tfc, _c2 = emit_attn(NT - 1)
            attn_done[NG] = tfc
            if _c2 is not None:
                _c2()
            ffn_mm_done[NG - 2] = emit_ffn(NT - 4, NT - 3, NG - 1)
            for _f in pend_ln4:
                _f()
            pend_ln4.clear()
            emit_ffn(NT - 2, NT - 1, NG - 1)
            for _f in pend_ln4:
                _f()
            pend_ln4.clear()

        Sc.add("sp", lambda q: q.nop(),
               deps=[("sout0", Sc.cnt["sout0"])], inc=False)
        if True:
            pass

        _emit_engine_blocks()

    return nc


_CACHE = {}


def kernel(**inputs):
    g = _prep(inputs)
    if "nc" not in _CACHE:
        _CACHE["nc"] = build_program()
    nc = _CACHE["nc"]
    x = np.asarray(inputs["L0_attn_x"], dtype=np.float32)
    in_maps = []
    for b in range(8):
        m = dict(g)
        m["xb"] = np.ascontiguousarray(x[b])
        in_maps.append(m)
    res = run_bass_kernel_spmd(nc, in_maps, core_ids=list(range(8)))
    out = np.stack([np.asarray(res.results[b]["out"]) for b in range(8)], axis=0)
    return out.astype(np.float32)
